# version 22
# speedup vs baseline: 1.0537x; 1.0252x over previous
import contextlib
import numpy as np
import ml_dtypes
import concourse.bass as bass
import concourse.mybir as mybir
from concourse.bass_utils import run_bass_kernel_spmd

F32 = mybir.dt.float32
BF16 = mybir.dt.bfloat16
AF = mybir.ActivationFunctionType
ALU = mybir.AluOpType
BF = ml_dtypes.bfloat16

S = 16384
DM = 1024
E = 2048
NST = 16
RLOW = 64
NCORE = 8
NCHUNK = 16
CH = S // NCHUNK
TPC = 2 * CH
EPC = E // NCORE
EPS = 1e-5

CORE_CHUNKS = [(i, NCHUNK - 1 - i) for i in range(NCORE)]


def _tok_rows(core):
    c0, c1 = CORE_CHUNKS[core]
    return np.r_[c0 * CH:(c0 + 1) * CH, c1 * CH:(c1 + 1) * CH]


class Ctr:
    def __init__(self, es, nc, name):
        self.sem = es.enter_context(nc.semaphore(name))
        self.n = 0

    def inc(self, inst, k=1):
        inst.then_inc(self.sem, k)
        self.n += k

    def wait(self, eng, n=None):
        n = self.n if n is None else n
        if n > 0:
            eng.wait_ge(self.sem, n)


def _cvec(v, width=128):
    v = np.asarray(v)
    k = v.shape[0] // width
    return np.ascontiguousarray(v.reshape(k, width).T)


def _ln_stats_math(nc, dve, act, stat, ps_s, ps_q, ceps, want_rms=False):
    V, A_ = nc.vector, nc.scalar
    m, q, v, r = stat["m"][:, :], stat["q"][:, :], stat["v"][:, :], stat["r"][:, :]
    ar, br, t1 = stat["ar"][:, :], stat["br"][:, :], stat["t1"][:, :]
    act.inc(A_.activation(out=m, in_=ps_s, func=AF.Identity, scale=1.0 / DM))
    act.inc(A_.activation(out=q, in_=ps_q, func=AF.Identity, scale=1.0 / DM))
    act.inc(A_.activation(out=t1, in_=ps_s, func=AF.Square, scale=1.0 / DM))
    act.wait(V)
    dve.inc(V.tensor_sub(out=v, in0=q, in1=t1))
    dve.wait(V)
    dve.wait(A_)
    act.inc(A_.activation(out=t1, in_=v, func=AF.Ln, bias=ceps))
    act.inc(A_.activation(out=r, in_=t1, func=AF.Exp, scale=-0.5))
    if want_rms:
        sr, t2 = stat["sr"][:, :], stat["t2"][:, :]
        act.inc(A_.activation(out=t2, in_=r, func=AF.Square))
        act.wait(V)
        dve.inc(V.tensor_mul(out=t2, in0=v, in1=t2))
        dve.wait(V)
        dve.wait(A_)
        act.inc(A_.activation(out=t2, in_=t2, func=AF.Ln, bias=ceps))
        act.inc(A_.activation(out=sr, in_=t2, func=AF.Exp, scale=-0.5))
        act.wait(V)
        dve.inc(V.tensor_mul(out=ar, in0=r, in1=sr))
        dve.wait(V)
    else:
        act.wait(V)
        dve.inc(V.tensor_copy(out=ar, in_=r))
        dve.wait(V)
    dve.inc(V.tensor_mul(out=br, in0=m, in1=ar))
    dve.wait(V)


_CACHE = {}


def build_l1():
    nc = bass.Bass("TRN2", target_bir_lowering=False)
    x_in = nc.dram_tensor("x_fm", [DM, TPC], F32, kind="ExternalInput").ap()
    wi_in = nc.dram_tensor("w_in_T", [DM, 2 * E], F32, kind="ExternalInput").ap()
    nw_in = nc.dram_tensor("nw_t", [128, 8], F32, kind="ExternalInput").ap()
    cw_in = nc.dram_tensor("conv_w_t", [128, 16, 4], F32, kind="ExternalInput").ap()
    cb_in = nc.dram_tensor("conv_b_t", [128, 16], F32, kind="ExternalInput").ap()
    xp_in = nc.dram_tensor("xproj_T", [E, 96], F32, kind="ExternalInput").ap()
    dtw_in = nc.dram_tensor("dtw_T", [RLOW, E], F32, kind="ExternalInput").ap()
    dtb_in = nc.dram_tensor("dtb_t", [128, 16], F32, kind="ExternalInput").ap()
    ones_in = nc.dram_tensor("ones_bf", [128, 128], BF16, kind="ExternalInput").ap()
    xh_in = nc.dram_tensor("x_halo", [DM, 8], F32, kind="ExternalInput").ap()

    u_out = nc.dram_tensor("u_bf", [E, TPC], BF16, kind="ExternalOutput").ap()
    zs_out = nc.dram_tensor("zs_bf", [E, TPC], BF16, kind="ExternalOutput").ap()
    dt_out = nc.dram_tensor("dt_bf", [E, TPC], BF16, kind="ExternalOutput").ap()
    xdbl_out = nc.dram_tensor("xdbl", [96, TPC], F32, kind="ExternalOutput").ap()
    r1m1_out = nc.dram_tensor("r1m1", [2, TPC], F32, kind="ExternalOutput").ap()

    x3 = x_in.rearrange("(c p) t -> c p t", p=128)
    xh3 = xh_in.rearrange("(c p) t -> c p t", p=128)
    u3 = u_out.rearrange("(c p) t -> c p t", p=128)
    zs3 = zs_out.rearrange("(c p) t -> c p t", p=128)
    dt3 = dt_out.rearrange("(c p) t -> c p t", p=128)
    wi3 = wi_in.rearrange("(c p) e -> c p e", p=128)
    xp3 = xp_in.rearrange("(c p) f -> c p f", p=128)

    TT = 512
    NTT = TPC // TT

    with contextlib.ExitStack() as es:
        def sb(name, shape, dt=F32):
            return es.enter_context(nc.sbuf_tensor(name, shape, dt))

        def psum(name, shape, dt=F32):
            return es.enter_context(nc.psum_tensor(name, shape, dt))

        x_sb = [[sb(f"x{p}_{c}", [128, TT]) for c in range(8)] for p in range(2)]
        w_bf = [sb(f"w{c}", [128, 2 * E], BF16) for c in range(8)]
        ones_sb = sb("ones", [128, 128], BF16)
        nw_sb = sb("nw", [128, 8])
        cw_sb = sb("cw", [128, 16, 4])
        cb_sb = sb("cb", [128, 16])
        xp_bf = [sb(f"xp{c}", [128, 96], BF16) for c in range(16)]
        dtw_bf = sb("dtw", [RLOW, E], BF16)
        dtb_sb = sb("dtb", [128, 16])

        xb_scr2 = [sb(f"xb{i}", [128, TT], BF16) for i in range(2)]
        sq_scr2 = [sb(f"sqs{i}", [128, TT], BF16) for i in range(2)]
        xb_scr = xb_scr2[0]
        sq_scr = sq_scr2[0]
        stat = {k: sb("st_" + k, [128, TT])
                for k in ["m", "q", "v", "r", "sr", "ar", "br", "t1", "t2"]}
        xin_bf = [sb(f"xin{c}", [128, TPC], BF16) for c in range(8)]

        ur = [sb(f"ur{p}", [128, TT + 3]) for p in range(2)]
        halo = sb("halo", [128, 16, 3])
        halo2 = sb("halo2", [128, 16, 3])
        xh_sb = [sb(f"xh{c}", [128, 8]) for c in range(8)]
        xinh_bf = [sb(f"xinh{c}", [128, 8], BF16) for c in range(8)]
        uh_scr = sb("uh", [128, 8])
        cacc = [sb(f"cacc{p}", [128, TT]) for p in range(2)]
        ubf_sb = [sb(f"ubf{p}", [128, TT], BF16) for p in range(2)]
        zbf_sb = [sb(f"zbf{p}", [128, TT], BF16) for p in range(2)]
        xdbl_sb = sb("xdbl_sb", [96, TT])
        dtlow_bf = sb("dtlow", [RLOW, TT], BF16)
        sp_s1 = sb("sp_s1", [128, TT])
        sp_ax = sb("sp_ax", [128, TT])
        sp_ln = sb("sp_ln", [128, TT])
        dtbf_sb = [sb(f"dtbf{p}", [128, TT], BF16) for p in range(2)]
        ceps = sb("ceps", [128, 1])
        cone = sb("cone", [128, 1])

        ps_u = [psum(f"ps_u{p}", [128, TT]) for p in range(2)]
        ps_x = psum("ps_x", [96, TT])
        ps_dt = [psum(f"ps_dt{p}", [128, TT]) for p in range(2)]
        ps_s = psum("ps_s", [128, TT])
        ps_q = psum("ps_q", [128, TT])

        ld = Ctr(es, nc, "ld")
        gld = Ctr(es, nc, "gld")
        pe = Ctr(es, nc, "pe")
        act = Ctr(es, nc, "act")
        dve = Ctr(es, nc, "dve")
        st = Ctr(es, nc, "st")

        SY, G, PE, A_, V = nc.sync, nc.gpsimd, nc.tensor, nc.scalar, nc.vector

        ld.inc(SY.dma_start(out=ones_sb[:, :], in_=ones_in[:, :]), 16)
        ld.inc(SY.dma_start(out=nw_sb[:, :], in_=nw_in[:, :]), 16)
        ld.inc(SY.dma_start(out=cw_sb[:, :, :], in_=cw_in[:, :, :]), 16)
        ld.inc(SY.dma_start(out=cb_sb[:, :], in_=cb_in[:, :]), 16)
        ld.inc(SY.dma_start(out=dtb_sb[:, :], in_=dtb_in[:, :]), 16)
        for c in range(8):
            ld.inc(SY.dma_start(out=xh_sb[c][:, :], in_=xh3[c]), 16)
        gld.inc(G.dma_start(out=dtw_bf[:, :], in_=dtw_in[:, :]), 16)
        for c in range(16):
            gld.inc(G.dma_start(out=xp_bf[c][:, :], in_=xp3[c]), 16)
        for c in range(8):
            gld.inc(G.dma_start(out=w_bf[c][:, :], in_=wi3[c]), 16)
        ld.wait(V)
        gld.wait(V)
        for c in range(8):
            dve.inc(V.tensor_scalar_mul(out=w_bf[c][:, :], in0=w_bf[c][:, :],
                                        scalar1=nw_sb[:, c:c + 1]))
        dve.inc(V.memset(halo[:, :, :], 0.0))
        dve.inc(V.memset(ceps[:, :], EPS))
        dve.inc(V.memset(cone[:, :], 1.0))

        sh = {k: stat[k][:, 0:8] for k in stat}
        ld.wait(V); ld.wait(A_); ld.wait(PE)
        for c in range(8):
            pe.wait(V); pe.wait(A_)
            dve.inc(V.tensor_copy(out=xb_scr[:, 0:8], in_=xh_sb[c][:, :]))
            act.inc(A_.activation(out=sq_scr[:, 0:8], in_=xh_sb[c][:, :], func=AF.Square))
            dve.wait(PE); act.wait(PE)
            pe.inc(PE.matmul(out=ps_s[:, 0:8], lhsT=ones_sb[:, :], rhs=xb_scr[:, 0:8],
                             start=(c == 0), stop=(c == 7)))
            pe.inc(PE.matmul(out=ps_q[:, 0:8], lhsT=ones_sb[:, :], rhs=sq_scr[:, 0:8],
                             start=(c == 0), stop=(c == 7)))
        pe.wait(A_); pe.wait(V)
        _ln_stats_math(nc, dve, act, sh, ps_s[:, 0:8], ps_q[:, 0:8], ceps[:, :], want_rms=True)
        for c in range(8):
            dve.inc(V.tensor_mul(out=sh["t1"], in0=xh_sb[c][:, :], in1=sh["ar"]))
            dve.wait(V)
            dve.inc(V.tensor_sub(out=xinh_bf[c][:, :], in0=sh["t1"], in1=sh["br"]))
            dve.wait(V)
        nc.all_engine_barrier()
        for eo in range(16):
            for c in range(8):
                pe.inc(PE.matmul(out=ps_u[0][:, 0:8],
                                 lhsT=w_bf[c][:, eo * 128:(eo + 1) * 128],
                                 rhs=xinh_bf[c][:, :], start=(c == 0), stop=(c == 7)))
            pe.wait(A_)
            act.inc(A_.activation(out=uh_scr[:, :], in_=ps_u[0][:, 0:8], func=AF.Identity))
            act.wait(V)
            dve.inc(V.tensor_copy(out=halo[:, eo, :], in_=uh_scr[:, 1:4]))
            dve.inc(V.tensor_copy(out=halo2[:, eo, :], in_=uh_scr[:, 5:8]))
            dve.wait(PE)
        nc.all_engine_barrier()

        xload_done = [0, 0]
        xuse_done = [0, 0]
        for t in range(NTT):
            sl = slice(t * TT, (t + 1) * TT)
            p = t % 2
            dve.wait(SY, xuse_done[p])
            for c in range(8):
                ld.inc(SY.dma_start(out=x_sb[p][c][:, :], in_=x3[c][:, sl]), 16)
            xload_done[p] = ld.n
            for c in range(8):
                pe.wait(V)
                pe.wait(A_)
                ld.wait(V, xload_done[p])
                ld.wait(A_, xload_done[p])
                dve.inc(V.tensor_copy(out=xb_scr[:, :], in_=x_sb[p][c][:, :]))
                act.inc(A_.activation(out=sq_scr[:, :], in_=x_sb[p][c][:, :], func=AF.Square))
                dve.wait(PE)
                act.wait(PE)
                pe.inc(PE.matmul(out=ps_s[:, :], lhsT=ones_sb[:, :], rhs=xb_scr[:, :],
                                 start=(c == 0), stop=(c == 7)))
                pe.inc(PE.matmul(out=ps_q[:, :], lhsT=ones_sb[:, :], rhs=sq_scr[:, :],
                                 start=(c == 0), stop=(c == 7)))
            pe.wait(A_)
            pe.wait(V)
            _ln_stats_math(nc, dve, act, stat, ps_s[:, :], ps_q[:, :], ceps[:, :], want_rms=True)
            dve.wait(SY)
            act.wait(SY)
            st.inc(SY.dma_start(out=r1m1_out[0:1, sl], in_=stat["r"][0:1, :]), 16)
            st.inc(SY.dma_start(out=r1m1_out[1:2, sl], in_=stat["m"][0:1, :]), 16)
            for c in range(8):
                dve.inc(V.tensor_mul(out=stat["t1"][:, :], in0=x_sb[p][c][:, :],
                                     in1=stat["ar"][:, :]))
                dve.inc(V.tensor_sub(out=xin_bf[c][:, sl], in0=stat["t1"][:, :],
                                     in1=stat["br"][:, :]))
            xuse_done[p] = dve.n
            st.wait(A_)
            st.wait(V)
        nc.all_engine_barrier()

        evac_done = [0, 0]
        store_done = {}
        for t in range(NTT):
            sl = slice(t * TT, (t + 1) * TT)
            for eo in range(32):
                p = eo % 2
                act.wait(PE, evac_done[p])
                for c in range(8):
                    pe.inc(PE.matmul(out=ps_u[p][:, :],
                                     lhsT=w_bf[c][:, eo * 128:(eo + 1) * 128],
                                     rhs=xin_bf[c][:, sl],
                                     start=(c == 0), stop=(c == 7)))
                if eo < 16:
                    pe.wait(A_)
                    dve.wait(A_)
                    act.inc(A_.activation(out=ur[p][:, 3:3 + TT], in_=ps_u[p][:, :],
                                          func=AF.Identity))
                    evac_done[p] = act.n
                    act.wait(V)
                    hsrc = halo2 if t == 2 else halo
                    dve.inc(V.tensor_copy(out=ur[p][:, 0:3], in_=hsrc[:, eo, :]))
                    dve.wait(V)
                    dve.inc(V.tensor_scalar_mul(out=cacc[p][:, :], in0=ur[p][:, 0:TT],
                                                scalar1=cw_sb[:, eo, 0:1]))
                    for j in range(1, 4):
                        dve.inc(V.scalar_tensor_tensor(
                            out=cacc[p][:, :], in0=ur[p][:, j:j + TT],
                            scalar=cw_sb[:, eo, j:j + 1], in1=cacc[p][:, :],
                            op0=ALU.mult, op1=ALU.add))
                    dve.inc(V.tensor_copy(out=halo[:, eo, :], in_=ur[p][:, TT:TT + 3]))
                    key = ("u", p)
                    st.wait(A_, store_done.get(key, 0))
                    dve.wait(A_)
                    act.inc(A_.activation(out=ubf_sb[p][:, :], in_=cacc[p][:, :],
                                          func=AF.Silu, bias=cb_sb[:, eo:eo + 1]))
                    act.wait(SY)
                    st.inc(SY.dma_start(out=u3[eo][:, sl], in_=ubf_sb[p][:, :]), 16)
                    store_done[key] = st.n
                    act.wait(PE)
                    pe.inc(PE.matmul(out=ps_x[:, :], lhsT=xp_bf[eo][:, :],
                                     rhs=ubf_sb[p][:, :],
                                     start=(eo == 0), stop=(eo == 15)))
                else:
                    key = ("z", p)
                    st.wait(A_, store_done.get(key, 0))
                    pe.wait(A_)
                    act.inc(A_.activation(out=zbf_sb[p][:, :], in_=ps_u[p][:, :],
                                          func=AF.Silu))
                    evac_done[p] = act.n
                    act.wait(SY)
                    st.inc(SY.dma_start(out=zs3[eo - 16][:, sl], in_=zbf_sb[p][:, :]), 16)
                    store_done[key] = st.n
            pe.wait(A_)
            act.inc(A_.activation(out=xdbl_sb[:, :], in_=ps_x[:, :], func=AF.Identity))
            act.wait(SY)
            st.inc(SY.dma_start(out=xdbl_out[:, sl], in_=xdbl_sb[:, :]), 16)
            act.wait(V)
            dve.inc(V.tensor_copy(out=dtlow_bf[:, :], in_=xdbl_sb[0:64, :]))
            for e2 in range(16):
                p = e2 % 2
                act.wait(PE, evac_done[p])
                dve.wait(PE)
                pe.inc(PE.matmul(out=ps_dt[p][:, :],
                                 lhsT=dtw_bf[:, e2 * 128:(e2 + 1) * 128],
                                 rhs=dtlow_bf[:, :], start=True, stop=True))
                pe.wait(A_)
                dve.wait(A_)
                act.inc(A_.activation(out=sp_s1[:, :], in_=ps_dt[p][:, :],
                                      func=AF.Identity, bias=dtb_sb[:, e2:e2 + 1]))
                evac_done[p] = act.n
                act.inc(A_.activation(out=sp_ax[:, :], in_=sp_s1[:, :], func=AF.Abs))
                act.inc(A_.activation(out=sp_ax[:, :], in_=sp_ax[:, :], func=AF.Exp,
                                      scale=-1.0))
                act.inc(A_.activation(out=sp_ln[:, :], in_=sp_ax[:, :], func=AF.Ln,
                                      bias=cone[:, :]))
                act.inc(A_.activation(out=sp_ax[:, :], in_=sp_s1[:, :], func=AF.Relu))
                key = ("dt", p)
                st.wait(V, store_done.get(key, 0))
                act.wait(V)
                dve.inc(V.tensor_add(out=dtbf_sb[p][:, :], in0=sp_ax[:, :],
                                     in1=sp_ln[:, :]))
                dve.wait(SY)
                st.inc(SY.dma_start(out=dt3[e2][:, sl], in_=dtbf_sb[p][:, :]), 16)
                store_done[key] = st.n
        st.wait(SY)
    return nc


def _host_l1_inputs(inp):
    d = {}
    d["w_in_T"] = np.ascontiguousarray(inp["L0_attn_in_proj_w"].T)
    d["nw_t"] = _cvec(inp["L0_attn_norm_w"])
    d["conv_w_t"] = np.ascontiguousarray(
        inp["L0_attn_conv_w"].reshape(16, 128, 4).transpose(1, 0, 2))
    d["conv_b_t"] = _cvec(inp["L0_attn_conv_b"])
    d["xproj_T"] = np.ascontiguousarray(inp["L0_attn_x_proj_w"].T)
    d["dtw_T"] = np.ascontiguousarray(inp["L0_attn_dt_proj_w"].T)
    d["dtb_t"] = _cvec(inp["L0_attn_dt_proj_b"])
    d["ones_bf"] = np.ones((128, 128), dtype=BF)
    return d


def run_l1(inp):
    if "l1" not in _CACHE:
        _CACHE["l1"] = build_l1()
    nc = _CACHE["l1"]
    shared = _host_l1_inputs(inp)
    x = np.asarray(inp["L0_attn_x"])[0]
    in_maps = []
    for core in range(NCORE):
        m = dict(shared)
        m["x_fm"] = np.ascontiguousarray(x[_tok_rows(core)].T)
        xh = np.zeros((8, DM), np.float32)
        c0, c1 = CORE_CHUNKS[core]
        if c0 > 0:
            xh[0:4] = x[c0 * CH - 4:c0 * CH]
        xh[4:8] = x[c1 * CH - 4:c1 * CH]
        m["x_halo"] = np.ascontiguousarray(xh.T)
        in_maps.append(m)
    res = run_bass_kernel_spmd(nc, in_maps, core_ids=list(range(NCORE)))
    return res.results


def build_l2():
    nc = bass.Bass("TRN2", target_bir_lowering=False)
    NT2 = S // 512
    TT = 512
    u_in = nc.dram_tensor("u_sl", [EPC, S], BF16, kind="ExternalInput").ap()
    dt_in = nc.dram_tensor("dt_sl", [EPC, S], BF16, kind="ExternalInput").ap()
    zs_in = nc.dram_tensor("zs_sl", [EPC, S], BF16, kind="ExternalInput").ap()
    B_in = nc.dram_tensor("B_t", [NT2, NST, TT], F32, kind="ExternalInput").ap()
    C_in = nc.dram_tensor("C_t", [NT2, NST, TT], F32, kind="ExternalInput").ap()
    Al_in = nc.dram_tensor("A_log_t", [2, 128, NST], F32, kind="ExternalInput").ap()
    D_in = nc.dram_tensor("D_t", [128, 2], F32, kind="ExternalInput").ap()

    yg_out = nc.dram_tensor("yg_sl", [EPC, S], BF16, kind="ExternalOutput").ap()

    u3 = u_in.rearrange("(c p) t -> c p t", p=128)
    dt3 = dt_in.rearrange("(c p) t -> c p t", p=128)
    zs3 = zs_in.rearrange("(c p) t -> c p t", p=128)
    yg3 = yg_out.rearrange("(c p) t -> c p t", p=128)

    with contextlib.ExitStack() as es:
        def sb(name, shape, dt=F32):
            return es.enter_context(nc.sbuf_tensor(name, shape, dt))

        A_t = [sb(f"A{e}", [128, NST]) for e in range(2)]
        A_scr = sb("Ascr", [128, NST])
        cm1 = sb("cm1", [128, 1])
        D_sb = sb("D", [128, 2])
        hc = [sb(f"hc{e}", [128, NST], BF16) for e in range(2)]
        dt_sb = [sb(f"dt{p}", [128, TT], BF16) for p in range(2)]
        u_sb = [sb(f"u{p}", [128, TT], BF16) for p in range(2)]
        zs_sb = [sb(f"zs{p}", [128, TT], BF16) for p in range(2)]
        Br = [sb(f"Br{p}", [128, NST, TT], BF16) for p in range(2)]
        Cr = [sb(f"Cr{p}", [128, NST, TT], BF16) for p in range(2)]
        dtu = sb("dtu", [128, TT], BF16)
        dA = sb("dA", [128, NST, TT], BF16)
        bt = sb("bt", [128, NST, TT], BF16)
        ht = sb("ht", [128, NST, TT], BF16)
        tr8 = sb("tr8", [128, 8, TT], BF16)
        tr4 = sb("tr4", [128, 4, TT], BF16)
        tr2 = sb("tr2", [128, 2, TT], BF16)
        y_sb = sb("y", [128, TT])
        yd_sb = sb("yd", [128, TT], BF16)
        yg_sb = [sb(f"yg{p}", [128, TT], BF16) for p in range(2)]

        ld = Ctr(es, nc, "ld")
        gld = Ctr(es, nc, "gld")
        act = Ctr(es, nc, "act")
        dve = Ctr(es, nc, "dve")
        st = Ctr(es, nc, "st")

        SY, G, A_, V = nc.sync, nc.gpsimd, nc.scalar, nc.vector

        ld.inc(SY.dma_start(out=D_sb[:, :], in_=D_in[:, :]), 16)
        for e in range(2):
            ld.inc(SY.dma_start(out=A_t[e][:, :], in_=Al_in[e]), 16)
        ld.wait(V)
        ld.wait(A_)
        dve.inc(V.memset(cm1[:, :], -1.0))
        for e in range(2):
            dve.inc(V.memset(hc[e][:, :], 0.0))
        dve.wait(A_)
        for e in range(2):
            act.inc(A_.activation(out=A_scr[:, :], in_=A_t[e][:, :], func=AF.Exp))
            act.wait(V)
            dve.wait(V)
            dve.inc(V.tensor_scalar_mul(out=A_t[e][:, :], in0=A_scr[:, :],
                                        scalar1=cm1[:, :]))
            dve.wait(V)
            dve.wait(A_)

        ld_done = [0, 0]
        gld_done = [0, 0]
        use_done = [0, 0]
        bc_use_done = [0, 0]
        stp_done = [0, 0]
        for tt in range(NT2):
            pb = tt % 2
            dve.wait(G, bc_use_done[pb])
            gld.inc(G.dma_start(out=Br[pb][:, :, :],
                                in_=B_in[tt].unsqueeze(0).broadcast_to([128, NST, TT])), 16)
            gld.inc(G.dma_start(out=Cr[pb][:, :, :],
                                in_=C_in[tt].unsqueeze(0).broadcast_to([128, NST, TT])), 16)
            gld_done[pb] = gld.n
            for ec in range(2):
                p = (2 * tt + ec) % 2
                sl = slice(tt * TT, (tt + 1) * TT)
                dve.wait(SY, use_done[p])
                act.wait(SY)
                ld.inc(SY.dma_start(out=dt_sb[p][:, :], in_=dt3[ec][:, sl]), 16)
                ld.inc(SY.dma_start(out=u_sb[p][:, :], in_=u3[ec][:, sl]), 16)
                ld.inc(SY.dma_start(out=zs_sb[p][:, :], in_=zs3[ec][:, sl]), 16)
                ld_done[p] = ld.n
                ld.wait(V, ld_done[p])
                dve.inc(V.tensor_mul(out=dtu[:, :], in0=dt_sb[p][:, :], in1=u_sb[p][:, :]))
                dve.wait(V)
                ld.wait(A_, ld_done[p])
                dve.wait(A_)
                for n in range(NST):
                    act.inc(A_.activation(out=dA[:, n, :], in_=dt_sb[p][:, :],
                                          func=AF.Exp, scale=A_t[ec][:, n:n + 1]))
                gld.wait(V, gld_done[pb])
                dve.inc(V.tensor_mul(out=bt[:, :, :],
                                     in0=dtu[:, :].unsqueeze(1).broadcast_to([128, NST, TT]),
                                     in1=Br[pb][:, :, :]))
                dve.wait(V)
                act.wait(V)
                for n in range(NST):
                    dve.inc(V.tensor_tensor_scan(out=ht[:, n, :], data0=dA[:, n, :],
                                                 data1=bt[:, n, :],
                                                 initial=hc[ec][:, n:n + 1],
                                                 op0=ALU.mult, op1=ALU.add))
                dve.wait(V)
                dve.inc(V.tensor_copy(out=hc[ec][:, :], in_=ht[:, :, TT - 1:TT]))
                dve.wait(V)
                dve.inc(V.tensor_mul(out=dA[:, :, :], in0=ht[:, :, :], in1=Cr[pb][:, :, :]))
                bc_use_done[pb] = dve.n
                dve.wait(V)
                dve.inc(V.tensor_add(out=tr8[:, :, :], in0=dA[:, 0:8, :], in1=dA[:, 8:16, :]))
                dve.wait(V)
                dve.inc(V.tensor_add(out=tr4[:, :, :], in0=tr8[:, 0:4, :], in1=tr8[:, 4:8, :]))
                dve.wait(V)
                dve.inc(V.tensor_add(out=tr2[:, :, :], in0=tr4[:, 0:2, :], in1=tr4[:, 2:4, :]))
                dve.wait(V)
                dve.inc(V.tensor_add(out=y_sb[:, :], in0=tr2[:, 0, :], in1=tr2[:, 1, :]))
                dve.wait(V)
                dve.inc(V.scalar_tensor_tensor(out=yd_sb[:, :], in0=u_sb[p][:, :],
                                               scalar=D_sb[:, ec:ec + 1], in1=y_sb[:, :],
                                               op0=ALU.mult, op1=ALU.add))
                dve.wait(V)
                st.wait(V, stp_done[p])
                dve.inc(V.tensor_mul(out=yg_sb[p][:, :], in0=yd_sb[:, :], in1=zs_sb[p][:, :]))
                use_done[p] = dve.n
                dve.wait(SY)
                st.inc(SY.dma_start(out=yg3[ec][:, sl], in_=yg_sb[p][:, :]), 16)
                stp_done[p] = st.n
        st.wait(SY)
    return nc


def run_l2(u_full, dt_full, zs_full, B_full, C_full, A_log, Dv):
    if "l2" not in _CACHE:
        _CACHE["l2"] = build_l2()
    nc = _CACHE["l2"]
    NT2 = S // 512
    B_t = np.ascontiguousarray(B_full.reshape(NST, NT2, 512).transpose(1, 0, 2))
    C_t = np.ascontiguousarray(C_full.reshape(NST, NT2, 512).transpose(1, 0, 2))
    in_maps = []
    for core in range(NCORE):
        s0 = EPC * core
        in_maps.append({
            "u_sl": np.ascontiguousarray(u_full[s0:s0 + EPC]),
            "dt_sl": np.ascontiguousarray(dt_full[s0:s0 + EPC]),
            "zs_sl": np.ascontiguousarray(zs_full[s0:s0 + EPC]),
            "B_t": B_t, "C_t": C_t,
            "A_log_t": np.ascontiguousarray(A_log[s0:s0 + EPC].reshape(2, 128, NST)),
            "D_t": np.ascontiguousarray(Dv[s0:s0 + EPC].reshape(2, 128).T),
        })
    res = run_bass_kernel_spmd(nc, in_maps, core_ids=list(range(NCORE)))
    return np.concatenate([res.results[c]["yg_sl"] for c in range(NCORE)], axis=0)


def build_l3():
    nc = bass.Bass("TRN2", target_bir_lowering=False)
    TT = 512
    NTT = TPC // TT
    x_in = nc.dram_tensor("x_fm", [DM, TPC], F32, kind="ExternalInput").ap()
    yg_in = nc.dram_tensor("yg_tok", [E, TPC], BF16, kind="ExternalInput").ap()
    r1_in = nc.dram_tensor("r1_rep", [128, TPC], F32, kind="ExternalInput").ap()
    m1_in = nc.dram_tensor("m1_rep", [128, TPC], F32, kind="ExternalInput").ap()
    ow_in = nc.dram_tensor("outw_T", [E, DM], F32, kind="ExternalInput").ap()
    w0_in = nc.dram_tensor("w0_T", [DM, DM], F32, kind="ExternalInput").ap()
    al_in = nc.dram_tensor("alpha_rep", [128, 1], F32, kind="ExternalInput").ap()
    fb_in = nc.dram_tensor("fb_t", [128, 8], F32, kind="ExternalInput").ap()
    wq_in = nc.dram_tensor("wq_T", [DM, DM], F32, kind="ExternalInput").ap()
    wk_in = nc.dram_tensor("wk_T", [DM, DM], F32, kind="ExternalInput").ap()
    wv_in = nc.dram_tensor("wv_T", [DM, DM], F32, kind="ExternalInput").ap()
    ones_in = nc.dram_tensor("ones_bf", [128, 128], BF16, kind="ExternalInput").ap()
    onesf_in = nc.dram_tensor("ones_f", [128, 128], F32, kind="ExternalInput").ap()

    x2_out = nc.dram_tensor("x2_fm", [DM, TPC], F32, kind="ExternalOutput").ap()
    q_out = nc.dram_tensor("q_bf", [DM, TPC], BF16, kind="ExternalOutput").ap()
    k_out = nc.dram_tensor("k_bf", [DM, TPC], BF16, kind="ExternalOutput").ap()
    v_out = nc.dram_tensor("v_bf", [TPC, DM], BF16, kind="ExternalOutput").ap()
    dbg_mo = nc.dram_tensor("dbg_mo", [DM, TPC], F32, kind="ExternalOutput").ap()
    dbg_x1 = nc.dram_tensor("dbg_x1", [DM, TPC], F32, kind="ExternalOutput").ap()

    x3 = x_in.rearrange("(c p) t -> c p t", p=128)
    yg3 = yg_in.rearrange("(c p) t -> c p t", p=128)
    ow3 = ow_in.rearrange("(c p) d -> c p d", p=128)
    w03 = w0_in.rearrange("(c p) d -> c p d", p=128)
    wq3 = wq_in.rearrange("(c p) d -> c p d", p=128)
    wk3 = wk_in.rearrange("(c p) d -> c p d", p=128)
    wv3 = wv_in.rearrange("(c p) d -> c p d", p=128)
    x23 = x2_out.rearrange("(c p) t -> c p t", p=128)
    q3 = q_out.rearrange("(c p) t -> c p t", p=128)
    k3 = k_out.rearrange("(c p) t -> c p t", p=128)
    v3 = v_out.rearrange("(tc p) d -> tc p d", p=128)

    with contextlib.ExitStack() as es:
        def sb(name, shape, dt=F32, st_=None):
            return (st_ or es).enter_context(nc.sbuf_tensor(name, shape, dt))

        def psum(name, shape, dt=F32, st_=None):
            return (st_ or es).enter_context(nc.psum_tensor(name, shape, dt))

        ones_sb = sb("ones", [128, 128], BF16)
        onesf_sb = sb("onesf", [128, 128], F32)
        ceps = sb("ceps", [128, 1])
        cm1 = sb("cm1", [128, 1])
        stat = {k: sb("st_" + k, [128, TT])
                for k in ["m", "q", "v", "r", "sr", "ar", "br", "t1", "t2"]}
        xb_scr2 = [sb(f"xb{i}", [128, TT], BF16) for i in range(2)]
        sq_scr2 = [sb(f"sqs{i}", [128, TT], BF16) for i in range(2)]
        xb_scr = xb_scr2[0]
        sq_scr = sq_scr2[0]
        x_sb = [[sb(f"x{p}_{c}", [128, TT]) for c in range(8)] for p in range(2)]
        ps_s = psum("ps_s", [128, TT])
        ps_q = psum("ps_q", [128, TT])

        ld = Ctr(es, nc, "ld")
        gld = Ctr(es, nc, "gld")
        pe = Ctr(es, nc, "pe")
        act = Ctr(es, nc, "act")
        dve = Ctr(es, nc, "dve")
        st = Ctr(es, nc, "st")

        SY, G, PE, A_, V = nc.sync, nc.gpsimd, nc.tensor, nc.scalar, nc.vector

        ld.inc(SY.dma_start(out=ones_sb[:, :], in_=ones_in[:, :]), 16)
        ld.inc(SY.dma_start(out=onesf_sb[:, :], in_=onesf_in[:, :]), 16)
        ld.wait(V)
        dve.inc(V.memset(ceps[:, :], EPS))
        dve.inc(V.memset(cm1[:, :], -1.0))
        dve.wait(V); dve.wait(A_); dve.wait(PE)

        def ln_stats(src_tiles, sl_or_none, pe_, want_rms=False):
            for c in range(8):
                pe.wait(V); pe.wait(A_)
                dve.inc(V.tensor_copy(out=xb_scr[:, :], in_=src_tiles[c]))
                act.inc(A_.activation(out=sq_scr[:, :], in_=src_tiles[c], func=AF.Square))
                dve.wait(PE); act.wait(PE)
                pe.inc(PE.matmul(out=ps_s[:, :], lhsT=ones_sb[:, :], rhs=xb_scr[:, :],
                                 start=(c == 0), stop=(c == 7)))
                pe.inc(PE.matmul(out=ps_q[:, :], lhsT=ones_sb[:, :], rhs=sq_scr[:, :],
                                 start=(c == 0), stop=(c == 7)))
            pe.wait(A_); pe.wait(V)
            _ln_stats_math(nc, dve, act, stat, ps_s[:, :], ps_q[:, :], ceps[:, :],
                           want_rms=want_rms)

        with contextlib.ExitStack() as esa:
            ow_bf = [sb(f"ow{c}", [128, DM], BF16, esa) for c in range(16)]
            tern = [sb(f"tern{c}", [128, DM], BF16, esa) for c in range(8)]
            w0f = sb("w0f", [128, DM], F32, esa)
            scrA = sb("scrA", [128, DM], F32, esa)
            scrB = sb("scrB", [128, DM], F32, esa)
            asum = sb("asum", [128, 16], F32, esa)
            delta = sb("delta", [128, 1], F32, esa)
            ndelta = sb("ndelta", [128, 1], F32, esa)
            al_sb = sb("al", [128, 1], F32, esa)
            fb_sb = sb("fb", [128, 8], F32, esa)
            yg_sb = [[sb(f"yg{p}_{c}", [128, TT], BF16, esa) for c in range(16)]
                     for p in range(2)]
            r1_sb = sb("r1s", [128, TT], F32, esa)
            m1_sb = sb("m1s", [128, TT], F32, esa)
            tmp_sb = [sb(f"tmp{c}", [128, TT], F32, esa) for c in range(8)]
            fo_sb = [sb(f"fo{c}", [128, TT], F32, esa) for c in range(8)]
            xln_bf = [sb(f"xln{c}", [128, TT], BF16, esa) for c in range(8)]
            evac = sb("evac", [128, TT], F32, esa)
            ps_a = [psum(f"ps_a{p}", [128, TT], st_=esa) for p in range(2)]
            ps_d = psum("ps_d", [128, 16], st_=esa)

            ld.inc(SY.dma_start(out=al_sb[:, :], in_=al_in[:, :]), 16)
            ld.inc(SY.dma_start(out=fb_sb[:, :], in_=fb_in[:, :]), 16)
            dve.inc(V.memset(asum[:, :], 0.0))
            dve.wait(V)
            for c in range(16):
                gld.inc(G.dma_start(out=ow_bf[c][:, :], in_=ow3[c]), 16)
            for c in range(8):
                ld.inc(SY.dma_start(out=w0f[:, :], in_=w03[c]), 16)
                ld.wait(A_)
                act.inc(A_.activation(out=scrA[:, :], in_=w0f[:, :], func=AF.Abs))
                act.wait(V)
                dve.inc(V.tensor_reduce(out=asum[:, c:c + 1], in_=scrA[:, :],
                                        axis=mybir.AxisListType.X, op=ALU.add))
                dve.wait(V)
                dve.wait(SY); act.wait(SY)
            pe.wait(V)
            dve.wait(PE)
            pe.inc(PE.matmul(out=ps_d[:, :], lhsT=onesf_sb[:, :], rhs=asum[:, :16],
                             start=True, stop=True))
            pe.wait(V)
            dve.inc(V.tensor_reduce(out=delta[:, :], in_=ps_d[:, 0:8],
                                    axis=mybir.AxisListType.X, op=ALU.add))
            dve.wait(V); dve.wait(A_)
            act.inc(A_.activation(out=delta[:, :], in_=delta[:, :], func=AF.Identity,
                                  scale=0.7 / (DM * DM)))
            act.wait(V)
            dve.inc(V.tensor_scalar_mul(out=ndelta[:, :], in0=delta[:, :],
                                        scalar1=cm1[:, :]))
            dve.wait(V)
            for c in range(8):
                dve.wait(SY)
                ld.inc(SY.dma_start(out=w0f[:, :], in_=w03[c]), 16)
                ld.wait(V)
                dve.inc(V.tensor_scalar(out=scrA[:, :], in0=w0f[:, :],
                                        scalar1=delta[:, 0:1], scalar2=None,
                                        op0=ALU.is_gt))
                dve.inc(V.tensor_scalar(out=scrB[:, :], in0=w0f[:, :],
                                        scalar1=ndelta[:, 0:1], scalar2=None,
                                        op0=ALU.is_lt))
                dve.wait(V)
                dve.inc(V.tensor_sub(out=tern[c][:, :], in0=scrA[:, :], in1=scrB[:, :]))
                dve.wait(V)

            gld.wait(PE)
            ld.wait(PE)
            ld.wait(V)

            yld_done = [0, 0]
            yuse_done = [0, 0]
            evac_done = [0, 0]
            r1use_done = 0
            x2store_done = [0, 0]
            for t in range(NTT):
                sl = slice(t * TT, (t + 1) * TT)
                p = t % 2
                dve.wait(SY, yuse_done[p])
                st.wait(SY, x2store_done[p])
                for c in range(8):
                    ld.inc(SY.dma_start(out=x_sb[p][c][:, :], in_=x3[c][:, sl]), 16)
                for c in range(16):
                    ld.inc(SY.dma_start(out=yg_sb[p][c][:, :], in_=yg3[c][:, sl]), 16)
                dve.wait(SY, r1use_done)
                ld.inc(SY.dma_start(out=r1_sb[:, :], in_=r1_in[:, sl]), 16)
                ld.inc(SY.dma_start(out=m1_sb[:, :], in_=m1_in[:, sl]), 16)
                yld_done[p] = ld.n
                ld.wait(V, yld_done[p])
                for c in range(8):
                    dve.inc(V.tensor_sub(out=tmp_sb[c][:, :], in0=x_sb[p][c][:, :],
                                         in1=m1_sb[:, :]))
                    dve.wait(V)
                    dve.inc(V.tensor_mul(out=tmp_sb[c][:, :], in0=tmp_sb[c][:, :],
                                         in1=r1_sb[:, :]))
                    dve.wait(V)
                r1use_done = dve.n
                for do in range(8):
                    pp = do % 2
                    act.wait(PE, evac_done[pp])
                    ld.wait(PE, yld_done[p])
                    for ec in range(16):
                        pe.inc(PE.matmul(out=ps_a[pp][:, :],
                                         lhsT=ow_bf[ec][:, do * 128:(do + 1) * 128],
                                         rhs=yg_sb[p][ec][:, :],
                                         start=(ec == 0), stop=(ec == 15)))
                    pe.wait(A_)
                    dve.wait(A_)
                    act.inc(A_.activation(out=evac[:, :], in_=ps_a[pp][:, :],
                                          func=AF.Identity))
                    evac_done[pp] = act.n
                    act.wait(V)
                    dve.inc(V.tensor_add(out=tmp_sb[do][:, :], in0=tmp_sb[do][:, :],
                                         in1=evac[:, :]))
                    dve.wait(V)
                yuse_done[p] = dve.n
                dve.wait(SY)
                for c in range(8):
                    st.inc(SY.dma_start(out=dbg_mo.rearrange("(c p) t -> c p t", p=128)[c][:, sl], in_=tmp_sb[c][:, :]), 16)
                ln_stats([tmp_sb[c][:, :] for c in range(8)], None, pe)
                for c in range(8):
                    dve.inc(V.tensor_mul(out=stat["t1"][:, :], in0=tmp_sb[c][:, :],
                                         in1=stat["ar"][:, :]))
                    dve.wait(V)
                    dve.inc(V.tensor_sub(out=stat["t1"][:, :], in0=stat["t1"][:, :],
                                         in1=stat["br"][:, :]))
                    dve.wait(V)
                    dve.inc(V.tensor_add(out=x_sb[p][c][:, :], in0=x_sb[p][c][:, :],
                                         in1=stat["t1"][:, :]))
                    dve.wait(V)
                dve.wait(SY)
                for c in range(8):
                    st.inc(SY.dma_start(out=dbg_x1.rearrange("(c p) t -> c p t", p=128)[c][:, sl], in_=x_sb[p][c][:, :]), 16)
                st.wait(V)
                ln_stats([x_sb[p][c][:, :] for c in range(8)], None, pe)
                for c in range(8):
                    dve.inc(V.tensor_mul(out=stat["t1"][:, :], in0=x_sb[p][c][:, :],
                                         in1=stat["ar"][:, :]))
                    dve.wait(V)
                    dve.inc(V.tensor_sub(out=xln_bf[c][:, :], in0=stat["t1"][:, :],
                                         in1=stat["br"][:, :]))
                    dve.wait(V)
                for do in range(8):
                    pp = do % 2
                    act.wait(PE, evac_done[pp])
                    dve.wait(PE)
                    for c in range(8):
                        pe.inc(PE.matmul(out=ps_a[pp][:, :],
                                         lhsT=tern[c][:, do * 128:(do + 1) * 128],
                                         rhs=xln_bf[c][:, :],
                                         start=(c == 0), stop=(c == 7)))
                    pe.wait(A_)
                    act.inc(A_.activation(out=fo_sb[do][:, :], in_=ps_a[pp][:, :],
                                          func=AF.Identity, bias=fb_sb[:, do:do + 1],
                                          scale=al_sb[:, :]))
                    evac_done[pp] = act.n
                act.wait(V)
                ln_stats([fo_sb[c][:, :] for c in range(8)], None, pe)
                st.wait(V)
                for c in range(8):
                    dve.inc(V.tensor_mul(out=stat["t1"][:, :], in0=fo_sb[c][:, :],
                                         in1=stat["ar"][:, :]))
                    dve.wait(V)
                    dve.inc(V.tensor_sub(out=stat["t1"][:, :], in0=stat["t1"][:, :],
                                         in1=stat["br"][:, :]))
                    dve.wait(V)
                    dve.inc(V.tensor_add(out=x_sb[p][c][:, :], in0=x_sb[p][c][:, :],
                                         in1=stat["t1"][:, :]))
                    dve.wait(V)
                    dve.wait(SY)
                    st.inc(SY.dma_start(out=x23[c][:, sl], in_=x_sb[p][c][:, :]), 16)
                x2store_done[p] = st.n
            st.wait(SY)
            nc.all_engine_barrier()

        with contextlib.ExitStack() as esb:
            wq_bf = [sb(f"wqb{c}", [128, DM], BF16, esb) for c in range(8)]
            wk_bf = [sb(f"wkb{c}", [128, DM], BF16, esb) for c in range(8)]
            wv_bf = [sb(f"wvb{c}", [128, DM], BF16, esb) for c in range(8)]
            xln_bf = [sb(f"xl{c}", [128, TT], BF16, esb) for c in range(8)]
            qkv_scr = [sb(f"qkvs{p}", [128, TT], BF16, esb) for p in range(2)]
            v_scr = [sb(f"vs{p}", [128, DM], BF16, esb) for p in range(2)]
            ps_b = [psum(f"ps_b{p}", [128, TT], st_=esb) for p in range(2)]

            for c in range(8):
                gld.inc(G.dma_start(out=wq_bf[c][:, :], in_=wq3[c]), 16)
                gld.inc(G.dma_start(out=wk_bf[c][:, :], in_=wk3[c]), 16)
                gld.inc(G.dma_start(out=wv_bf[c][:, :], in_=wv3[c]), 16)
            gld.wait(PE)
            evac_done = [0, 0]
            store_done = {}
            for t in range(NTT):
                sl = slice(t * TT, (t + 1) * TT)
                p = t % 2
                dve.wait(SY)
                for c in range(8):
                    ld.inc(SY.dma_start(out=x_sb[p][c][:, :], in_=x23[c][:, sl]), 16)
                ld.wait(V); ld.wait(A_); ld.wait(PE)
                ln_stats([x_sb[p][c][:, :] for c in range(8)], None, pe)
                for c in range(8):
                    dve.inc(V.tensor_mul(out=stat["t1"][:, :], in0=x_sb[p][c][:, :],
                                         in1=stat["ar"][:, :]))
                    dve.wait(V)
                    dve.inc(V.tensor_sub(out=xln_bf[c][:, :], in0=stat["t1"][:, :],
                                         in1=stat["br"][:, :]))
                    dve.wait(V)
                dve.wait(PE)
                for wname, wtiles, out3 in (("q", wq_bf, q3), ("k", wk_bf, k3)):
                    for do in range(8):
                        pp = do % 2
                        act.wait(PE, evac_done[pp])
                        for c in range(8):
                            pe.inc(PE.matmul(out=ps_b[pp][:, :],
                                             lhsT=wtiles[c][:, do * 128:(do + 1) * 128],
                                             rhs=xln_bf[c][:, :],
                                             start=(c == 0), stop=(c == 7)))
                        pe.wait(A_)
                        key = (wname, pp)
                        st.wait(A_, store_done.get(key, 0))
                        act.inc(A_.activation(out=qkv_scr[pp][:, :], in_=ps_b[pp][:, :],
                                              func=AF.Identity))
                        evac_done[pp] = act.n
                        act.wait(SY)
                        st.inc(SY.dma_start(out=out3[do][:, sl], in_=qkv_scr[pp][:, :]), 16)
                        store_done[key] = st.n
                for tq in range(TT // 128):
                    pp = tq % 2
                    tc_global = (t * TT + tq * 128) // 128
                    for dv in range(2):
                        act.wait(PE, evac_done[pp])
                        for c in range(8):
                            pe.inc(PE.matmul(
                                out=ps_b[pp][:, dv * 256:dv * 256 + 256] if False else ps_b[pp][:, :],
                                lhsT=xln_bf[c][:, tq * 128:(tq + 1) * 128],
                                rhs=wv_bf[c][:, dv * 512:(dv + 1) * 512],
                                start=(c == 0), stop=(c == 7)))
                        pe.wait(A_)
                        key = ("v", pp, dv)
                        st.wait(A_, store_done.get(key, 0))
                        act.inc(A_.activation(out=v_scr[pp][:, dv * 512:(dv + 1) * 512],
                                              in_=ps_b[pp][:, :], func=AF.Identity))
                        evac_done[pp] = act.n
                        act.wait(SY)
                        st.inc(SY.dma_start(out=v3[tc_global][:, dv * 512:(dv + 1) * 512],
                                            in_=v_scr[pp][:, dv * 512:(dv + 1) * 512]), 16)
                        store_done[key] = st.n
            st.wait(SY)
    return nc


def run_l3(inp, x_slices, yg_full, r1m1_list):
    if "l3" not in _CACHE:
        _CACHE["l3"] = build_l3()
    nc = _CACHE["l3"]
    shared = {
        "outw_T": np.ascontiguousarray(inp["L0_attn_out_proj_w"].T),
        "w0_T": np.ascontiguousarray(inp["L0_ffn_weight"].T),
        "alpha_rep": np.ascontiguousarray(np.broadcast_to(
            inp["L0_ffn_alpha"].reshape(1, 1), (128, 1))).astype(np.float32),
        "fb_t": _cvec(inp["L0_ffn_bias"]),
        "wq_T": np.ascontiguousarray(inp["L1_attn_Wq"].T),
        "wk_T": np.ascontiguousarray(inp["L1_attn_Wk"].T),
        "wv_T": np.ascontiguousarray(inp["L1_attn_Wv"].T),
        "ones_bf": np.ones((128, 128), dtype=BF),
        "ones_f": np.ones((128, 128), dtype=np.float32),
    }
    in_maps = []
    for core in range(NCORE):
        rows = _tok_rows(core)
        m = dict(shared)
        m["x_fm"] = x_slices[core]
        m["yg_tok"] = np.ascontiguousarray(yg_full[:, rows])
        m["r1_rep"] = np.ascontiguousarray(
            np.broadcast_to(r1m1_list[core][0:1], (128, TPC)))
        m["m1_rep"] = np.ascontiguousarray(
            np.broadcast_to(r1m1_list[core][1:2], (128, TPC)))
        in_maps.append(m)
    res = run_bass_kernel_spmd(nc, in_maps, core_ids=list(range(NCORE)))
    return res.results


def build_l4():
    nc = bass.Bass("TRN2", target_bir_lowering=False)
    TT = 512
    q_in = nc.dram_tensor("q_sl", [DM, TPC], BF16, kind="ExternalInput").ap()
    k_in = nc.dram_tensor("k_all", [DM, S], BF16, kind="ExternalInput").ap()
    v_in = nc.dram_tensor("v_all", [S, DM], BF16, kind="ExternalInput").ap()
    x2_in = nc.dram_tensor("x2_fm", [DM, TPC], F32, kind="ExternalInput").ap()
    mb_in = nc.dram_tensor("mask_big", [128, 1920], F32, kind="ExternalInput").ap()
    sel_in = nc.dram_tensor("sel_rep", [128, 32], F32, kind="ExternalInput").ap()
    binv_in = nc.dram_tensor("binv_rep", [128, 32], F32, kind="ExternalInput").ap()
    idb_in = nc.dram_tensor("ident_bf", [128, 128], BF16, kind="ExternalInput").ap()
    ones_in = nc.dram_tensor("ones_bf", [128, 128], BF16, kind="ExternalInput").ap()
    x3_out = nc.dram_tensor("x3_fm", [DM, TPC], F32, kind="ExternalOutput").ap()

    q3 = q_in.rearrange("(c p) t -> c p t", p=128)
    k3 = k_in.rearrange("(c p) t -> c p t", p=128)
    v3 = v_in.rearrange("(tc p) d -> tc p d", p=128)
    x23 = x2_in.rearrange("(c p) t -> c p t", p=128)
    x33 = x3_out.rearrange("(c p) t -> c p t", p=128)

    KT = 1024

    with contextlib.ExitStack() as es:
        def sb(name, shape, dt=F32):
            return es.enter_context(nc.sbuf_tensor(name, shape, dt))

        def psum(name, shape, dt=F32):
            return es.enter_context(nc.psum_tensor(name, shape, dt))

        ones_sb = sb("ones", [128, 128], BF16)
        ident_bf = sb("identb", [128, 128], BF16)
        mask_sb = sb("mask", [128, 1920])
        sel_sb = sb("sel", [128, 32])
        binv_sb = sb("binv", [128, 32])
        cn32 = sb("cn32", [128, 1])
        ceps = sb("ceps", [128, 1])
        stat = {k: sb("st_" + k, [128, TT])
                for k in ["m", "q", "v", "r", "sr", "ar", "br", "t1", "t2"]}
        xb_scr2 = [sb(f"xb{i}", [128, TT], BF16) for i in range(2)]
        sq_scr2 = [sb(f"sqs{i}", [128, TT], BF16) for i in range(2)]
        xb_scr = xb_scr2[0]
        sq_scr = sq_scr2[0]
        q_sb = [sb(f"q{c}", [128, CH], BF16) for c in range(8)]
        kt_sb = [[sb(f"k{p}_{c}", [128, KT], BF16) for c in range(8)] for p in range(2)]
        vt_sb = [[sb(f"v{p}_{c}", [128, DM], BF16) for c in range(8)] for p in range(2)]
        O_sb = [sb(f"O{j}", [128, DM]) for j in range(8)]
        mrow = [sb(f"m{j}", [128, 1]) for j in range(8)]
        lrow = [sb(f"l{j}", [128, 1]) for j in range(8)]
        mnew = sb("mnew", [128, 1])
        mloc = sb("mloc", [128, 1])
        frow = sb("frow", [128, 1])
        bneg = sb("bneg", [128, 1])
        rs0 = sb("rs0", [128, 1])
        rl = sb("rl", [128, 1])
        p_bf = sb("pbf", [128, KT], BF16)
        pt_sb = sb("pt", [128, KT], BF16)
        of_bf = sb("of", [128, DM], BF16)
        attn_bf = [sb(f"at{c}", [128, CH], BF16) for c in range(8)]
        x2_sb = [sb(f"x2_{c}", [128, TT]) for c in range(8)]

        ps_S = psum("ps_S", [128, KT])
        ps_pt = psum("ps_pt", [128, KT], BF16)
        ps_av = psum("ps_av", [128, DM])
        ps_s = psum("ps_s", [128, TT])
        ps_q = psum("ps_q", [128, TT])

        ld = Ctr(es, nc, "ld")
        pe = Ctr(es, nc, "pe")
        act = Ctr(es, nc, "act")
        dve = Ctr(es, nc, "dve")
        st = Ctr(es, nc, "st")

        SY, G, PE, A_, V = nc.sync, nc.gpsimd, nc.tensor, nc.scalar, nc.vector

        ld.inc(SY.dma_start(out=ones_sb[:, :], in_=ones_in[:, :]), 16)
        ld.inc(SY.dma_start(out=ident_bf[:, :], in_=idb_in[:, :]), 16)
        ld.inc(SY.dma_start(out=mask_sb[:, :], in_=mb_in[:, :]), 16)
        ld.inc(SY.dma_start(out=sel_sb[:, :], in_=sel_in[:, :]), 16)
        ld.inc(SY.dma_start(out=binv_sb[:, :], in_=binv_in[:, :]), 16)
        ld.wait(V)
        dve.inc(V.memset(ceps[:, :], EPS))
        dve.inc(V.memset(cn32[:, :], -1.0 / 32.0))
        dve.wait(V); dve.wait(A_)

        def ln_stats(src_tiles, pss, psq, want_rms=False):
            mm_done = [pe.n, pe.n]
            for c in range(8):
                cp = c % 2
                pe.wait(V, mm_done[cp])
                pe.wait(A_, mm_done[cp])
                dve.inc(V.tensor_copy(out=xb_scr2[cp][:, :], in_=src_tiles[c]))
                act.inc(A_.activation(out=sq_scr2[cp][:, :], in_=src_tiles[c], func=AF.Square))
                dve.wait(PE); act.wait(PE)
                pe.inc(PE.matmul(out=pss, lhsT=ones_sb[:, :], rhs=xb_scr2[cp][:, :],
                                 start=(c == 0), stop=(c == 7)))
                pe.inc(PE.matmul(out=psq, lhsT=ones_sb[:, :], rhs=sq_scr2[cp][:, :],
                                 start=(c == 0), stop=(c == 7)))
                mm_done[cp] = pe.n
            pe.wait(A_); pe.wait(V)
            _ln_stats_math(nc, dve, act, stat, pss, psq, ceps[:, :],
                           want_rms=want_rms)

        kld_done = [0, 0]
        for qc in range(2):
            dve.wait(SY); act.wait(SY); pe.wait(SY)
            for c in range(8):
                ld.inc(SY.dma_start(out=q_sb[c][:, :],
                                    in_=q3[c][:, qc * CH:(qc + 1) * CH]), 16)
            qld = ld.n
            for j in range(8):
                dve.inc(V.memset(O_sb[j][:, :], 0.0))
                dve.inc(V.memset(mrow[j][:, :], -1e30))
                dve.inc(V.memset(lrow[j][:, :], 0.0))
            dve.wait(V); dve.wait(A_); dve.wait(PE)
            for kt in range(16):
                idx = qc * 16 + kt
                p = kt % 2
                pe.wait(SY)
                for c in range(8):
                    ld.inc(SY.dma_start(out=kt_sb[p][c][:, :],
                                        in_=k3[c][:, kt * KT:(kt + 1) * KT]), 16)
                for c in range(8):
                    ld.inc(SY.dma_start(out=vt_sb[p][c][:, :],
                                        in_=v3[kt * 8 + c]), 16)
                kld_done[p] = ld.n
                ld.wait(PE, kld_done[p])
                ld.wait(PE, qld)
                for j in range(8):
                    act.wait(PE)
                    dve.wait(PE)
                    for half in range(2):
                        for c in range(8):
                            pe.inc(PE.matmul(
                                out=ps_S[:, half * 512:(half + 1) * 512],
                                lhsT=q_sb[c][:, j * 128:(j + 1) * 128],
                                rhs=kt_sb[p][c][:, half * 512:(half + 1) * 512],
                                start=(c == 0), stop=(c == 7)))
                    pe.wait(V); pe.wait(A_)
                    dve.inc(V.scalar_tensor_tensor(
                        out=ps_S[:, :], in0=mask_sb[:, 896 - 128 * j:1920 - 128 * j],
                        scalar=sel_sb[:, idx:idx + 1], in1=ps_S[:, :],
                        op0=ALU.mult, op1=ALU.add))
                    dve.wait(V)
                    dve.inc(V.tensor_reduce(out=mloc[:, :], in_=ps_S[:, :],
                                            axis=mybir.AxisListType.X, op=ALU.max))
                    dve.wait(V)
                    dve.inc(V.tensor_max(out=mnew[:, :], in0=mrow[j][:, :],
                                         in1=mloc[:, :]))
                    dve.wait(V)
                    dve.inc(V.tensor_sub(out=frow[:, :], in0=mrow[j][:, :],
                                         in1=mnew[:, :]))
                    dve.wait(V)
                    dve.inc(V.scalar_tensor_tensor(
                        out=bneg[:, :], in0=mnew[:, :], scalar=cn32[:, :],
                        in1=binv_sb[:, idx:idx + 1], op0=ALU.mult, op1=ALU.add))
                    dve.wait(V); dve.wait(A_)
                    act.inc(A_.activation(out=frow[:, :], in_=frow[:, :], func=AF.Exp,
                                          scale=1.0 / 32.0))
                    act.inc(A_.activation(out=p_bf[:, :], in_=ps_S[:, :], func=AF.Exp,
                                          scale=1.0 / 32.0, bias=bneg[:, :],
                                          accum_out=rs0[:, :]))
                    act.wait(V)
                    dve.inc(V.scalar_tensor_tensor(out=lrow[j][:, :], in0=lrow[j][:, :],
                                                   scalar=frow[:, :], in1=rs0[:, :],
                                                   op0=ALU.mult, op1=ALU.add))
                    dve.wait(V)
                    dve.inc(V.tensor_copy(out=mrow[j][:, :], in_=mnew[:, :]))
                    dve.wait(V)
                    act.wait(PE)
                    for kc in range(8):
                        pe.inc(PE.transpose(out=ps_pt[:, kc * 128:(kc + 1) * 128],
                                            in_=p_bf[:, kc * 128:(kc + 1) * 128],
                                            identity=ident_bf[:, :]))
                    pe.wait(A_)
                    act.inc(A_.activation(out=pt_sb[:, :], in_=ps_pt[:, :],
                                          func=AF.Identity))
                    act.wait(PE)
                    dve.wait(PE)
                    for kc in range(8):
                        pe.inc(PE.matmul(out=ps_av[:, 0:512],
                                         lhsT=pt_sb[:, kc * 128:(kc + 1) * 128],
                                         rhs=vt_sb[p][kc][:, 0:512],
                                         start=(kc == 0), stop=(kc == 7)))
                        pe.inc(PE.matmul(out=ps_av[:, 512:1024],
                                         lhsT=pt_sb[:, kc * 128:(kc + 1) * 128],
                                         rhs=vt_sb[p][kc][:, 512:1024],
                                         start=(kc == 0), stop=(kc == 7)))
                    pe.wait(V)
                    dve.inc(V.scalar_tensor_tensor(out=O_sb[j][:, :], in0=O_sb[j][:, :],
                                                   scalar=frow[:, :], in1=ps_av[:, :],
                                                   op0=ALU.mult, op1=ALU.add))
                    dve.wait(V)
            for j in range(8):
                dve.wait(A_)
                act.inc(A_.activation(out=rl[:, :], in_=lrow[j][:, :], func=AF.Ln))
                act.inc(A_.activation(out=rl[:, :], in_=rl[:, :], func=AF.Exp,
                                      scale=-1.0))
                act.wait(V)
                dve.inc(V.tensor_scalar_mul(out=of_bf[:, :], in0=O_sb[j][:, :],
                                            scalar1=rl[:, :]))
                dve.wait(V); dve.wait(PE)
                act.wait(PE)
                for dc in range(8):
                    pe.inc(PE.transpose(out=ps_pt[:, dc * 128:(dc + 1) * 128],
                                        in_=of_bf[:, dc * 128:(dc + 1) * 128],
                                        identity=ident_bf[:, :]))
                pe.wait(A_)
                for dc in range(8):
                    act.inc(A_.activation(out=attn_bf[dc][:, j * 128:(j + 1) * 128],
                                          in_=ps_pt[:, dc * 128:(dc + 1) * 128],
                                          func=AF.Identity))
                act.wait(PE); act.wait(V)
            for tloc in range(CH // TT):
                sl_loc = slice(tloc * TT, (tloc + 1) * TT)
                sl_out = slice(qc * CH + tloc * TT, qc * CH + (tloc + 1) * TT)
                st.wait(SY)
                dve.wait(SY)
                for c in range(8):
                    ld.inc(SY.dma_start(out=x2_sb[c][:, :], in_=x23[c][:, sl_out]), 16)
                ld.wait(V); ld.wait(A_); ld.wait(PE)
                ln_stats([attn_bf[c][:, sl_loc] for c in range(8)])
                for c in range(8):
                    dve.inc(V.tensor_mul(out=stat["t1"][:, :], in0=attn_bf[c][:, sl_loc],
                                         in1=stat["ar"][:, :]))
                    dve.wait(V)
                    dve.inc(V.tensor_sub(out=stat["t1"][:, :], in0=stat["t1"][:, :],
                                         in1=stat["br"][:, :]))
                    dve.wait(V)
                    dve.inc(V.tensor_add(out=x2_sb[c][:, :], in0=x2_sb[c][:, :],
                                         in1=stat["t1"][:, :]))
                    dve.wait(V); dve.wait(SY)
                    st.inc(SY.dma_start(out=x33[c][:, sl_out], in_=x2_sb[c][:, :]), 16)
        st.wait(SY)
    return nc


def run_l4(q_slices, k_full, v_full, x2_slices):
    if "l4" not in _CACHE:
        _CACHE["l4"] = build_l4()
    nc = _CACHE["l4"]
    mask_big = np.zeros((128, 1920), np.float32)
    for rr in range(128):
        mask_big[rr, 897 + rr:] = -1e9
    shared = {
        "k_all": np.ascontiguousarray(k_full),
        "v_all": np.ascontiguousarray(v_full),
        "mask_big": mask_big,
        "ident_bf": np.eye(128, dtype=BF),
        "ones_bf": np.ones((128, 128), dtype=BF),
    }
    in_maps = []
    for core in range(NCORE):
        m = dict(shared)
        m["q_sl"] = q_slices[core]
        m["x2_fm"] = x2_slices[core]
        sel = np.zeros((1, 32), np.float32)
        binv = np.zeros((1, 32), np.float32)
        for qc in range(2):
            gq = CORE_CHUNKS[core][qc]
            sel[0, qc * 16 + gq] = 1.0
            for kt in range(gq + 1, 16):
                binv[0, qc * 16 + kt] = -1e30
        m["sel_rep"] = np.ascontiguousarray(np.broadcast_to(sel, (128, 32)))
        m["binv_rep"] = np.ascontiguousarray(np.broadcast_to(binv, (128, 32)))
        in_maps.append(m)
    res = run_bass_kernel_spmd(nc, in_maps, core_ids=list(range(NCORE)))
    return res.results


def build_l5():
    nc = bass.Bass("TRN2", target_bir_lowering=False)
    TT = 512
    NTT = TPC // TT
    DFF = 4096
    FSL = 1024
    NFS = DFF // FSL
    x3_in = nc.dram_tensor("x3_fm", [DM, TPC], F32, kind="ExternalInput").ap()
    rw_in = nc.dram_tensor("router_w", [DM, 8], F32, kind="ExternalInput").ap()
    rb_in = nc.dram_tensor("rb_rep", [128, 8], F32, kind="ExternalInput").ap()
    w1_in = nc.dram_tensor("W1", [8, DM, DFF], F32, kind="ExternalInput").ap()
    b1_in = nc.dram_tensor("b1_t", [8, 128, 32], F32, kind="ExternalInput").ap()
    w2_in = nc.dram_tensor("W2", [8, DFF, DM], F32, kind="ExternalInput").ap()
    b2_in = nc.dram_tensor("b2", [8, DM], F32, kind="ExternalInput").ap()
    ones_in = nc.dram_tensor("ones_bf", [128, 128], BF16, kind="ExternalInput").ap()
    onesf_in = nc.dram_tensor("ones_f", [128, 128], F32, kind="ExternalInput").ap()
    idf_in = nc.dram_tensor("ident_f", [128, 128], F32, kind="ExternalInput").ap()

    out_out = nc.dram_tensor("out_fm", [DM, TPC], F32, kind="ExternalOutput").ap()
    cfm_dram = nc.dram_tensor("cfm_scr", [8, TPC], F32).ap()

    x33 = x3_in.rearrange("(c p) t -> c p t", p=128)
    rw3 = rw_in.rearrange("(c p) e -> c p e", p=128)
    o3 = out_out.rearrange("(c p) t -> c p t", p=128)

    with contextlib.ExitStack() as es:
        def sb(name, shape, dt=F32):
            return es.enter_context(nc.sbuf_tensor(name, shape, dt))

        def psum(name, shape, dt=F32):
            return es.enter_context(nc.psum_tensor(name, shape, dt))

        ones_sb = sb("ones", [128, 128], BF16)
        onesf_sb = sb("onesf", [128, 128], F32)
        idf_sb = sb("idf", [128, 128], F32)
        ceps = sb("ceps", [128, 1])
        cm1 = sb("cm1", [128, 1])
        cone = sb("cone", [128, 1])
        cneg = sb("cneg", [128, 1])
        stat = {k: sb("st_" + k, [128, TT])
                for k in ["m", "q", "v", "r", "sr", "ar", "br", "t1", "t2"]}
        xb_scr2 = [sb(f"xb{i}", [128, TT], BF16) for i in range(2)]
        sq_scr2 = [sb(f"sqs{i}", [128, TT], BF16) for i in range(2)]
        xb_scr = xb_scr2[0]
        sq_scr = sq_scr2[0]
        x_sb = [sb(f"x{c}", [128, TT]) for c in range(8)]
        xln_bf = [sb(f"xln{c}", [128, TPC], BF16) for c in range(8)]
        rw_sb = [sb(f"rw{c}", [128, 8], BF16) for c in range(8)]
        rb_sb = sb("rb", [128, 8])
        lg = sb("lg", [128, 8])
        ge1 = sb("ge1", [128, 8])
        msk = sb("msk", [128, 8])
        m1c = sb("m1c", [128, 1])
        m2c = sb("m2c", [128, 1])
        nm1 = sb("nm1", [128, 1])
        d21 = sb("d21", [128, 1])
        rinv = sb("rinv", [128, 1])
        ee = sb("ee", [128, 8])
        cc_sb = sb("cc", [128, 8])
        cfm = sb("cfm", [8, TPC])
        crep_all = sb("crep", [128, TPC])
        b1_sb = sb("b1", [128, 32])
        b2_sb = sb("b2s", [8, DM])
        w1_bf = [sb(f"w1_{c}", [128, FSL], BF16) for c in range(8)]
        w2_bf = [sb(f"w2_{c}", [128, DM], BF16) for c in range(8)]
        h_bf = [sb(f"h{f}", [128, TT], BF16) for f in range(8)]
        yev = sb("yev", [128, TT])
        y_acc = [sb(f"ya{c}", [128, TPC]) for c in range(8)]

        ps_r = psum("ps_r", [128, 8])
        ps_ct = psum("ps_ct", [8, 128])
        ps_h = psum("ps_h", [128, TT])
        ps_y = [psum(f"ps_y{p}", [128, TT]) for p in range(2)]
        ps_s = psum("ps_s", [128, TT])
        ps_q = psum("ps_q", [128, TT])

        ld = Ctr(es, nc, "ld")
        gld = Ctr(es, nc, "gld")
        pe = Ctr(es, nc, "pe")
        act = Ctr(es, nc, "act")
        dve = Ctr(es, nc, "dve")
        st = Ctr(es, nc, "st")

        SY, G, PE, A_, V = nc.sync, nc.gpsimd, nc.tensor, nc.scalar, nc.vector

        ld.inc(SY.dma_start(out=ones_sb[:, :], in_=ones_in[:, :]), 16)
        ld.inc(SY.dma_start(out=onesf_sb[:, :], in_=onesf_in[:, :]), 16)
        ld.inc(SY.dma_start(out=idf_sb[:, :], in_=idf_in[:, :]), 16)
        ld.inc(SY.dma_start(out=rb_sb[:, :], in_=rb_in[:, :]), 16)
        ld.inc(SY.dma_start(out=b2_sb[:, :], in_=b2_in[:, :]), 16)
        for c in range(8):
            gld.inc(G.dma_start(out=rw_sb[c][:, :], in_=rw3[c]), 16)
        ld.wait(V)
        gld.wait(PE)
        dve.inc(V.memset(ceps[:, :], EPS))
        dve.inc(V.memset(cm1[:, :], -1.0))
        dve.inc(V.memset(cone[:, :], 1.0))
        dve.inc(V.memset(cneg[:, :], -1e30))
        dve.wait(V); dve.wait(A_)

        def ln_stats(src_tiles, pss, psq, want_rms=False):
            mm_done = [pe.n, pe.n]
            for c in range(8):
                cp = c % 2
                pe.wait(V, mm_done[cp])
                pe.wait(A_, mm_done[cp])
                dve.inc(V.tensor_copy(out=xb_scr2[cp][:, :], in_=src_tiles[c]))
                act.inc(A_.activation(out=sq_scr2[cp][:, :], in_=src_tiles[c], func=AF.Square))
                dve.wait(PE); act.wait(PE)
                pe.inc(PE.matmul(out=pss, lhsT=ones_sb[:, :], rhs=xb_scr2[cp][:, :],
                                 start=(c == 0), stop=(c == 7)))
                pe.inc(PE.matmul(out=psq, lhsT=ones_sb[:, :], rhs=sq_scr2[cp][:, :],
                                 start=(c == 0), stop=(c == 7)))
                mm_done[cp] = pe.n
            pe.wait(A_); pe.wait(V)
            _ln_stats_math(nc, dve, act, stat, pss, psq, ceps[:, :],
                           want_rms=want_rms)

        xload_done = [0, 0]
        for t in range(NTT):
            sl = slice(t * TT, (t + 1) * TT)
            dve.wait(SY)
            for c in range(8):
                ld.inc(SY.dma_start(out=x_sb[c][:, :], in_=x33[c][:, sl]), 16)
            ld.wait(V); ld.wait(A_); ld.wait(PE)
            ln_stats([x_sb[c][:, :] for c in range(8)], ps_s[:, :], ps_q[:, :])
            for c in range(8):
                dve.inc(V.tensor_mul(out=stat["t1"][:, :], in0=x_sb[c][:, :],
                                     in1=stat["ar"][:, :]))
                dve.wait(V)
                dve.inc(V.tensor_sub(out=xln_bf[c][:, sl], in0=stat["t1"][:, :],
                                     in1=stat["br"][:, :]))
                dve.wait(V)

        for tc in range(TPC // 128):
            tsl = slice(tc * 128, (tc + 1) * 128)
            act.wait(PE); dve.wait(PE)
            for c in range(8):
                pe.inc(PE.matmul(out=ps_r[:, :], lhsT=xln_bf[c][:, tsl],
                                 rhs=rw_sb[c][:, :], start=(c == 0), stop=(c == 7)))
            pe.wait(A_); pe.wait(V)
            act.inc(A_.activation(out=lg[:, :], in_=ps_r[:, :], func=AF.Identity))
            act.wait(V)
            dve.inc(V.tensor_add(out=lg[:, :], in0=lg[:, :], in1=rb_sb[:, :]))
            dve.wait(V)
            dve.inc(V.tensor_reduce(out=m1c[:, :], in_=lg[:, :],
                                    axis=mybir.AxisListType.X, op=ALU.max))
            dve.wait(V)
            dve.inc(V.tensor_scalar(out=ge1[:, :], in0=lg[:, :], scalar1=m1c[:, 0:1],
                                    scalar2=None, op0=ALU.is_ge))
            dve.wait(V)
            dve.inc(V.scalar_tensor_tensor(out=msk[:, :], in0=ge1[:, :],
                                           scalar=cneg[:, :], in1=lg[:, :],
                                           op0=ALU.mult, op1=ALU.add))
            dve.wait(V)
            dve.inc(V.tensor_reduce(out=m2c[:, :], in_=msk[:, :],
                                    axis=mybir.AxisListType.X, op=ALU.max))
            dve.wait(V)
            dve.inc(V.tensor_scalar(out=ge1[:, :], in0=lg[:, :], scalar1=m2c[:, 0:1],
                                    scalar2=None, op0=ALU.is_ge))
            dve.wait(V)
            dve.inc(V.tensor_scalar_mul(out=nm1[:, :], in0=m1c[:, :],
                                        scalar1=cm1[:, :]))
            dve.wait(V)
            dve.inc(V.tensor_sub(out=d21[:, :], in0=m2c[:, :], in1=m1c[:, :]))
            dve.wait(V); dve.wait(A_)
            act.inc(A_.activation(out=ee[:, :], in_=lg[:, :], func=AF.Exp,
                                  bias=nm1[:, :]))
            act.inc(A_.activation(out=d21[:, :], in_=d21[:, :], func=AF.Exp))
            act.wait(V)
            dve.inc(V.tensor_add(out=d21[:, :], in0=d21[:, :], in1=cone[:, :]))
            dve.wait(V); dve.wait(A_)
            act.inc(A_.activation(out=d21[:, :], in_=d21[:, :], func=AF.Ln))
            act.inc(A_.activation(out=rinv[:, :], in_=d21[:, :], func=AF.Exp,
                                  scale=-1.0))
            act.wait(V)
            dve.inc(V.tensor_mul(out=cc_sb[:, :], in0=ee[:, :], in1=ge1[:, :]))
            dve.wait(V)
            dve.inc(V.tensor_scalar_mul(out=cc_sb[:, :], in0=cc_sb[:, :],
                                        scalar1=rinv[:, 0:1]))
            dve.wait(V); dve.wait(PE)
            pe.inc(PE.transpose(out=ps_ct[:, :], in_=cc_sb[:, :].bitcast(F32) if False else cc_sb[:, :],
                                identity=idf_sb[:, :]))
            pe.wait(A_)
            act.inc(A_.activation(out=cfm[:, tsl], in_=ps_ct[:, :], func=AF.Identity))
            act.wait(PE); act.wait(V)

        act.wait(SY)
        st.inc(SY.dma_start(out=cfm_dram[:, :], in_=cfm[:, :]), 16)

        for t in range(NTT):
            sl = slice(t * TT, (t + 1) * TT)
            for do in range(8):
                pp = do % 2
                act.wait(PE)
                pe.inc(PE.matmul(out=ps_y[pp][:, :], lhsT=b2_sb[:, do * 128:(do + 1) * 128],
                                 rhs=cfm[:, sl], start=True, stop=True))
                pe.wait(A_)
                act.inc(A_.activation(out=y_acc[do][:, sl], in_=ps_y[pp][:, :],
                                      func=AF.Identity))
        act.wait(V); act.wait(PE)

        w1_3 = w1_in.rearrange("e (c p) f -> e c p f", p=128)
        w2_3 = w2_in.rearrange("e (c p) d -> e c p d", p=128)
        st.wait(G)
        for e in range(8):
            dve.wait(G)
            gld.inc(G.dma_start(out=crep_all[:, :],
                                in_=cfm_dram[e:e + 1, :].broadcast_to([128, TPC])), 16)
            gld.wait(V)
            act.wait(SY)
            ld.inc(SY.dma_start(out=b1_sb[:, :], in_=b1_in[e]), 16)
            for fs in range(NFS):
                gld.wait(G) if False else None
                pe.wait(G)
                for c in range(8):
                    gld.inc(G.dma_start(out=w1_bf[c][:, :],
                                        in_=w1_3[e, c][:, fs * FSL:(fs + 1) * FSL]), 16)
                for fc in range(8):
                    gld.inc(G.dma_start(out=w2_bf[fc][:, :],
                                        in_=w2_3[e, fs * 8 + fc]), 16)
                gld.wait(PE)
                ld.wait(A_)
                for t in range(NTT):
                    sl = slice(t * TT, (t + 1) * TT)
                    for fo in range(8):
                        act.wait(PE)
                        dve.wait(PE)
                        for c in range(8):
                            pe.inc(PE.matmul(out=ps_h[:, :],
                                             lhsT=w1_bf[c][:, fo * 128:(fo + 1) * 128],
                                             rhs=xln_bf[c][:, sl],
                                             start=(c == 0), stop=(c == 7)))
                        pe.wait(A_)
                        act.inc(A_.activation(out=h_bf[fo][:, :], in_=ps_h[:, :],
                                              func=AF.Relu,
                                              bias=b1_sb[:, fs * 8 + fo:fs * 8 + fo + 1]))
                    act.wait(PE)
                    for do in range(8):
                        pp = do % 2
                        dve.wait(PE)
                        for fc in range(8):
                            pe.inc(PE.matmul(out=ps_y[pp][:, :],
                                             lhsT=w2_bf[fc][:, do * 128:(do + 1) * 128],
                                             rhs=h_bf[fc][:, :],
                                             start=(fc == 0), stop=(fc == 7)))
                        pe.wait(V)
                        dve.inc(V.tensor_mul(out=yev[:, :], in0=ps_y[pp][:, :],
                                             in1=crep_all[:, sl]))
                        dve.wait(V)
                        dve.inc(V.tensor_add(out=y_acc[do][:, sl], in0=y_acc[do][:, sl],
                                             in1=yev[:, :]))
                        dve.wait(V)

        for t in range(NTT):
            sl = slice(t * TT, (t + 1) * TT)
            dve.wait(SY); st.wait(SY)
            for c in range(8):
                ld.inc(SY.dma_start(out=x_sb[c][:, :], in_=x33[c][:, sl]), 16)
            ld.wait(V); ld.wait(A_); ld.wait(PE)
            ln_stats([y_acc[c][:, sl] for c in range(8)], ps_s[:, :], ps_q[:, :])
            for c in range(8):
                dve.inc(V.tensor_mul(out=stat["t1"][:, :], in0=y_acc[c][:, sl],
                                     in1=stat["ar"][:, :]))
                dve.wait(V)
                dve.inc(V.tensor_sub(out=stat["t1"][:, :], in0=stat["t1"][:, :],
                                     in1=stat["br"][:, :]))
                dve.wait(V)
                dve.inc(V.tensor_add(out=x_sb[c][:, :], in0=x_sb[c][:, :],
                                     in1=stat["t1"][:, :]))
                dve.wait(V); dve.wait(SY)
                st.inc(SY.dma_start(out=o3[c][:, sl], in_=x_sb[c][:, :]), 16)
        st.wait(SY)
    return nc


def run_l5(inp, x3_slices):
    if "l5" not in _CACHE:
        _CACHE["l5"] = build_l5()
    nc = _CACHE["l5"]
    shared = {
        "router_w": np.ascontiguousarray(inp["L1_ffn_router_w"]),
        "rb_rep": np.ascontiguousarray(np.broadcast_to(
            inp["L1_ffn_router_b"].reshape(1, 8), (128, 8))).astype(np.float32),
        "W1": np.ascontiguousarray(inp["L1_ffn_W1"]),
        "b1_t": np.ascontiguousarray(
            inp["L1_ffn_b1"].reshape(8, 32, 128).transpose(0, 2, 1)),
        "W2": np.ascontiguousarray(inp["L1_ffn_W2"]),
        "b2": np.ascontiguousarray(inp["L1_ffn_b2"]),
        "ones_bf": np.ones((128, 128), dtype=BF),
        "ones_f": np.ones((128, 128), dtype=np.float32),
        "ident_f": np.eye(128, dtype=np.float32),
    }
    in_maps = []
    for core in range(NCORE):
        m = dict(shared)
        m["x3_fm"] = x3_slices[core]
        in_maps.append(m)
    res = run_bass_kernel_spmd(nc, in_maps, core_ids=list(range(NCORE)))
    return res.results


def kernel(**inp):
    inp = {k: np.asarray(v) for k, v in inp.items()}
    x = inp["L0_attn_x"][0]

    l1 = run_l1(inp)
    u_full = np.empty((E, S), dtype=BF)
    dt_full = np.empty((E, S), dtype=BF)
    zs_full = np.empty((E, S), dtype=BF)
    B_full = np.empty((NST, S), dtype=np.float32)
    C_full = np.empty((NST, S), dtype=np.float32)
    r1m1 = []
    for core in range(NCORE):
        rows = _tok_rows(core)
        r = l1[core]
        u_full[:, rows] = r["u_bf"]
        dt_full[:, rows] = r["dt_bf"]
        zs_full[:, rows] = r["zs_bf"]
        B_full[:, rows] = r["xdbl"][64:80]
        C_full[:, rows] = r["xdbl"][80:96]
        r1m1.append(r["r1m1"])

    yg_full = run_l2(u_full, dt_full, zs_full, B_full, C_full,
                     inp["L0_attn_A_log"], inp["L0_attn_D"])

    x_slices = [np.ascontiguousarray(x[_tok_rows(c)].T) for c in range(NCORE)]
    l3 = run_l3(inp, x_slices, yg_full, r1m1)
    k_full = np.empty((DM, S), dtype=BF)
    v_full = np.empty((S, DM), dtype=BF)
    q_slices, x2_slices = [], []
    for core in range(NCORE):
        rows = _tok_rows(core)
        r = l3[core]
        k_full[:, rows] = r["k_bf"]
        v_full[rows] = r["v_bf"]
        q_slices.append(r["q_bf"])
        x2_slices.append(r["x2_fm"])

    l4 = run_l4(q_slices, k_full, v_full, x2_slices)
    x3_slices = [l4[c]["x3_fm"] for c in range(NCORE)]

    l5 = run_l5(inp, x3_slices)
    out = np.empty((S, DM), dtype=np.float32)
    for core in range(NCORE):
        out[_tok_rows(core)] = l5[core]["out_fm"].T
    return out.reshape(1, S, DM)


from concourse.bass_core_rust import IndirectOffsetOnAxis

I32 = mybir.dt.int32


def _chunk_loc(c):
    r = min(c, NCHUNK - 1 - c)
    return r, (0 if c < NCORE else CH)


def build_fused():
    nc = bass.Bass("TRN2", target_bir_lowering=False)
    TT = 512
    NTT = TPC // TT
    NT2 = S // TT
    DFF = 4096
    FSL = 512
    NFS = DFF // FSL
    KT = 1024

    x_in = nc.dram_tensor("x_fm", [DM, TPC], F32, kind="ExternalInput").ap()
    xh_in = nc.dram_tensor("x_halo", [DM, 8], F32, kind="ExternalInput").ap()
    wi_in = nc.dram_tensor("w_in_T", [DM, 2 * E], F32, kind="ExternalInput").ap()
    nw_in = nc.dram_tensor("nw_t", [128, 8], F32, kind="ExternalInput").ap()
    cw_in = nc.dram_tensor("conv_w_t", [128, 16, 4], F32, kind="ExternalInput").ap()
    cb_in = nc.dram_tensor("conv_b_t", [128, 16], F32, kind="ExternalInput").ap()
    xp_in = nc.dram_tensor("xproj_T", [E, 96], F32, kind="ExternalInput").ap()
    dtw_in = nc.dram_tensor("dtw_T", [RLOW, E], F32, kind="ExternalInput").ap()
    dtb_in = nc.dram_tensor("dtb_t", [128, 16], F32, kind="ExternalInput").ap()
    Al_in = nc.dram_tensor("A_log_t", [2, 128, NST], F32, kind="ExternalInput").ap()
    D_in = nc.dram_tensor("D_t", [128, 2], F32, kind="ExternalInput").ap()
    ow_in = nc.dram_tensor("outw_T", [E, DM], F32, kind="ExternalInput").ap()
    w0_in = nc.dram_tensor("w0_T", [DM, DM], F32, kind="ExternalInput").ap()
    al_in = nc.dram_tensor("alpha_rep", [128, 1], F32, kind="ExternalInput").ap()
    fb_in = nc.dram_tensor("fb_t", [128, 8], F32, kind="ExternalInput").ap()
    wq_in = nc.dram_tensor("wq_T", [DM, DM], F32, kind="ExternalInput").ap()
    wk_in = nc.dram_tensor("wk_T", [DM, DM], F32, kind="ExternalInput").ap()
    wv_in = nc.dram_tensor("wv_T", [DM, DM], F32, kind="ExternalInput").ap()
    mb_in = nc.dram_tensor("mask_big", [128, 1920], F32, kind="ExternalInput").ap()
    sel_in = nc.dram_tensor("sel_rep", [128, 32], F32, kind="ExternalInput").ap()
    binv_in = nc.dram_tensor("binv_rep", [128, 32], F32, kind="ExternalInput").ap()
    rw_in = nc.dram_tensor("router_w", [DM, 8], F32, kind="ExternalInput").ap()
    rb_in = nc.dram_tensor("rb_rep", [128, 8], F32, kind="ExternalInput").ap()
    w1_in = nc.dram_tensor("W1", [8, DM, DFF], F32, kind="ExternalInput").ap()
    b1_in = nc.dram_tensor("b1_t", [8, 128, 32], F32, kind="ExternalInput").ap()
    w2_in = nc.dram_tensor("W2", [8, DFF, DM], F32, kind="ExternalInput").ap()
    b2_in = nc.dram_tensor("b2", [8, DM], F32, kind="ExternalInput").ap()
    ones_in = nc.dram_tensor("ones_bf", [128, 128], BF16, kind="ExternalInput").ap()
    onesf_in = nc.dram_tensor("ones_f", [128, 128], F32, kind="ExternalInput").ap()
    idb_in = nc.dram_tensor("ident_bf", [128, 128], BF16, kind="ExternalInput").ap()
    idf_in = nc.dram_tensor("ident_f", [128, 128], F32, kind="ExternalInput").ap()
    idx0_in = nc.dram_tensor("idx0", [128, 1], I32, kind="ExternalInput").ap()
    dtws_in = nc.dram_tensor("dtw_sl", [RLOW, EPC], F32, kind="ExternalInput").ap()
    dtbs_in = nc.dram_tensor("dtb_sl", [128, 2], F32, kind="ExternalInput").ap()
    idxq_in = nc.dram_tensor("idxq", [128, 2], I32, kind="ExternalInput").ap()

    out_out = nc.dram_tensor("out_fm", [DM, TPC], F32, kind="ExternalOutput").ap()

    ag1_in = nc.dram_tensor("ag1_in", [2 * E, TPC], BF16).ap()
    ag1_out = nc.dram_tensor("ag1_out", [NCORE * 2 * E, TPC], BF16, addr_space="Shared").ap()
    ag2_in = nc.dram_tensor("ag2_in", [NTT * 96, TT], F32).ap()
    ag2_out = nc.dram_tensor("ag2_out", [NCORE * NTT * 96, TT], F32, addr_space="Shared").ap()
    ag3_in = nc.dram_tensor("ag3_in", [NCHUNK * EPC, CH], BF16).ap()
    ag3_out = nc.dram_tensor("ag3_out", [NCORE * NCHUNK * EPC, CH], BF16, addr_space="Shared").ap()
    ag4_in = nc.dram_tensor("ag4_in", [DM, TPC], BF16).ap()
    ag4_out = nc.dram_tensor("ag4_out", [NCORE * DM, TPC], BF16, addr_space="Shared").ap()
    ag5_in = nc.dram_tensor("ag5_in", [TPC, DM], BF16).ap()
    ag5_out = nc.dram_tensor("ag5_out", [NCORE * TPC, DM], BF16, addr_space="Shared").ap()
    r1m1_loc = nc.dram_tensor("r1m1_loc", [2, TPC], F32).ap()
    dt_loc = nc.dram_tensor("dt_loc", [EPC, S], BF16).ap()
    x2_loc = nc.dram_tensor("x2_loc", [DM, TPC], F32).ap()
    q_loc = nc.dram_tensor("q_loc", [DM, TPC], BF16).ap()
    x3_loc = nc.dram_tensor("x3_loc", [DM, TPC], F32).ap()
    cfm_dram = nc.dram_tensor("cfm_scr", [8, TPC], F32).ap()

    x3r = x_in.rearrange("(c p) t -> c p t", p=128)
    xh3 = xh_in.rearrange("(c p) t -> c p t", p=128)
    wi3 = wi_in.rearrange("(c p) e -> c p e", p=128)
    xp3 = xp_in.rearrange("(c p) f -> c p f", p=128)
    u3 = ag1_in[0:E, :].rearrange("(c p) t -> c p t", p=128)
    zs3 = ag1_in[E:2 * E, :].rearrange("(c p) t -> c p t", p=128)
    yg3d = ag3_in.rearrange("(k p) t -> k p t", p=128)
    ow3 = ow_in.rearrange("(c p) d -> c p d", p=128)
    w03 = w0_in.rearrange("(c p) d -> c p d", p=128)
    wq3 = wq_in.rearrange("(c p) d -> c p d", p=128)
    wk3 = wk_in.rearrange("(c p) d -> c p d", p=128)
    wv3 = wv_in.rearrange("(c p) d -> c p d", p=128)
    x23 = x2_loc.rearrange("(c p) t -> c p t", p=128)
    q3 = q_loc.rearrange("(c p) t -> c p t", p=128)
    k3w = ag4_in.rearrange("(c p) t -> c p t", p=128)
    v3w = ag5_in.rearrange("(tc p) d -> tc p d", p=128)
    k3r = ag4_out.rearrange("(r c p) t -> r c p t", p=128, c=8)
    v3r = ag5_out.rearrange("(r tc p) d -> r tc p d", p=128, tc=16)
    x33 = x3_loc.rearrange("(c p) t -> c p t", p=128)
    dtl3 = dt_loc.rearrange("(c p) t -> c p t", p=128)
    rw3 = rw_in.rearrange("(c p) e -> c p e", p=128)
    o3 = out_out.rearrange("(c p) t -> c p t", p=128)
    w1_3 = w1_in.rearrange("e (c p) f -> e c p f", p=128)
    w2_3 = w2_in.rearrange("e (c p) d -> e c p d", p=128)

    with contextlib.ExitStack() as es:
        def sb(name, shape, dt=F32, st_=None):
            return (st_ or es).enter_context(nc.sbuf_tensor(name, shape, dt))

        def psum(name, shape, dt=F32, st_=None):
            return (st_ or es).enter_context(nc.psum_tensor(name, shape, dt))

        ones_sb = sb("ones", [128, 128], BF16)
        onesf_sb = sb("onesf", [128, 128], F32)
        idb_sb = sb("identb", [128, 128], BF16)
        idf_sb = sb("identf", [128, 128], F32)
        ceps = sb("ceps", [128, 1])
        cm1 = sb("cm1", [128, 1])
        cone = sb("cone", [128, 1])
        cneg = sb("cneg", [128, 1])
        cn32 = sb("cn32", [128, 1])
        stat = {k: sb("st_" + k, [128, TT])
                for k in ["m", "q", "v", "r", "sr", "ar", "br", "t1", "t2"]}
        xb_scr2 = [sb(f"xb{i}", [128, TT], BF16) for i in range(2)]
        sq_scr2 = [sb(f"sqs{i}", [128, TT], BF16) for i in range(2)]
        xb_scr = xb_scr2[0]
        sq_scr = sq_scr2[0]
        idx0_sb = sb("idx0_sb", [128, 1], I32)
        idxq_sb = sb("idxq_sb", [128, 2], I32)

        ld = Ctr(es, nc, "ld")
        gld = Ctr(es, nc, "gld")
        pe = Ctr(es, nc, "pe")
        act = Ctr(es, nc, "act")
        dve = Ctr(es, nc, "dve")
        st = Ctr(es, nc, "st")
        cc = [es.enter_context(nc.semaphore(f"cc{i}")) for i in range(5)]
        sx = [Ctr(es, nc, f"sx{i}") for i in range(2)]
        su = [Ctr(es, nc, f"su{i}") for i in range(2)]
        sbc = [Ctr(es, nc, f"sbc{i}") for i in range(2)]
        sx3 = [Ctr(es, nc, f"sx3{i}") for i in range(2)]
        syg = [Ctr(es, nc, f"syg{i}") for i in range(2)]
        sr1 = Ctr(es, nc, "sr1")
        sq4 = Ctr(es, nc, "sq4")
        skv = [Ctr(es, nc, f"skv{i}") for i in range(2)]
        sw5 = [Ctr(es, nc, f"sw5{i}") for i in range(2)]
        scr5 = Ctr(es, nc, "scr5")

        SY, G, PE, A_, V = nc.sync, nc.gpsimd, nc.tensor, nc.scalar, nc.vector

        ld.inc(SY.dma_start(out=ones_sb[:, :], in_=ones_in[:, :]), 16)
        ld.inc(SY.dma_start(out=onesf_sb[:, :], in_=onesf_in[:, :]), 16)
        ld.inc(SY.dma_start(out=idb_sb[:, :], in_=idb_in[:, :]), 16)
        ld.inc(SY.dma_start(out=idf_sb[:, :], in_=idf_in[:, :]), 16)
        ld.inc(SY.dma_start(out=idx0_sb[:, :], in_=idx0_in[:, :]), 16)
        ld.inc(SY.dma_start(out=idxq_sb[:, :], in_=idxq_in[:, :]), 16)
        ld.wait(V)
        dve.inc(V.memset(ceps[:, :], EPS))
        dve.inc(V.memset(cm1[:, :], -1.0))
        dve.inc(V.memset(cone[:, :], 1.0))
        dve.inc(V.memset(cneg[:, :], -1e30))
        dve.inc(V.memset(cn32[:, :], -1.0 / 32.0))
        dve.wait(V); dve.wait(A_); dve.wait(PE)

        def ln_stats(src_tiles, pss, psq, want_rms=False):
            mm_done = [pe.n, pe.n]
            for c in range(8):
                cp = c % 2
                pe.wait(V, mm_done[cp])
                pe.wait(A_, mm_done[cp])
                dve.inc(V.tensor_copy(out=xb_scr2[cp][:, :], in_=src_tiles[c]))
                act.inc(A_.activation(out=sq_scr2[cp][:, :], in_=src_tiles[c], func=AF.Square))
                dve.wait(PE); act.wait(PE)
                pe.inc(PE.matmul(out=pss, lhsT=ones_sb[:, :], rhs=xb_scr2[cp][:, :],
                                 start=(c == 0), stop=(c == 7)))
                pe.inc(PE.matmul(out=psq, lhsT=ones_sb[:, :], rhs=sq_scr2[cp][:, :],
                                 start=(c == 0), stop=(c == 7)))
                mm_done[cp] = pe.n
            pe.wait(A_); pe.wait(V)
            _ln_stats_math(nc, dve, act, stat, pss, psq, ceps[:, :],
                           want_rms=want_rms)

        with contextlib.ExitStack() as e1:
            x_sb = [[sb(f"p1x{p}_{c}", [128, TT], st_=e1) for c in range(8)]
                    for p in range(2)]
            w_bf = [sb(f"p1w{c}", [128, 2 * E], BF16, e1) for c in range(8)]
            nw_sb = sb("p1nw", [128, 8], st_=e1)
            cw_sb = sb("p1cw", [128, 16, 4], st_=e1)
            cb_sb = sb("p1cb", [128, 16], st_=e1)
            xp_bf = [sb(f"p1xp{c}", [128, 96], BF16, e1) for c in range(16)]
            xin_bf = [sb(f"p1xin{c}", [128, TPC], BF16, e1) for c in range(8)]
            ur = [sb(f"p1ur{p}", [128, TT + 3], st_=e1) for p in range(2)]
            halo = sb("p1halo", [128, 16, 3], st_=e1)
            halo2 = sb("p1halo2", [128, 16, 3], st_=e1)
            xh_sb = [sb(f"p1xh{c}", [128, 8], st_=e1) for c in range(8)]
            xinh_bf = [sb(f"p1xinh{c}", [128, 8], BF16, e1) for c in range(8)]
            uh_scr = sb("p1uh", [128, 8], st_=e1)
            cacc = [sb(f"p1cacc{p}", [128, TT], st_=e1) for p in range(2)]
            ubf_sb = [sb(f"p1ubf{p}", [128, TT], BF16, e1) for p in range(2)]
            zbf_sb = [sb(f"p1zbf{p}", [128, TT], BF16, e1) for p in range(2)]
            xdbl_sb = sb("p1xdbl", [96, TT], st_=e1)
            ps_u = [psum(f"p1ps_u{p}", [128, TT], st_=e1) for p in range(2)]
            ps_x = psum("p1ps_x", [96, TT], st_=e1)
            ps_s = psum("p1ps_s", [128, TT], st_=e1)
            ps_q = psum("p1ps_q", [128, TT], st_=e1)

            ld.inc(SY.dma_start(out=nw_sb[:, :], in_=nw_in[:, :]), 16)
            ld.inc(SY.dma_start(out=cw_sb[:, :, :], in_=cw_in[:, :, :]), 16)
            ld.inc(SY.dma_start(out=cb_sb[:, :], in_=cb_in[:, :]), 16)
            for c in range(8):
                ld.inc(SY.dma_start(out=xh_sb[c][:, :], in_=xh3[c]), 16)
            for c in range(16):
                gld.inc(G.dma_start(out=xp_bf[c][:, :], in_=xp3[c]), 16)
            for c in range(8):
                gld.inc(G.dma_start(out=w_bf[c][:, :], in_=wi3[c]), 16)
            ld.wait(V)
            gld.wait(V)
            for c in range(8):
                dve.inc(V.tensor_scalar_mul(out=w_bf[c][:, :], in0=w_bf[c][:, :],
                                            scalar1=nw_sb[:, c:c + 1]))
            dve.inc(V.memset(halo[:, :, :], 0.0))
            dve.wait(V)

            sh = {k: stat[k][:, 0:8] for k in stat}
            ld.wait(V); ld.wait(A_); ld.wait(PE)
            for c in range(8):
                pe.wait(V); pe.wait(A_)
                dve.inc(V.tensor_copy(out=xb_scr[:, 0:8], in_=xh_sb[c][:, :]))
                act.inc(A_.activation(out=sq_scr[:, 0:8], in_=xh_sb[c][:, :], func=AF.Square))
                dve.wait(PE); act.wait(PE)
                pe.inc(PE.matmul(out=ps_s[:, 0:8], lhsT=ones_sb[:, :], rhs=xb_scr[:, 0:8],
                                 start=(c == 0), stop=(c == 7)))
                pe.inc(PE.matmul(out=ps_q[:, 0:8], lhsT=ones_sb[:, :], rhs=sq_scr[:, 0:8],
                                 start=(c == 0), stop=(c == 7)))
            pe.wait(A_); pe.wait(V)
            _ln_stats_math(nc, dve, act, sh, ps_s[:, 0:8], ps_q[:, 0:8], ceps[:, :],
                           want_rms=True)
            for c in range(8):
                dve.inc(V.tensor_mul(out=sh["t1"], in0=xh_sb[c][:, :], in1=sh["ar"]))
                dve.wait(V)
                dve.inc(V.tensor_sub(out=xinh_bf[c][:, :], in0=sh["t1"], in1=sh["br"]))
                dve.wait(V)
            nc.all_engine_barrier()
            for eo in range(16):
                for c in range(8):
                    pe.inc(PE.matmul(out=ps_u[0][:, 0:8],
                                     lhsT=w_bf[c][:, eo * 128:(eo + 1) * 128],
                                     rhs=xinh_bf[c][:, :], start=(c == 0), stop=(c == 7)))
                pe.wait(A_)
                act.inc(A_.activation(out=uh_scr[:, :], in_=ps_u[0][:, 0:8], func=AF.Identity))
                act.wait(V)
                dve.inc(V.tensor_copy(out=halo[:, eo, :], in_=uh_scr[:, 1:4]))
                dve.inc(V.tensor_copy(out=halo2[:, eo, :], in_=uh_scr[:, 5:8]))
                dve.wait(PE)
            nc.all_engine_barrier()

            xload_done = [0, 0]
            xuse_done = [0, 0]
            for t in range(NTT):
                sl = slice(t * TT, (t + 1) * TT)
                p = t % 2
                dve.wait(SY, xuse_done[p])
                for c in range(8):
                    sx[p].inc(SY.dma_start(out=x_sb[p][c][:, :], in_=x3r[c][:, sl]), 16)
                for c in range(8):
                    pe.wait(V)
                    pe.wait(A_)
                    sx[p].wait(V)
                    sx[p].wait(A_)
                    dve.inc(V.tensor_copy(out=xb_scr[:, :], in_=x_sb[p][c][:, :]))
                    act.inc(A_.activation(out=sq_scr[:, :], in_=x_sb[p][c][:, :], func=AF.Square))
                    dve.wait(PE)
                    act.wait(PE)
                    pe.inc(PE.matmul(out=ps_s[:, :], lhsT=ones_sb[:, :], rhs=xb_scr[:, :],
                                     start=(c == 0), stop=(c == 7)))
                    pe.inc(PE.matmul(out=ps_q[:, :], lhsT=ones_sb[:, :], rhs=sq_scr[:, :],
                                     start=(c == 0), stop=(c == 7)))
                pe.wait(A_)
                pe.wait(V)
                _ln_stats_math(nc, dve, act, stat, ps_s[:, :], ps_q[:, :], ceps[:, :],
                               want_rms=True)
                dve.wait(SY)
                act.wait(SY)
                st.inc(SY.dma_start(out=r1m1_loc[0:1, sl], in_=stat["r"][0:1, :]), 16)
                st.inc(SY.dma_start(out=r1m1_loc[1:2, sl], in_=stat["m"][0:1, :]), 16)
                for c in range(8):
                    dve.inc(V.tensor_mul(out=stat["t1"][:, :], in0=x_sb[p][c][:, :],
                                         in1=stat["ar"][:, :]))
                    dve.inc(V.tensor_sub(out=xin_bf[c][:, sl], in0=stat["t1"][:, :],
                                         in1=stat["br"][:, :]))
                xuse_done[p] = dve.n
                st.wait(A_)
                st.wait(V)
            nc.all_engine_barrier()

            evac_done = [0, 0]
            store_done = {}
            silu_done = [0, 0]
            xproj_done = [0, 0]
            for t in range(NTT):
                sl = slice(t * TT, (t + 1) * TT)
                pend_xproj = None
                for eo in range(32):
                    p = eo % 2
                    act.wait(PE, evac_done[p])
                    for c in range(8):
                        pe.inc(PE.matmul(out=ps_u[p][:, :],
                                         lhsT=w_bf[c][:, eo * 128:(eo + 1) * 128],
                                         rhs=xin_bf[c][:, sl],
                                         start=(c == 0), stop=(c == 7)))
                    if pend_xproj is not None:
                        peo = pend_xproj
                        pp2 = peo % 2
                        act.wait(PE, silu_done[pp2])
                        pe.inc(PE.matmul(out=ps_x[:, :], lhsT=xp_bf[peo][:, :],
                                         rhs=ubf_sb[pp2][:, :],
                                         start=(peo == 0), stop=(peo == 15)))
                        xproj_done[pp2] = pe.n
                        pend_xproj = None
                    if eo < 16:
                        pe.wait(A_)
                        dve.wait(A_)
                        act.inc(A_.activation(out=ur[p][:, 3:3 + TT], in_=ps_u[p][:, :],
                                              func=AF.Identity))
                        evac_done[p] = act.n
                        act.wait(V)
                        hsrc = halo2 if t == 2 else halo
                        dve.inc(V.tensor_copy(out=ur[p][:, 0:3], in_=hsrc[:, eo, :]))
                        dve.wait(V)
                        dve.inc(V.tensor_scalar_mul(out=cacc[p][:, :], in0=ur[p][:, 0:TT],
                                                    scalar1=cw_sb[:, eo, 0:1]))
                        for j in range(1, 4):
                            dve.inc(V.scalar_tensor_tensor(
                                out=cacc[p][:, :], in0=ur[p][:, j:j + TT],
                                scalar=cw_sb[:, eo, j:j + 1], in1=cacc[p][:, :],
                                op0=ALU.mult, op1=ALU.add))
                        dve.inc(V.tensor_copy(out=halo[:, eo, :], in_=ur[p][:, TT:TT + 3]))
                        key = ("u", p)
                        st.wait(A_, store_done.get(key, 0))
                        pe.wait(A_, xproj_done[p])
                        dve.wait(A_)
                        act.inc(A_.activation(out=ubf_sb[p][:, :], in_=cacc[p][:, :],
                                              func=AF.Silu, bias=cb_sb[:, eo:eo + 1]))
                        silu_done[p] = act.n
                        act.wait(SY)
                        st.inc(SY.dma_start(out=u3[eo][:, sl], in_=ubf_sb[p][:, :]), 16)
                        store_done[key] = st.n
                        pend_xproj = eo
                    else:
                        if pend_xproj is not None:
                            pass
                        key = ("z", p)
                        st.wait(A_, store_done.get(key, 0))
                        pe.wait(A_)
                        act.inc(A_.activation(out=zbf_sb[p][:, :], in_=ps_u[p][:, :],
                                              func=AF.Silu))
                        evac_done[p] = act.n
                        act.wait(SY)
                        st.inc(SY.dma_start(out=zs3[eo - 16][:, sl], in_=zbf_sb[p][:, :]), 16)
                        store_done[key] = st.n
                pe.wait(A_)
                act.inc(A_.activation(out=xdbl_sb[:, :], in_=ps_x[:, :], func=AF.Identity))
                act.wait(SY)
                st.inc(SY.dma_start(out=ag2_in[t * 96:(t + 1) * 96, :], in_=xdbl_sb[:, :]), 16)
                st.wait(A_)
            st.wait(SY)
            st.wait(G)
            nc.all_engine_barrier()

        G.collective_compute("AllGather", ALU.bypass, replica_groups=[list(range(NCORE))],
                             ins=[ag2_in.opt()], outs=[ag2_out.opt()]).then_inc(cc[1])
        G.collective_compute("AllGather", ALU.bypass, replica_groups=[list(range(NCORE))],
                             ins=[ag1_in.opt()], outs=[ag1_out.opt()]).then_inc(cc[0])
        G.wait_ge(cc[1], 1)
        nc.all_engine_barrier()

        with contextlib.ExitStack() as e2s:
            A_t = [sb(f"p2A{e}", [128, NST], st_=e2s) for e in range(2)]
            A_scr = sb("p2Ascr", [128, NST], st_=e2s)
            D_sb = sb("p2D", [128, 2], st_=e2s)
            hc = [sb(f"p2hc{e}", [128, NST], BF16, e2s) for e in range(2)]
            dt_sb = [sb(f"p2dt{p}", [128, TT], BF16, e2s) for p in range(2)]
            u_sb = [sb(f"p2u{p}", [128, TT], BF16, e2s) for p in range(2)]
            zs_sb = [sb(f"p2zs{p}", [128, TT], BF16, e2s) for p in range(2)]
            Br = [sb(f"p2Br{p}", [128, NST, TT], BF16, e2s) for p in range(2)]
            Cr = [sb(f"p2Cr{p}", [128, NST, TT], BF16, e2s) for p in range(2)]
            dtu = sb("p2dtu", [128, TT], BF16, e2s)
            dA = [sb(f"p2dA{p}", [128, NST, TT], BF16, e2s) for p in range(2)]
            bt = sb("p2bt", [128, NST, TT], BF16, e2s)
            dtw_bf = sb("p2dtw", [RLOW, EPC], BF16, e2s)
            dtb_sb = sb("p2dtb", [128, 2], st_=e2s)
            dtlow_bf = [sb(f"p2dtlow{p}", [RLOW, TT], BF16, e2s) for p in range(2)]
            sp_s1 = sb("p2sps1", [128, TT], st_=e2s)
            sp_ax = sb("p2spax", [128, TT], st_=e2s)
            sp_ln = sb("p2spln", [128, TT], st_=e2s)
            ps_dt = [psum(f"p2ps_dt{p}", [128, TT], st_=e2s) for p in range(2)]
            ht = sb("p2ht", [128, NST, TT], BF16, e2s)
            tr8 = sb("p2tr8", [128, 8, TT], BF16, e2s)
            tr4 = sb("p2tr4", [128, 4, TT], BF16, e2s)
            tr2 = sb("p2tr2", [128, 2, TT], BF16, e2s)
            y_sb = sb("p2y", [128, TT], st_=e2s)
            yd_sb = sb("p2yd", [128, TT], BF16, e2s)
            yg_sb = [sb(f"p2yg{p}", [128, TT], BF16, e2s) for p in range(2)]

            ld.inc(SY.dma_start(out=D_sb[:, :], in_=D_in[:, :]), 16)
            ld.inc(SY.dma_start(out=dtb_sb[:, :], in_=dtbs_in[:, :]), 16)
            gld.inc(G.dma_start(out=dtw_bf[:, :], in_=dtws_in[:, :]), 16)
            for e in range(2):
                ld.inc(SY.dma_start(out=A_t[e][:, :], in_=Al_in[e]), 16)
            ld.wait(V)
            ld.wait(A_)
            gld.wait(PE)
            for e in range(2):
                dve.inc(V.memset(hc[e][:, :], 0.0))
            dve.wait(A_)
            for e in range(2):
                act.inc(A_.activation(out=A_scr[:, :], in_=A_t[e][:, :], func=AF.Exp))
                act.wait(V)
                dve.wait(V)
                dve.inc(V.tensor_scalar_mul(out=A_t[e][:, :], in0=A_scr[:, :],
                                            scalar1=cm1[:, :]))
                dve.wait(V)
                dve.wait(A_)

            dtlow_pe_done = [0, 0]
            dt_st = {}
            for tt in range(NT2):
                pb = tt % 2
                gc = tt // 2
                h2 = tt % 2
                rk, coff = _chunk_loc(gc)
                tb = (0 if gc < NCORE else 2) + h2
                rowb2 = rk * NTT * 96 + tb * 96
                pe.wait(G, dtlow_pe_done[pb])
                sbc[pb].inc(G.dma_start(out=dtlow_bf[pb][:, :],
                                        in_=ag2_out[rowb2:rowb2 + 64, :]), 16)
                for ec in range(2):
                    p = (2 * tt + ec) % 2
                    sbc[pb].wait(PE)
                    act.wait(PE, dt_st.get(("ps", p), 0))
                    pe.inc(PE.matmul(out=ps_dt[p][:, :],
                                     lhsT=dtw_bf[:, ec * 128:(ec + 1) * 128],
                                     rhs=dtlow_bf[pb][:, :], start=True, stop=True))
                    dtlow_pe_done[pb] = pe.n
                    pe.wait(A_)
                    dve.wait(A_)
                    act.inc(A_.activation(out=sp_s1[:, :], in_=ps_dt[p][:, :],
                                          func=AF.Identity, bias=dtb_sb[:, ec:ec + 1]))
                    dt_st[("ps", p)] = act.n
                    act.inc(A_.activation(out=sp_ax[:, :], in_=sp_s1[:, :], func=AF.Abs))
                    act.inc(A_.activation(out=sp_ax[:, :], in_=sp_ax[:, :], func=AF.Exp,
                                          scale=-1.0))
                    act.inc(A_.activation(out=sp_ln[:, :], in_=sp_ax[:, :], func=AF.Ln,
                                          bias=cone[:, :]))
                    act.inc(A_.activation(out=sp_ax[:, :], in_=sp_s1[:, :], func=AF.Relu))
                    st.wait(V, dt_st.get(("buf", p), 0))
                    act.wait(V)
                    dve.inc(V.tensor_add(out=dt_sb[p][:, :], in0=sp_ax[:, :],
                                         in1=sp_ln[:, :]))
                    dve.wait(SY)
                    st.inc(SY.dma_start(out=dtl3[ec][:, tt * TT:(tt + 1) * TT],
                                        in_=dt_sb[p][:, :]), 16)
                    dt_st[("buf", p)] = st.n
            st.wait(G)
            G.wait_ge(cc[0], 1)
            nc.all_engine_barrier()

            gld_done = [0, 0]
            bcld_done = [0, 0]
            use_done = [0, 0]
            bc_use_done = [0, 0]
            stp_done = [0, 0]
            dA_read_done = [0, 0]
            dA_wr_done = [0, 0]
            dt_done = [0, 0]
            for tt in range(NT2):
                pb = tt % 2
                gc = tt // 2
                h2 = tt % 2
                rk, coff = _chunk_loc(gc)
                colofs = coff + h2 * TT
                tb = (0 if gc < NCORE else 2) + h2
                rowb2 = rk * NTT * 96 + tb * 96
                dve.wait(G, bc_use_done[pb])
                sbc[pb].inc(G.dma_start(out=Br[pb][:, :, :],
                                    in_=ag2_out[rowb2 + 64:rowb2 + 80, :]
                                    .unsqueeze(0).broadcast_to([128, NST, TT])), 16)
                sbc[pb].inc(G.dma_start(out=Cr[pb][:, :, :],
                                    in_=ag2_out[rowb2 + 80:rowb2 + 96, :]
                                    .unsqueeze(0).broadcast_to([128, NST, TT])), 16)
                for ec in range(2):
                    p = (2 * tt + ec) % 2
                    dve.wait(G, use_done[p])
                    act.wait(G, dt_done[p])
                    for ti, tofs in ((0, 0), (1, E)):
                        dst = (u_sb if ti == 0 else zs_sb)[p]
                        eo_elem = ((rk * 2 * E + tofs + ec * 128) * TPC + colofs)
                        su[p].inc(G.indirect_dma_start(
                            out=dst[:, :], out_offset=None,
                            in_=ag1_out[:, :],
                            in_offset=IndirectOffsetOnAxis(idx0_sb[:, 0:1], axis=0),
                            element_offset=eo_elem), 16)
                    su[p].inc(G.dma_start(out=dt_sb[p][:, :],
                                          in_=dtl3[ec][:, tt * TT:(tt + 1) * TT]), 16)
                    su[p].wait(V)
                    dve.inc(V.tensor_mul(out=dtu[:, :], in0=dt_sb[p][:, :], in1=u_sb[p][:, :]))
                    su[p].wait(A_)
                    dve.wait(A_, dA_read_done[p])
                    for n in range(NST):
                        act.inc(A_.activation(out=dA[p][:, n, :], in_=dt_sb[p][:, :],
                                              func=AF.Exp, scale=A_t[ec][:, n:n + 1]))
                    dA_wr_done[p] = act.n
                    dt_done[p] = act.n
                    sbc[pb].wait(V)
                    dve.inc(V.tensor_mul(out=bt[:, :, :],
                                         in0=dtu[:, :].unsqueeze(1).broadcast_to([128, NST, TT]),
                                         in1=Br[pb][:, :, :]))
                    act.wait(V, dA_wr_done[p])
                    for n in range(NST):
                        dve.inc(V.tensor_tensor_scan(out=ht[:, n, :], data0=dA[p][:, n, :],
                                                     data1=bt[:, n, :],
                                                     initial=hc[ec][:, n:n + 1],
                                                     op0=ALU.mult, op1=ALU.add))
                    dve.wait(V)
                    dve.inc(V.tensor_copy(out=hc[ec][:, :], in_=ht[:, :, TT - 1:TT]))
                    dve.wait(V)
                    dve.inc(V.tensor_mul(out=dA[p][:, :, :], in0=ht[:, :, :], in1=Cr[pb][:, :, :]))
                    bc_use_done[pb] = dve.n
                    dve.inc(V.tensor_add(out=tr8[:, :, :], in0=dA[p][:, 0:8, :], in1=dA[p][:, 8:16, :]))
                    dA_read_done[p] = dve.n
                    dve.inc(V.tensor_add(out=tr4[:, :, :], in0=tr8[:, 0:4, :], in1=tr8[:, 4:8, :]))
                    dve.inc(V.tensor_add(out=tr2[:, :, :], in0=tr4[:, 0:2, :], in1=tr4[:, 2:4, :]))
                    dve.inc(V.tensor_add(out=y_sb[:, :], in0=tr2[:, 0, :], in1=tr2[:, 1, :]))
                    dve.inc(V.scalar_tensor_tensor(out=yd_sb[:, :], in0=u_sb[p][:, :],
                                                   scalar=D_sb[:, ec:ec + 1], in1=y_sb[:, :],
                                                   op0=ALU.mult, op1=ALU.add))
                    st.wait(V, stp_done[p])
                    dve.inc(V.tensor_mul(out=yg_sb[p][:, :], in0=yd_sb[:, :], in1=zs_sb[p][:, :]))
                    use_done[p] = dve.n
                    dve.wait(SY)
                    st.inc(SY.dma_start(out=yg3d[gc * 2 + ec][:, h2 * TT:(h2 + 1) * TT],
                                        in_=yg_sb[p][:, :]), 16)
                    stp_done[p] = st.n
            st.wait(SY)
            st.wait(G)
            nc.all_engine_barrier()

        G.collective_compute("AllGather", ALU.bypass, replica_groups=[list(range(NCORE))],
                             ins=[ag3_in.opt()], outs=[ag3_out.opt()]).then_inc(cc[2])
        nc.all_engine_barrier()

        with contextlib.ExitStack() as e3:
            x_sb = [[sb(f"p3x{p}_{c}", [128, TT], st_=e3) for c in range(8)]
                    for p in range(2)]
            ps_a = [psum(f"p3ps_a{p}", [128, TT], st_=e3) for p in range(2)]
            ps_s = psum("p3ps_s", [128, TT], st_=e3)
            ps_q = psum("p3ps_q", [128, TT], st_=e3)

            with contextlib.ExitStack() as e3a:
                ow_bf = [sb(f"p3ow{c}", [128, DM], BF16, e3a) for c in range(16)]
                tern = [sb(f"p3tern{c}", [128, DM], BF16, e3a) for c in range(8)]
                w0f = sb("p3w0f", [128, DM], F32, e3a)
                scrA = sb("p3scrA", [128, DM], F32, e3a)
                scrB = sb("p3scrB", [128, DM], F32, e3a)
                asum = sb("p3asum", [128, 16], st_=e3a)
                delta = sb("p3delta", [128, 1], st_=e3a)
                ndelta = sb("p3ndelta", [128, 1], st_=e3a)
                al_sb = sb("p3al", [128, 1], st_=e3a)
                fb_sb = sb("p3fb", [128, 8], st_=e3a)
                yg_sb = [[sb(f"p3yg{p}_{c}", [128, TT], BF16, e3a) for c in range(16)]
                         for p in range(2)]
                r1_sb = sb("p3r1s", [128, TT], st_=e3a)
                m1_sb = sb("p3m1s", [128, TT], st_=e3a)
                tmp_sb = [sb(f"p3tmp{c}", [128, TT], st_=e3a) for c in range(8)]
                fo_sb = [sb(f"p3fo{c}", [128, TT], st_=e3a) for c in range(8)]
                xln_bf = [sb(f"p3xln{c}", [128, TT], BF16, e3a) for c in range(8)]
                evac = sb("p3evac", [128, TT], st_=e3a)
                ps_d = psum("p3ps_d", [128, 16], st_=e3a)

                G.wait_ge(cc[2], 1)
                ld.inc(SY.dma_start(out=al_sb[:, :], in_=al_in[:, :]), 16)
                ld.inc(SY.dma_start(out=fb_sb[:, :], in_=fb_in[:, :]), 16)
                dve.inc(V.memset(asum[:, :], 0.0))
                dve.wait(V)
                for c in range(16):
                    gld.inc(G.dma_start(out=ow_bf[c][:, :], in_=ow3[c]), 16)
                for c in range(8):
                    ld.inc(SY.dma_start(out=w0f[:, :], in_=w03[c]), 16)
                    ld.wait(A_)
                    act.inc(A_.activation(out=scrA[:, :], in_=w0f[:, :], func=AF.Abs))
                    act.wait(V)
                    dve.inc(V.tensor_reduce(out=asum[:, c:c + 1], in_=scrA[:, :],
                                            axis=mybir.AxisListType.X, op=ALU.add))
                    dve.wait(V)
                    dve.wait(SY); act.wait(SY)
                pe.wait(V)
                dve.wait(PE)
                pe.inc(PE.matmul(out=ps_d[:, :], lhsT=onesf_sb[:, :], rhs=asum[:, :16],
                                 start=True, stop=True))
                pe.wait(V)
                dve.inc(V.tensor_reduce(out=delta[:, :], in_=ps_d[:, 0:8],
                                        axis=mybir.AxisListType.X, op=ALU.add))
                dve.wait(V); dve.wait(A_)
                act.inc(A_.activation(out=delta[:, :], in_=delta[:, :], func=AF.Identity,
                                      scale=0.7 / (DM * DM)))
                act.wait(V)
                dve.inc(V.tensor_scalar_mul(out=ndelta[:, :], in0=delta[:, :],
                                            scalar1=cm1[:, :]))
                dve.wait(V)
                for c in range(8):
                    dve.wait(SY)
                    ld.inc(SY.dma_start(out=w0f[:, :], in_=w03[c]), 16)
                    ld.wait(V)
                    dve.inc(V.tensor_scalar(out=scrA[:, :], in0=w0f[:, :],
                                            scalar1=delta[:, 0:1], scalar2=None,
                                            op0=ALU.is_gt))
                    dve.inc(V.tensor_scalar(out=scrB[:, :], in0=w0f[:, :],
                                            scalar1=ndelta[:, 0:1], scalar2=None,
                                            op0=ALU.is_lt))
                    dve.wait(V)
                    dve.inc(V.tensor_sub(out=tern[c][:, :], in0=scrA[:, :], in1=scrB[:, :]))
                    dve.wait(V)
                gld.wait(PE)
                ld.wait(PE)
                ld.wait(V)

                yld_done = [0, 0]
                yuse_done = [0, 0]
                evac_done = [0, 0]
                r1use_done = 0
                x2store_done = [0, 0]
                for t in range(NTT):
                    sl = slice(t * TT, (t + 1) * TT)
                    p = t % 2
                    qc = t // 2
                    h2 = t % 2
                    dve.wait(SY, yuse_done[p])
                    st.wait(SY, x2store_done[p])
                    for c in range(8):
                        sx3[p].inc(SY.dma_start(out=x_sb[p][c][:, :], in_=x3r[c][:, sl]), 16)
                    dve.wait(G, yuse_done[p])
                    for ec16 in range(16):
                        r = ec16 // 2
                        ecl = ec16 % 2
                        eo_elem = (r * NCHUNK * EPC + ecl * 128) * CH + h2 * TT
                        syg[p].inc(G.indirect_dma_start(
                            out=yg_sb[p][ec16][:, :], out_offset=None,
                            in_=ag3_out[:, :],
                            in_offset=IndirectOffsetOnAxis(idxq_sb[:, qc:qc + 1], axis=0),
                            element_offset=eo_elem), 16)
                    dve.wait(G, r1use_done)
                    sr1.inc(G.dma_start(out=r1_sb[:, :],
                                        in_=r1m1_loc[0:1, sl].broadcast_to([128, TT])), 16)
                    sr1.inc(G.dma_start(out=m1_sb[:, :],
                                        in_=r1m1_loc[1:2, sl].broadcast_to([128, TT])), 16)
                    syg[p].wait(V)
                    sr1.wait(V)
                    sx3[p].wait(V)
                    for c in range(8):
                        dve.inc(V.tensor_sub(out=tmp_sb[c][:, :], in0=x_sb[p][c][:, :],
                                             in1=m1_sb[:, :]))
                        dve.inc(V.tensor_mul(out=tmp_sb[c][:, :], in0=tmp_sb[c][:, :],
                                             in1=r1_sb[:, :]))
                    r1use_done = dve.n
                    for do in range(8):
                        pp = do % 2
                        act.wait(PE, evac_done[pp])
                        syg[p].wait(PE)
                        for ec in range(16):
                            pe.inc(PE.matmul(out=ps_a[pp][:, :],
                                             lhsT=ow_bf[ec][:, do * 128:(do + 1) * 128],
                                             rhs=yg_sb[p][ec][:, :],
                                             start=(ec == 0), stop=(ec == 15)))
                        pe.wait(A_)
                        dve.wait(A_)
                        act.inc(A_.activation(out=evac[:, :], in_=ps_a[pp][:, :],
                                              func=AF.Identity))
                        evac_done[pp] = act.n
                        act.wait(V)
                        dve.inc(V.tensor_add(out=tmp_sb[do][:, :], in0=tmp_sb[do][:, :],
                                             in1=evac[:, :]))
                    yuse_done[p] = dve.n
                    ln_stats([tmp_sb[c][:, :] for c in range(8)], ps_s[:, :], ps_q[:, :])
                    for c in range(8):
                        dve.inc(V.tensor_mul(out=stat["t1"][:, :], in0=tmp_sb[c][:, :],
                                             in1=stat["ar"][:, :]))
                        dve.inc(V.tensor_sub(out=stat["t1"][:, :], in0=stat["t1"][:, :],
                                             in1=stat["br"][:, :]))
                        dve.inc(V.tensor_add(out=x_sb[p][c][:, :], in0=x_sb[p][c][:, :],
                                             in1=stat["t1"][:, :]))
                    ln_stats([x_sb[p][c][:, :] for c in range(8)], ps_s[:, :], ps_q[:, :])
                    for c in range(8):
                        dve.inc(V.tensor_mul(out=stat["t1"][:, :], in0=x_sb[p][c][:, :],
                                             in1=stat["ar"][:, :]))
                        dve.inc(V.tensor_sub(out=xln_bf[c][:, :], in0=stat["t1"][:, :],
                                             in1=stat["br"][:, :]))
                    for do in range(8):
                        pp = do % 2
                        act.wait(PE, evac_done[pp])
                        dve.wait(PE)
                        for c in range(8):
                            pe.inc(PE.matmul(out=ps_a[pp][:, :],
                                             lhsT=tern[c][:, do * 128:(do + 1) * 128],
                                             rhs=xln_bf[c][:, :],
                                             start=(c == 0), stop=(c == 7)))
                        pe.wait(A_)
                        act.inc(A_.activation(out=fo_sb[do][:, :], in_=ps_a[pp][:, :],
                                              func=AF.Identity, bias=fb_sb[:, do:do + 1],
                                              scale=al_sb[:, :]))
                        evac_done[pp] = act.n
                    act.wait(V)
                    ln_stats([fo_sb[c][:, :] for c in range(8)], ps_s[:, :], ps_q[:, :])
                    st.wait(V)
                    for c in range(8):
                        dve.inc(V.tensor_mul(out=stat["t1"][:, :], in0=fo_sb[c][:, :],
                                             in1=stat["ar"][:, :]))
                        dve.inc(V.tensor_sub(out=stat["t1"][:, :], in0=stat["t1"][:, :],
                                             in1=stat["br"][:, :]))
                        dve.inc(V.tensor_add(out=x_sb[p][c][:, :], in0=x_sb[p][c][:, :],
                                             in1=stat["t1"][:, :]))
                        dve.wait(SY)
                        st.inc(SY.dma_start(out=x23[c][:, sl], in_=x_sb[p][c][:, :]), 16)
                    x2store_done[p] = st.n
                st.wait(SY)
                nc.all_engine_barrier()

            with contextlib.ExitStack() as e3b:
                wq_bf = [sb(f"p3wqb{c}", [128, DM], BF16, e3b) for c in range(8)]
                wk_bf = [sb(f"p3wkb{c}", [128, DM], BF16, e3b) for c in range(8)]
                wv_bf = [sb(f"p3wvb{c}", [128, DM], BF16, e3b) for c in range(8)]
                xln_bf = [sb(f"p3xl{c}", [128, TT], BF16, e3b) for c in range(8)]
                qkv_scr = [sb(f"p3qkvs{p}", [128, TT], BF16, e3b) for p in range(2)]
                v_scr = [sb(f"p3vs{p}", [128, DM], BF16, e3b) for p in range(2)]
                ps_b = [psum(f"p3ps_b{p}", [128, TT], st_=e3b) for p in range(2)]

                for c in range(8):
                    gld.inc(G.dma_start(out=wq_bf[c][:, :], in_=wq3[c]), 16)
                    gld.inc(G.dma_start(out=wk_bf[c][:, :], in_=wk3[c]), 16)
                    gld.inc(G.dma_start(out=wv_bf[c][:, :], in_=wv3[c]), 16)
                gld.wait(PE)
                evac_done = [0, 0]
                store_done = {}
                for t in range(NTT):
                    sl = slice(t * TT, (t + 1) * TT)
                    p = t % 2
                    dve.wait(SY)
                    for c in range(8):
                        ld.inc(SY.dma_start(out=x_sb[p][c][:, :], in_=x23[c][:, sl]), 16)
                    ld.wait(V); ld.wait(A_); ld.wait(PE)
                    ln_stats([x_sb[p][c][:, :] for c in range(8)], ps_s[:, :], ps_q[:, :])
                    for c in range(8):
                        dve.inc(V.tensor_mul(out=stat["t1"][:, :], in0=x_sb[p][c][:, :],
                                             in1=stat["ar"][:, :]))
                        dve.inc(V.tensor_sub(out=xln_bf[c][:, :], in0=stat["t1"][:, :],
                                             in1=stat["br"][:, :]))
                    dve.wait(PE)
                    for wname, wtiles, out3w in (("q", wq_bf, q3), ("k", wk_bf, k3w)):
                        for do in range(8):
                            pp = do % 2
                            act.wait(PE, evac_done[pp])
                            for c in range(8):
                                pe.inc(PE.matmul(out=ps_b[pp][:, :],
                                                 lhsT=wtiles[c][:, do * 128:(do + 1) * 128],
                                                 rhs=xln_bf[c][:, :],
                                                 start=(c == 0), stop=(c == 7)))
                            pe.wait(A_)
                            key = (wname, pp)
                            st.wait(A_, store_done.get(key, 0))
                            act.inc(A_.activation(out=qkv_scr[pp][:, :], in_=ps_b[pp][:, :],
                                                  func=AF.Identity))
                            evac_done[pp] = act.n
                            act.wait(SY)
                            st.inc(SY.dma_start(out=out3w[do][:, sl], in_=qkv_scr[pp][:, :]), 16)
                            store_done[key] = st.n
                    for tq in range(TT // 128):
                        pp = tq % 2
                        tc_global = (t * TT + tq * 128) // 128
                        for dv in range(2):
                            act.wait(PE, evac_done[pp])
                            for c in range(8):
                                pe.inc(PE.matmul(
                                    out=ps_b[pp][:, :],
                                    lhsT=xln_bf[c][:, tq * 128:(tq + 1) * 128],
                                    rhs=wv_bf[c][:, dv * 512:(dv + 1) * 512],
                                    start=(c == 0), stop=(c == 7)))
                            pe.wait(A_)
                            key = ("v", pp, dv)
                            st.wait(A_, store_done.get(key, 0))
                            act.inc(A_.activation(out=v_scr[pp][:, dv * 512:(dv + 1) * 512],
                                                  in_=ps_b[pp][:, :], func=AF.Identity))
                            evac_done[pp] = act.n
                            act.wait(SY)
                            st.inc(SY.dma_start(out=v3w[tc_global][:, dv * 512:(dv + 1) * 512],
                                                in_=v_scr[pp][:, dv * 512:(dv + 1) * 512]), 16)
                            store_done[key] = st.n
                st.wait(SY)
                st.wait(G)
                nc.all_engine_barrier()

        G.collective_compute("AllGather", ALU.bypass, replica_groups=[list(range(NCORE))],
                             ins=[ag4_in.opt()], outs=[ag4_out.opt()]).then_inc(cc[3])
        G.collective_compute("AllGather", ALU.bypass, replica_groups=[list(range(NCORE))],
                             ins=[ag5_in.opt()], outs=[ag5_out.opt()]).then_inc(cc[4])
        G.wait_ge(cc[3], 1)
        nc.all_engine_barrier()

        with contextlib.ExitStack() as e4:
            mask_sb = sb("p4mask", [128, 1920], st_=e4)
            sel_sb = sb("p4sel", [128, 32], st_=e4)
            binv_sb = sb("p4binv", [128, 32], st_=e4)
            q_sb = [sb(f"p4q{c}", [128, CH], BF16, e4) for c in range(8)]
            kt_sb = [[sb(f"p4k{p}_{c}", [128, KT], BF16, e4) for c in range(8)]
                     for p in range(2)]
            vt_sb = [[sb(f"p4v{p}_{c}", [128, DM], BF16, e4) for c in range(8)]
                     for p in range(2)]
            O_sb = [sb(f"p4O{j}", [128, DM], st_=e4) for j in range(8)]
            mrow = [sb(f"p4m{j}", [128, 1], st_=e4) for j in range(8)]
            lrow = [sb(f"p4l{j}", [128, 1], st_=e4) for j in range(8)]
            mnew = sb("p4mnew", [128, 1], st_=e4)
            mloc = sb("p4mloc", [128, 1], st_=e4)
            frow = [sb(f"p4frow{p}", [128, 1], st_=e4) for p in range(2)]
            bneg = sb("p4bneg", [128, 1], st_=e4)
            rs0 = [sb(f"p4rs0{p}", [128, 1], st_=e4) for p in range(2)]
            rl = sb("p4rl", [128, 1], st_=e4)
            p_bf = [sb(f"p4pbf{p}", [128, KT], BF16, e4) for p in range(2)]
            pt_sb = [sb(f"p4pt{p}", [128, KT], BF16, e4) for p in range(2)]
            of_bf = sb("p4of", [128, DM], BF16, e4)
            attn_bf = [sb(f"p4at{c}", [128, CH], BF16, e4) for c in range(8)]
            x2_sb = [sb(f"p4x2_{c}", [128, TT], st_=e4) for c in range(8)]
            ps_S = [psum(f"p4ps_S{p}", [128, KT], st_=e4) for p in range(2)]
            ps_pt = psum("p4ps_pt", [128, KT], BF16, e4)
            ps_av = psum("p4ps_av", [128, DM], st_=e4)

            ld.inc(SY.dma_start(out=mask_sb[:, :], in_=mb_in[:, :]), 16)
            ld.inc(SY.dma_start(out=sel_sb[:, :], in_=sel_in[:, :]), 16)
            ld.inc(SY.dma_start(out=binv_sb[:, :], in_=binv_in[:, :]), 16)
            SY.wait_ge(cc[4], 1)
            ld.wait(V); ld.wait(A_); ld.wait(PE)

            kvread_done = [0, 0]
            sread_done = [0, 0]
            sread_dve = [0, 0]
            pexp_done = [0, 0]
            ptevac_done = [0, 0]
            ostt_done = 0
            lstt_done = 0
            for qc in range(2):
                dve.wait(SY); act.wait(SY); pe.wait(SY)
                for c in range(8):
                    sq4.inc(SY.dma_start(out=q_sb[c][:, :],
                                        in_=q3[c][:, qc * CH:(qc + 1) * CH]), 16)
                for j in range(8):
                    dve.inc(V.memset(O_sb[j][:, :], 0.0))
                    dve.inc(V.memset(mrow[j][:, :], -1e30))
                    dve.inc(V.memset(lrow[j][:, :], 0.0))
                dve.wait(V); dve.wait(A_); dve.wait(PE)
                nkt = 8 if qc == 0 else 16
                kld_par = [0, 0]

                def load_kv(kt2):
                    p2 = kt2 % 2
                    rk2, coff2 = _chunk_loc(kt2)
                    pe.wait(SY, kvread_done[p2])
                    for c in range(8):
                        skv[p2].inc(SY.dma_start(out=kt_sb[p2][c][:, :],
                                            in_=k3r[rk2, c][:, coff2:coff2 + KT]), 16)
                    for c in range(8):
                        skv[p2].inc(SY.dma_start(out=vt_sb[p2][c][:, :],
                                            in_=v3r[rk2, (coff2 // 128) + c]), 16)

                load_kv(0)
                for kt in range(nkt):
                    idx = qc * 16 + kt
                    p = kt % 2
                    if kt + 1 < nkt:
                        load_kv(kt + 1)
                    skv[p].wait(PE)
                    sq4.wait(PE)

                    def s_stage(j):
                        sp = j % 2
                        act.wait(PE, sread_done[sp])
                        dve.wait(PE, sread_dve[sp])
                        for half in range(2):
                            for c in range(8):
                                pe.inc(PE.matmul(
                                    out=ps_S[sp][:, half * 512:(half + 1) * 512],
                                    lhsT=q_sb[c][:, j * 128:(j + 1) * 128],
                                    rhs=kt_sb[p][c][:, half * 512:(half + 1) * 512],
                                    start=(c == 0), stop=(c == 7)))
                        return pe.n

                    smm_done = {}
                    smm_done[0] = s_stage(0)
                    for j in range(8):
                        sp = j % 2
                        pe.wait(V, smm_done[j])
                        dve.inc(V.scalar_tensor_tensor(
                            out=ps_S[sp][:, :],
                            in0=mask_sb[:, 896 - 128 * j:1920 - 128 * j],
                            scalar=sel_sb[:, idx:idx + 1], in1=ps_S[sp][:, :],
                            op0=ALU.mult, op1=ALU.add))
                        dve.inc(V.tensor_reduce(out=mloc[:, :], in_=ps_S[sp][:, :],
                                                axis=mybir.AxisListType.X, op=ALU.max))
                        dve.wait(V)
                        dve.inc(V.tensor_max(out=mnew[:, :], in0=mrow[j][:, :],
                                             in1=mloc[:, :]))
                        dve.wait(V)
                        act.wait(V, pexp_done[sp])
                        dve.inc(V.tensor_sub(out=frow[sp][:, :], in0=mrow[j][:, :],
                                             in1=mnew[:, :]))
                        dve.wait(V)
                        dve.inc(V.scalar_tensor_tensor(
                            out=bneg[:, :], in0=mnew[:, :], scalar=cn32[:, :],
                            in1=binv_sb[:, idx:idx + 1], op0=ALU.mult, op1=ALU.add))
                        dve.wait(V)
                        dve.inc(V.tensor_copy(out=mrow[j][:, :], in_=mnew[:, :]))
                        dve.wait(V); dve.wait(A_)
                        act.inc(A_.activation(out=frow[sp][:, :], in_=frow[sp][:, :],
                                              func=AF.Exp, scale=1.0 / 32.0))
                        act.inc(A_.activation(out=p_bf[sp][:, :], in_=ps_S[sp][:, :],
                                              func=AF.Exp, scale=1.0 / 32.0,
                                              bias=bneg[:, :], accum_out=rs0[sp][:, :]))
                        pexp_done[sp] = act.n
                        sread_done[sp] = act.n
                        act.wait(V)
                        dve.inc(V.scalar_tensor_tensor(out=lrow[j][:, :],
                                                       in0=lrow[j][:, :],
                                                       scalar=frow[sp][:, :],
                                                       in1=rs0[sp][:, :],
                                                       op0=ALU.mult, op1=ALU.add))
                        lstt_done = dve.n
                        dve.wait(V)
                        if j + 1 < 8:
                            smm_done[j + 1] = s_stage(j + 1)
                        act.wait(PE, ptevac_done[sp])
                        act.wait(PE, pexp_done[sp])
                        for kc in range(8):
                            pe.inc(PE.transpose(out=ps_pt[:, kc * 128:(kc + 1) * 128],
                                                in_=p_bf[sp][:, kc * 128:(kc + 1) * 128],
                                                identity=idb_sb[:, :]))
                        pe.wait(A_)
                        act.inc(A_.activation(out=pt_sb[sp][:, :], in_=ps_pt[:, :],
                                              func=AF.Identity))
                        ptevac_done[sp] = act.n
                        act.wait(PE, ptevac_done[sp])
                        dve.wait(PE, ostt_done)
                        for kc in range(8):
                            pe.inc(PE.matmul(out=ps_av[:, 0:512],
                                             lhsT=pt_sb[sp][:, kc * 128:(kc + 1) * 128],
                                             rhs=vt_sb[p][kc][:, 0:512],
                                             start=(kc == 0), stop=(kc == 7)))
                            pe.inc(PE.matmul(out=ps_av[:, 512:1024],
                                             lhsT=pt_sb[sp][:, kc * 128:(kc + 1) * 128],
                                             rhs=vt_sb[p][kc][:, 512:1024],
                                             start=(kc == 0), stop=(kc == 7)))
                        pe.wait(V)
                        dve.inc(V.scalar_tensor_tensor(out=O_sb[j][:, :],
                                                       in0=O_sb[j][:, :],
                                                       scalar=frow[sp][:, :],
                                                       in1=ps_av[:, :],
                                                       op0=ALU.mult, op1=ALU.add))
                        ostt_done = dve.n
                        sread_dve[sp] = dve.n
                    kvread_done[p] = pe.n
                for j in range(8):
                    dve.wait(A_)
                    act.inc(A_.activation(out=rl[:, :], in_=lrow[j][:, :], func=AF.Ln))
                    act.inc(A_.activation(out=rl[:, :], in_=rl[:, :], func=AF.Exp,
                                          scale=-1.0))
                    act.wait(V)
                    dve.inc(V.tensor_scalar_mul(out=of_bf[:, :], in0=O_sb[j][:, :],
                                                scalar1=rl[:, :]))
                    dve.wait(V); dve.wait(PE)
                    act.wait(PE)
                    for dc in range(8):
                        pe.inc(PE.transpose(out=ps_pt[:, dc * 128:(dc + 1) * 128],
                                            in_=of_bf[:, dc * 128:(dc + 1) * 128],
                                            identity=idb_sb[:, :]))
                    pe.wait(A_)
                    for dc in range(8):
                        act.inc(A_.activation(out=attn_bf[dc][:, j * 128:(j + 1) * 128],
                                              in_=ps_pt[:, dc * 128:(dc + 1) * 128],
                                              func=AF.Identity))
                    act.wait(PE); act.wait(V)
                for tloc in range(CH // TT):
                    sl_loc = slice(tloc * TT, (tloc + 1) * TT)
                    sl_out = slice(qc * CH + tloc * TT, qc * CH + (tloc + 1) * TT)
                    st.wait(SY)
                    dve.wait(SY)
                    for c in range(8):
                        ld.inc(SY.dma_start(out=x2_sb[c][:, :], in_=x23[c][:, sl_out]), 16)
                    ld.wait(V); ld.wait(A_); ld.wait(PE)
                    ln_stats([attn_bf[c][:, sl_loc] for c in range(8)],
                             ps_S[0][:, 0:512], ps_S[0][:, 512:1024])
                    for c in range(8):
                        dve.inc(V.tensor_mul(out=stat["t1"][:, :], in0=attn_bf[c][:, sl_loc],
                                             in1=stat["ar"][:, :]))
                        dve.inc(V.tensor_sub(out=stat["t1"][:, :], in0=stat["t1"][:, :],
                                             in1=stat["br"][:, :]))
                        dve.inc(V.tensor_add(out=x2_sb[c][:, :], in0=x2_sb[c][:, :],
                                             in1=stat["t1"][:, :]))
                        dve.wait(V); dve.wait(SY)
                        st.inc(SY.dma_start(out=x33[c][:, sl_out], in_=x2_sb[c][:, :]), 16)
                pe.wait(A_); pe.wait(V)
                sread_done[0] = act.n
                sread_dve[0] = dve.n
            st.wait(SY)
            nc.all_engine_barrier()

        with contextlib.ExitStack() as e5:
            x_sb = [sb(f"p5x{c}", [128, TT], st_=e5) for c in range(8)]
            xln_bf = [sb(f"p5xln{c}", [128, TPC], BF16, e5) for c in range(8)]
            rw_sb = [sb(f"p5rw{c}", [128, 8], BF16, e5) for c in range(8)]
            rb_sb = sb("p5rb", [128, 8], st_=e5)
            lg = sb("p5lg", [128, 8], st_=e5)
            ge1 = sb("p5ge1", [128, 8], st_=e5)
            msk = sb("p5msk", [128, 8], st_=e5)
            m1c = sb("p5m1c", [128, 1], st_=e5)
            m2c = sb("p5m2c", [128, 1], st_=e5)
            nm1 = sb("p5nm1", [128, 1], st_=e5)
            d21 = sb("p5d21", [128, 1], st_=e5)
            rinv = sb("p5rinv", [128, 1], st_=e5)
            ee = sb("p5ee", [128, 8], st_=e5)
            cc_sb = sb("p5cc", [128, 8], st_=e5)
            cfm = sb("p5cfm", [8, TPC], st_=e5)
            crep_all = sb("p5crep", [128, TPC], BF16, e5)
            hs_bf = [sb(f"p5hs{f}", [128, TT], BF16, e5) for f in range(4)]
            b1_sb = sb("p5b1", [128, 32], st_=e5)
            b2_sb = sb("p5b2s", [8, DM], st_=e5)
            w1_bf = [[sb(f"p5w1_{pw}_{c}", [128, FSL], BF16, e5) for c in range(8)]
                     for pw in range(2)]
            w2_bf = [[sb(f"p5w2_{pw}_{c}", [128, DM], BF16, e5) for c in range(4)]
                     for pw in range(2)]
            h_bf = [sb(f"p5h{f}", [128, TT], BF16, e5) for f in range(4)]
            yev = sb("p5yev", [128, TT], st_=e5)
            y_acc = [sb(f"p5ya{c}", [128, TPC], st_=e5) for c in range(8)]
            ps_r = psum("p5ps_r", [128, 8], st_=e5)
            ps_ct = psum("p5ps_ct", [8, 128], st_=e5)
            ps_h = [psum(f"p5ps_h{p}", [128, TT], st_=e5) for p in range(2)]
            ps_y = [psum(f"p5ps_y{p}", [128, TT], st_=e5) for p in range(2)]
            ps_s = psum("p5ps_s", [128, TT], st_=e5)
            ps_q = psum("p5ps_q", [128, TT], st_=e5)

            ld.inc(SY.dma_start(out=rb_sb[:, :], in_=rb_in[:, :]), 16)
            ld.inc(SY.dma_start(out=b2_sb[:, :], in_=b2_in[:, :]), 16)
            for c in range(8):
                gld.inc(G.dma_start(out=rw_sb[c][:, :], in_=rw3[c]), 16)
            ld.wait(V)
            gld.wait(PE)

            for t in range(NTT):
                sl = slice(t * TT, (t + 1) * TT)
                dve.wait(SY)
                for c in range(8):
                    ld.inc(SY.dma_start(out=x_sb[c][:, :], in_=x33[c][:, sl]), 16)
                ld.wait(V); ld.wait(A_); ld.wait(PE)
                ln_stats([x_sb[c][:, :] for c in range(8)], ps_s[:, :], ps_q[:, :])
                for c in range(8):
                    dve.inc(V.tensor_mul(out=stat["t1"][:, :], in0=x_sb[c][:, :],
                                         in1=stat["ar"][:, :]))
                    dve.inc(V.tensor_sub(out=xln_bf[c][:, sl], in0=stat["t1"][:, :],
                                         in1=stat["br"][:, :]))

            for tc in range(TPC // 128):
                tsl = slice(tc * 128, (tc + 1) * 128)
                act.wait(PE); dve.wait(PE)
                for c in range(8):
                    pe.inc(PE.matmul(out=ps_r[:, :], lhsT=xln_bf[c][:, tsl],
                                     rhs=rw_sb[c][:, :], start=(c == 0), stop=(c == 7)))
                pe.wait(A_); pe.wait(V)
                act.inc(A_.activation(out=lg[:, :], in_=ps_r[:, :], func=AF.Identity))
                act.wait(V)
                dve.inc(V.tensor_add(out=lg[:, :], in0=lg[:, :], in1=rb_sb[:, :]))
                dve.wait(V)
                dve.inc(V.tensor_reduce(out=m1c[:, :], in_=lg[:, :],
                                        axis=mybir.AxisListType.X, op=ALU.max))
                dve.wait(V)
                dve.inc(V.tensor_scalar(out=ge1[:, :], in0=lg[:, :], scalar1=m1c[:, 0:1],
                                        scalar2=None, op0=ALU.is_ge))
                dve.wait(V)
                dve.inc(V.scalar_tensor_tensor(out=msk[:, :], in0=ge1[:, :],
                                               scalar=cneg[:, :], in1=lg[:, :],
                                               op0=ALU.mult, op1=ALU.add))
                dve.wait(V)
                dve.inc(V.tensor_reduce(out=m2c[:, :], in_=msk[:, :],
                                        axis=mybir.AxisListType.X, op=ALU.max))
                dve.wait(V)
                dve.inc(V.tensor_scalar(out=ge1[:, :], in0=lg[:, :], scalar1=m2c[:, 0:1],
                                        scalar2=None, op0=ALU.is_ge))
                dve.wait(V)
                dve.inc(V.tensor_scalar_mul(out=nm1[:, :], in0=m1c[:, :],
                                            scalar1=cm1[:, :]))
                dve.wait(V)
                dve.inc(V.tensor_sub(out=d21[:, :], in0=m2c[:, :], in1=m1c[:, :]))
                dve.wait(V); dve.wait(A_)
                act.inc(A_.activation(out=ee[:, :], in_=lg[:, :], func=AF.Exp,
                                      bias=nm1[:, :]))
                act.inc(A_.activation(out=d21[:, :], in_=d21[:, :], func=AF.Exp))
                act.wait(V)
                dve.inc(V.tensor_add(out=d21[:, :], in0=d21[:, :], in1=cone[:, :]))
                dve.wait(V); dve.wait(A_)
                act.inc(A_.activation(out=d21[:, :], in_=d21[:, :], func=AF.Ln))
                act.inc(A_.activation(out=rinv[:, :], in_=d21[:, :], func=AF.Exp,
                                      scale=-1.0))
                act.wait(V)
                dve.inc(V.tensor_mul(out=cc_sb[:, :], in0=ee[:, :], in1=ge1[:, :]))
                dve.wait(V)
                dve.inc(V.tensor_scalar_mul(out=cc_sb[:, :], in0=cc_sb[:, :],
                                            scalar1=rinv[:, 0:1]))
                dve.wait(V); dve.wait(PE)
                pe.inc(PE.transpose(out=ps_ct[:, :], in_=cc_sb[:, :],
                                    identity=idf_sb[:, :]))
                pe.wait(A_)
                act.inc(A_.activation(out=cfm[:, tsl], in_=ps_ct[:, :], func=AF.Identity))
                act.wait(PE); act.wait(V)

            act.wait(SY)
            st.inc(SY.dma_start(out=cfm_dram[:, :], in_=cfm[:, :]), 16)

            for t in range(NTT):
                sl = slice(t * TT, (t + 1) * TT)
                for do in range(8):
                    pp = do % 2
                    act.wait(PE)
                    pe.inc(PE.matmul(out=ps_y[pp][:, :], lhsT=b2_sb[:, do * 128:(do + 1) * 128],
                                     rhs=cfm[:, sl], start=True, stop=True))
                    pe.wait(A_)
                    act.inc(A_.activation(out=y_acc[do][:, sl], in_=ps_y[pp][:, :],
                                          func=AF.Identity))
            act.wait(V); act.wait(PE)

            st.wait(G)
            hevac_done = [0, 0]
            yread_done = [0, 0]
            wread_done = [0, 0]
            wld_done = [0, 0]

            def load_wslice(e2, fs2, pw2):
                pe.wait(G, wread_done[pw2])
                for c in range(8):
                    sw5[pw2].inc(G.dma_start(out=w1_bf[pw2][c][:, :],
                                        in_=w1_3[e2, c][:, fs2 * FSL:(fs2 + 1) * FSL]), 16)
                for fc in range(4):
                    sw5[pw2].inc(G.dma_start(out=w2_bf[pw2][fc][:, :],
                                        in_=w2_3[e2, fs2 * 4 + fc]), 16)

            load_wslice(0, 0, 0)
            for e in range(8):
                dve.wait(G)
                scr5.inc(G.dma_start(out=crep_all[:, :],
                                    in_=cfm_dram[e:e + 1, :].broadcast_to([128, TPC])), 16)
                scr5.wait(V)
                act.wait(SY)
                ld.inc(SY.dma_start(out=b1_sb[:, :], in_=b1_in[e]), 16)
                for fs in range(NFS):
                    pw = (e * NFS + fs) % 2
                    nxt = e * NFS + fs + 1
                    if nxt < 8 * NFS:
                        load_wslice(nxt // NFS, nxt % NFS, 1 - pw)
                    sw5[pw].wait(PE)
                    ld.wait(A_)
                    pre_n = None
                    fold0_done = 0
                    for t in range(NTT):
                        sl = slice(t * TT, (t + 1) * TT)
                        for fo in range(FSL // 128):
                            ph = fo % 2
                            if fo == 0 and pre_n is not None:
                                dve.wait(A_, fold0_done)
                                pe.wait(A_, pre_n)
                            else:
                                act.wait(PE, hevac_done[ph])
                                for c in range(8):
                                    pe.inc(PE.matmul(out=ps_h[ph][:, :],
                                                     lhsT=w1_bf[pw][c][:, fo * 128:(fo + 1) * 128],
                                                     rhs=xln_bf[c][:, sl],
                                                     start=(c == 0), stop=(c == 7)))
                                pe.wait(A_)
                            act.inc(A_.activation(out=h_bf[fo][:, :], in_=ps_h[ph][:, :],
                                                  func=AF.Relu,
                                                  bias=b1_sb[:, fs * 4 + fo:fs * 4 + fo + 1]))
                            hevac_done[ph] = act.n
                            act.wait(V)
                            dve.inc(V.tensor_mul(out=hs_bf[fo][:, :], in0=h_bf[fo][:, :],
                                                 in1=crep_all[:, sl]))
                            if fo == 0:
                                fold0_done = dve.n
                        if t + 1 < NTT:
                            sl_nx = slice((t + 1) * TT, (t + 2) * TT)
                            act.wait(PE, hevac_done[0])
                            for c in range(8):
                                pe.inc(PE.matmul(out=ps_h[0][:, :],
                                                 lhsT=w1_bf[pw][c][:, 0:128],
                                                 rhs=xln_bf[c][:, sl_nx],
                                                 start=(c == 0), stop=(c == 7)))
                            pre_n = pe.n
                        else:
                            pre_n = None
                        dve.wait(PE)
                        for do in range(8):
                            pp = do % 2
                            dve.wait(PE, yread_done[pp])
                            for fc in range(4):
                                pe.inc(PE.matmul(out=ps_y[pp][:, :],
                                                 lhsT=w2_bf[pw][fc][:, do * 128:(do + 1) * 128],
                                                 rhs=hs_bf[fc][:, :],
                                                 start=(fc == 0), stop=(fc == 3)))
                            pe.wait(V)
                            dve.inc(V.tensor_add(out=y_acc[do][:, sl], in0=y_acc[do][:, sl],
                                                 in1=ps_y[pp][:, :]))
                            yread_done[pp] = dve.n
                    wread_done[pw] = pe.n

            for t in range(NTT):
                sl = slice(t * TT, (t + 1) * TT)
                dve.wait(SY); st.wait(SY)
                for c in range(8):
                    ld.inc(SY.dma_start(out=x_sb[c][:, :], in_=x33[c][:, sl]), 16)
                ld.wait(V); ld.wait(A_); ld.wait(PE)
                ln_stats([y_acc[c][:, sl] for c in range(8)], ps_s[:, :], ps_q[:, :])
                for c in range(8):
                    dve.inc(V.tensor_mul(out=stat["t1"][:, :], in0=y_acc[c][:, sl],
                                         in1=stat["ar"][:, :]))
                    dve.inc(V.tensor_sub(out=stat["t1"][:, :], in0=stat["t1"][:, :],
                                         in1=stat["br"][:, :]))
                    dve.inc(V.tensor_add(out=x_sb[c][:, :], in0=x_sb[c][:, :],
                                         in1=stat["t1"][:, :]))
                    dve.wait(V); dve.wait(SY)
                    st.inc(SY.dma_start(out=o3[c][:, sl], in_=x_sb[c][:, :]), 16)
            st.wait(SY)
    return nc


def kernel_fused(inp):
    if "fused" not in _CACHE:
        _CACHE["fused"] = build_fused()
    nc = _CACHE["fused"]
    x = inp["L0_attn_x"][0]
    mask_big = np.zeros((128, 1920), np.float32)
    for rr in range(128):
        mask_big[rr, 897 + rr:] = -1e9
    shared = dict(_host_l1_inputs(inp))
    shared.update({
        "outw_T": np.ascontiguousarray(inp["L0_attn_out_proj_w"].T),
        "w0_T": np.ascontiguousarray(inp["L0_ffn_weight"].T),
        "alpha_rep": np.ascontiguousarray(np.broadcast_to(
            inp["L0_ffn_alpha"].reshape(1, 1), (128, 1))).astype(np.float32),
        "fb_t": _cvec(inp["L0_ffn_bias"]),
        "wq_T": np.ascontiguousarray(inp["L1_attn_Wq"].T),
        "wk_T": np.ascontiguousarray(inp["L1_attn_Wk"].T),
        "wv_T": np.ascontiguousarray(inp["L1_attn_Wv"].T),
        "mask_big": mask_big,
        "ident_bf": np.eye(128, dtype=BF),
        "ident_f": np.eye(128, dtype=np.float32),
        "ones_f": np.ones((128, 128), dtype=np.float32),
        "router_w": np.ascontiguousarray(inp["L1_ffn_router_w"]),
        "rb_rep": np.ascontiguousarray(np.broadcast_to(
            inp["L1_ffn_router_b"].reshape(1, 8), (128, 8))).astype(np.float32),
        "W1": np.ascontiguousarray(inp["L1_ffn_W1"]),
        "b1_t": np.ascontiguousarray(
            inp["L1_ffn_b1"].reshape(8, 32, 128).transpose(0, 2, 1)),
        "W2": np.ascontiguousarray(inp["L1_ffn_W2"]),
        "b2": np.ascontiguousarray(inp["L1_ffn_b2"]),
    })
    in_maps = []
    for core in range(NCORE):
        m = dict(shared)
        rows = _tok_rows(core)
        m["x_fm"] = np.ascontiguousarray(x[rows].T)
        xh = np.zeros((8, DM), np.float32)
        c0, c1 = CORE_CHUNKS[core]
        if c0 > 0:
            xh[0:4] = x[c0 * CH - 4:c0 * CH]
        xh[4:8] = x[c1 * CH - 4:c1 * CH]
        m["x_halo"] = np.ascontiguousarray(xh.T)
        s0 = EPC * core
        m["A_log_t"] = np.ascontiguousarray(
            inp["L0_attn_A_log"][s0:s0 + EPC].reshape(2, 128, NST))
        m["D_t"] = np.ascontiguousarray(
            inp["L0_attn_D"][s0:s0 + EPC].reshape(2, 128).T)
        sel = np.zeros((1, 32), np.float32)
        binv = np.zeros((1, 32), np.float32)
        for qc in range(2):
            gq = CORE_CHUNKS[core][qc]
            sel[0, qc * 16 + gq] = 1.0
            for kt in range(gq + 1, 16):
                binv[0, qc * 16 + kt] = -1e30
        m["sel_rep"] = np.ascontiguousarray(np.broadcast_to(sel, (128, 32)))
        m["binv_rep"] = np.ascontiguousarray(np.broadcast_to(binv, (128, 32)))
        m["idx0"] = (EPC * core + np.arange(128, dtype=np.int32)).reshape(128, 1)
        m["dtw_sl"] = np.ascontiguousarray(inp["L0_attn_dt_proj_w"].T[:, s0:s0 + EPC])
        m["dtb_sl"] = np.ascontiguousarray(
            inp["L0_attn_dt_proj_b"][s0:s0 + EPC].reshape(2, 128).T)
        idxq = np.zeros((128, 2), np.int32)
        for qc in range(2):
            idxq[:, qc] = CORE_CHUNKS[core][qc] * EPC + np.arange(128)
        m["idxq"] = np.ascontiguousarray(idxq)
        in_maps.append(m)
    res = run_bass_kernel_spmd(nc, in_maps, core_ids=list(range(NCORE)))
    out = np.empty((S, DM), dtype=np.float32)
    for core in range(NCORE):
        out[_tok_rows(core)] = res.results[core]["out_fm"].T
    return out.reshape(1, S, DM)


def kernel(**inp):
    return kernel_fused({k: np.asarray(v) for k, v in inp.items()})


# version 23 (final)
# speedup vs baseline: 1.0629x; 1.0087x over previous
import contextlib
import numpy as np
import ml_dtypes
import concourse.bass as bass
import concourse.mybir as mybir
from concourse.bass_utils import run_bass_kernel_spmd

F32 = mybir.dt.float32
BF16 = mybir.dt.bfloat16
AF = mybir.ActivationFunctionType
ALU = mybir.AluOpType
BF = ml_dtypes.bfloat16

S = 16384
DM = 1024
E = 2048
NST = 16
RLOW = 64
NCORE = 8
NCHUNK = 16
CH = S // NCHUNK
TPC = 2 * CH
EPC = E // NCORE
EPS = 1e-5

CORE_CHUNKS = [(i, NCHUNK - 1 - i) for i in range(NCORE)]


def _tok_rows(core):
    c0, c1 = CORE_CHUNKS[core]
    return np.r_[c0 * CH:(c0 + 1) * CH, c1 * CH:(c1 + 1) * CH]


class Ctr:
    def __init__(self, es, nc, name):
        self.sem = es.enter_context(nc.semaphore(name))
        self.n = 0

    def inc(self, inst, k=1):
        inst.then_inc(self.sem, k)
        self.n += k

    def wait(self, eng, n=None):
        n = self.n if n is None else n
        if n > 0:
            eng.wait_ge(self.sem, n)


def _cvec(v, width=128):
    v = np.asarray(v)
    k = v.shape[0] // width
    return np.ascontiguousarray(v.reshape(k, width).T)


def _ln_stats_math(nc, dve, act, stat, ps_s, ps_q, ceps, want_rms=False):
    V, A_ = nc.vector, nc.scalar
    m, q, v, r = stat["m"][:, :], stat["q"][:, :], stat["v"][:, :], stat["r"][:, :]
    ar, br, t1 = stat["ar"][:, :], stat["br"][:, :], stat["t1"][:, :]
    act.inc(A_.activation(out=m, in_=ps_s, func=AF.Identity, scale=1.0 / DM))
    act.inc(A_.activation(out=q, in_=ps_q, func=AF.Identity, scale=1.0 / DM))
    act.inc(A_.activation(out=t1, in_=ps_s, func=AF.Square, scale=1.0 / DM))
    act.wait(V)
    dve.inc(V.tensor_sub(out=v, in0=q, in1=t1))
    dve.wait(V)
    dve.wait(A_)
    act.inc(A_.activation(out=t1, in_=v, func=AF.Ln, bias=ceps))
    act.inc(A_.activation(out=r, in_=t1, func=AF.Exp, scale=-0.5))
    if want_rms:
        sr, t2 = stat["sr"][:, :], stat["t2"][:, :]
        act.inc(A_.activation(out=t2, in_=r, func=AF.Square))
        act.wait(V)
        dve.inc(V.tensor_mul(out=t2, in0=v, in1=t2))
        dve.wait(V)
        dve.wait(A_)
        act.inc(A_.activation(out=t2, in_=t2, func=AF.Ln, bias=ceps))
        act.inc(A_.activation(out=sr, in_=t2, func=AF.Exp, scale=-0.5))
        act.wait(V)
        dve.inc(V.tensor_mul(out=ar, in0=r, in1=sr))
        dve.wait(V)
    else:
        act.wait(V)
        dve.inc(V.tensor_copy(out=ar, in_=r))
        dve.wait(V)
    dve.inc(V.tensor_mul(out=br, in0=m, in1=ar))
    dve.wait(V)


_CACHE = {}


def build_l1():
    nc = bass.Bass("TRN2", target_bir_lowering=False)
    x_in = nc.dram_tensor("x_fm", [DM, TPC], F32, kind="ExternalInput").ap()
    wi_in = nc.dram_tensor("w_in_T", [DM, 2 * E], F32, kind="ExternalInput").ap()
    nw_in = nc.dram_tensor("nw_t", [128, 8], F32, kind="ExternalInput").ap()
    cw_in = nc.dram_tensor("conv_w_t", [128, 16, 4], F32, kind="ExternalInput").ap()
    cb_in = nc.dram_tensor("conv_b_t", [128, 16], F32, kind="ExternalInput").ap()
    xp_in = nc.dram_tensor("xproj_T", [E, 96], F32, kind="ExternalInput").ap()
    dtw_in = nc.dram_tensor("dtw_T", [RLOW, E], F32, kind="ExternalInput").ap()
    dtb_in = nc.dram_tensor("dtb_t", [128, 16], F32, kind="ExternalInput").ap()
    ones_in = nc.dram_tensor("ones_bf", [128, 128], BF16, kind="ExternalInput").ap()
    xh_in = nc.dram_tensor("x_halo", [DM, 8], F32, kind="ExternalInput").ap()

    u_out = nc.dram_tensor("u_bf", [E, TPC], BF16, kind="ExternalOutput").ap()
    zs_out = nc.dram_tensor("zs_bf", [E, TPC], BF16, kind="ExternalOutput").ap()
    dt_out = nc.dram_tensor("dt_bf", [E, TPC], BF16, kind="ExternalOutput").ap()
    xdbl_out = nc.dram_tensor("xdbl", [96, TPC], F32, kind="ExternalOutput").ap()
    r1m1_out = nc.dram_tensor("r1m1", [2, TPC], F32, kind="ExternalOutput").ap()

    x3 = x_in.rearrange("(c p) t -> c p t", p=128)
    xh3 = xh_in.rearrange("(c p) t -> c p t", p=128)
    u3 = u_out.rearrange("(c p) t -> c p t", p=128)
    zs3 = zs_out.rearrange("(c p) t -> c p t", p=128)
    dt3 = dt_out.rearrange("(c p) t -> c p t", p=128)
    wi3 = wi_in.rearrange("(c p) e -> c p e", p=128)
    xp3 = xp_in.rearrange("(c p) f -> c p f", p=128)

    TT = 512
    NTT = TPC // TT

    with contextlib.ExitStack() as es:
        def sb(name, shape, dt=F32):
            return es.enter_context(nc.sbuf_tensor(name, shape, dt))

        def psum(name, shape, dt=F32):
            return es.enter_context(nc.psum_tensor(name, shape, dt))

        x_sb = [[sb(f"x{p}_{c}", [128, TT]) for c in range(8)] for p in range(2)]
        w_bf = [sb(f"w{c}", [128, 2 * E], BF16) for c in range(8)]
        ones_sb = sb("ones", [128, 128], BF16)
        nw_sb = sb("nw", [128, 8])
        cw_sb = sb("cw", [128, 16, 4])
        cb_sb = sb("cb", [128, 16])
        xp_bf = [sb(f"xp{c}", [128, 96], BF16) for c in range(16)]
        dtw_bf = sb("dtw", [RLOW, E], BF16)
        dtb_sb = sb("dtb", [128, 16])

        xb_scr2 = [sb(f"xb{i}", [128, TT], BF16) for i in range(2)]
        sq_scr2 = [sb(f"sqs{i}", [128, TT], BF16) for i in range(2)]
        xb_scr = xb_scr2[0]
        sq_scr = sq_scr2[0]
        stat = {k: sb("st_" + k, [128, TT])
                for k in ["m", "q", "v", "r", "sr", "ar", "br", "t1", "t2"]}
        xin_bf = [sb(f"xin{c}", [128, TPC], BF16) for c in range(8)]

        ur = [sb(f"ur{p}", [128, TT + 3]) for p in range(2)]
        halo = sb("halo", [128, 16, 3])
        halo2 = sb("halo2", [128, 16, 3])
        xh_sb = [sb(f"xh{c}", [128, 8]) for c in range(8)]
        xinh_bf = [sb(f"xinh{c}", [128, 8], BF16) for c in range(8)]
        uh_scr = sb("uh", [128, 8])
        cacc = [sb(f"cacc{p}", [128, TT]) for p in range(2)]
        ubf_sb = [sb(f"ubf{p}", [128, TT], BF16) for p in range(2)]
        zbf_sb = [sb(f"zbf{p}", [128, TT], BF16) for p in range(2)]
        xdbl_sb = sb("xdbl_sb", [96, TT])
        dtlow_bf = sb("dtlow", [RLOW, TT], BF16)
        sp_s1 = sb("sp_s1", [128, TT])
        sp_ax = sb("sp_ax", [128, TT])
        sp_ln = sb("sp_ln", [128, TT])
        dtbf_sb = [sb(f"dtbf{p}", [128, TT], BF16) for p in range(2)]
        ceps = sb("ceps", [128, 1])
        cone = sb("cone", [128, 1])

        ps_u = [psum(f"ps_u{p}", [128, TT]) for p in range(2)]
        ps_x = psum("ps_x", [96, TT])
        ps_dt = [psum(f"ps_dt{p}", [128, TT]) for p in range(2)]
        ps_s = psum("ps_s", [128, TT])
        ps_q = psum("ps_q", [128, TT])

        ld = Ctr(es, nc, "ld")
        gld = Ctr(es, nc, "gld")
        pe = Ctr(es, nc, "pe")
        act = Ctr(es, nc, "act")
        dve = Ctr(es, nc, "dve")
        st = Ctr(es, nc, "st")

        SY, G, PE, A_, V = nc.sync, nc.gpsimd, nc.tensor, nc.scalar, nc.vector

        ld.inc(SY.dma_start(out=ones_sb[:, :], in_=ones_in[:, :]), 16)
        ld.inc(SY.dma_start(out=nw_sb[:, :], in_=nw_in[:, :]), 16)
        ld.inc(SY.dma_start(out=cw_sb[:, :, :], in_=cw_in[:, :, :]), 16)
        ld.inc(SY.dma_start(out=cb_sb[:, :], in_=cb_in[:, :]), 16)
        ld.inc(SY.dma_start(out=dtb_sb[:, :], in_=dtb_in[:, :]), 16)
        for c in range(8):
            ld.inc(SY.dma_start(out=xh_sb[c][:, :], in_=xh3[c]), 16)
        gld.inc(G.dma_start(out=dtw_bf[:, :], in_=dtw_in[:, :]), 16)
        for c in range(16):
            gld.inc(G.dma_start(out=xp_bf[c][:, :], in_=xp3[c]), 16)
        for c in range(8):
            gld.inc(G.dma_start(out=w_bf[c][:, :], in_=wi3[c]), 16)
        ld.wait(V)
        gld.wait(V)
        for c in range(8):
            dve.inc(V.tensor_scalar_mul(out=w_bf[c][:, :], in0=w_bf[c][:, :],
                                        scalar1=nw_sb[:, c:c + 1]))
        dve.inc(V.memset(halo[:, :, :], 0.0))
        dve.inc(V.memset(ceps[:, :], EPS))
        dve.inc(V.memset(cone[:, :], 1.0))

        sh = {k: stat[k][:, 0:8] for k in stat}
        ld.wait(V); ld.wait(A_); ld.wait(PE)
        for c in range(8):
            pe.wait(V); pe.wait(A_)
            dve.inc(V.tensor_copy(out=xb_scr[:, 0:8], in_=xh_sb[c][:, :]))
            act.inc(A_.activation(out=sq_scr[:, 0:8], in_=xh_sb[c][:, :], func=AF.Square))
            dve.wait(PE); act.wait(PE)
            pe.inc(PE.matmul(out=ps_s[:, 0:8], lhsT=ones_sb[:, :], rhs=xb_scr[:, 0:8],
                             start=(c == 0), stop=(c == 7)))
            pe.inc(PE.matmul(out=ps_q[:, 0:8], lhsT=ones_sb[:, :], rhs=sq_scr[:, 0:8],
                             start=(c == 0), stop=(c == 7)))
        pe.wait(A_); pe.wait(V)
        _ln_stats_math(nc, dve, act, sh, ps_s[:, 0:8], ps_q[:, 0:8], ceps[:, :], want_rms=True)
        for c in range(8):
            dve.inc(V.tensor_mul(out=sh["t1"], in0=xh_sb[c][:, :], in1=sh["ar"]))
            dve.wait(V)
            dve.inc(V.tensor_sub(out=xinh_bf[c][:, :], in0=sh["t1"], in1=sh["br"]))
            dve.wait(V)
        nc.all_engine_barrier()
        for eo in range(16):
            for c in range(8):
                pe.inc(PE.matmul(out=ps_u[0][:, 0:8],
                                 lhsT=w_bf[c][:, eo * 128:(eo + 1) * 128],
                                 rhs=xinh_bf[c][:, :], start=(c == 0), stop=(c == 7)))
            pe.wait(A_)
            act.inc(A_.activation(out=uh_scr[:, :], in_=ps_u[0][:, 0:8], func=AF.Identity))
            act.wait(V)
            dve.inc(V.tensor_copy(out=halo[:, eo, :], in_=uh_scr[:, 1:4]))
            dve.inc(V.tensor_copy(out=halo2[:, eo, :], in_=uh_scr[:, 5:8]))
            dve.wait(PE)
        nc.all_engine_barrier()

        xload_done = [0, 0]
        xuse_done = [0, 0]
        for t in range(NTT):
            sl = slice(t * TT, (t + 1) * TT)
            p = t % 2
            dve.wait(SY, xuse_done[p])
            for c in range(8):
                ld.inc(SY.dma_start(out=x_sb[p][c][:, :], in_=x3[c][:, sl]), 16)
            xload_done[p] = ld.n
            for c in range(8):
                pe.wait(V)
                pe.wait(A_)
                ld.wait(V, xload_done[p])
                ld.wait(A_, xload_done[p])
                dve.inc(V.tensor_copy(out=xb_scr[:, :], in_=x_sb[p][c][:, :]))
                act.inc(A_.activation(out=sq_scr[:, :], in_=x_sb[p][c][:, :], func=AF.Square))
                dve.wait(PE)
                act.wait(PE)
                pe.inc(PE.matmul(out=ps_s[:, :], lhsT=ones_sb[:, :], rhs=xb_scr[:, :],
                                 start=(c == 0), stop=(c == 7)))
                pe.inc(PE.matmul(out=ps_q[:, :], lhsT=ones_sb[:, :], rhs=sq_scr[:, :],
                                 start=(c == 0), stop=(c == 7)))
            pe.wait(A_)
            pe.wait(V)
            _ln_stats_math(nc, dve, act, stat, ps_s[:, :], ps_q[:, :], ceps[:, :], want_rms=True)
            dve.wait(SY)
            act.wait(SY)
            st.inc(SY.dma_start(out=r1m1_out[0:1, sl], in_=stat["r"][0:1, :]), 16)
            st.inc(SY.dma_start(out=r1m1_out[1:2, sl], in_=stat["m"][0:1, :]), 16)
            for c in range(8):
                dve.inc(V.tensor_mul(out=stat["t1"][:, :], in0=x_sb[p][c][:, :],
                                     in1=stat["ar"][:, :]))
                dve.inc(V.tensor_sub(out=xin_bf[c][:, sl], in0=stat["t1"][:, :],
                                     in1=stat["br"][:, :]))
            xuse_done[p] = dve.n
            st.wait(A_)
            st.wait(V)
        nc.all_engine_barrier()

        evac_done = [0, 0]
        store_done = {}
        for t in range(NTT):
            sl = slice(t * TT, (t + 1) * TT)
            for eo in range(32):
                p = eo % 2
                act.wait(PE, evac_done[p])
                for c in range(8):
                    pe.inc(PE.matmul(out=ps_u[p][:, :],
                                     lhsT=w_bf[c][:, eo * 128:(eo + 1) * 128],
                                     rhs=xin_bf[c][:, sl],
                                     start=(c == 0), stop=(c == 7)))
                if eo < 16:
                    pe.wait(A_)
                    dve.wait(A_)
                    act.inc(A_.activation(out=ur[p][:, 3:3 + TT], in_=ps_u[p][:, :],
                                          func=AF.Identity))
                    evac_done[p] = act.n
                    act.wait(V)
                    hsrc = halo2 if t == 2 else halo
                    dve.inc(V.tensor_copy(out=ur[p][:, 0:3], in_=hsrc[:, eo, :]))
                    dve.wait(V)
                    dve.inc(V.tensor_scalar_mul(out=cacc[p][:, :], in0=ur[p][:, 0:TT],
                                                scalar1=cw_sb[:, eo, 0:1]))
                    for j in range(1, 4):
                        dve.inc(V.scalar_tensor_tensor(
                            out=cacc[p][:, :], in0=ur[p][:, j:j + TT],
                            scalar=cw_sb[:, eo, j:j + 1], in1=cacc[p][:, :],
                            op0=ALU.mult, op1=ALU.add))
                    dve.inc(V.tensor_copy(out=halo[:, eo, :], in_=ur[p][:, TT:TT + 3]))
                    key = ("u", p)
                    st.wait(A_, store_done.get(key, 0))
                    dve.wait(A_)
                    act.inc(A_.activation(out=ubf_sb[p][:, :], in_=cacc[p][:, :],
                                          func=AF.Silu, bias=cb_sb[:, eo:eo + 1]))
                    act.wait(SY)
                    st.inc(SY.dma_start(out=u3[eo][:, sl], in_=ubf_sb[p][:, :]), 16)
                    store_done[key] = st.n
                    act.wait(PE)
                    pe.inc(PE.matmul(out=ps_x[:, :], lhsT=xp_bf[eo][:, :],
                                     rhs=ubf_sb[p][:, :],
                                     start=(eo == 0), stop=(eo == 15)))
                else:
                    key = ("z", p)
                    st.wait(A_, store_done.get(key, 0))
                    pe.wait(A_)
                    act.inc(A_.activation(out=zbf_sb[p][:, :], in_=ps_u[p][:, :],
                                          func=AF.Silu))
                    evac_done[p] = act.n
                    act.wait(SY)
                    st.inc(SY.dma_start(out=zs3[eo - 16][:, sl], in_=zbf_sb[p][:, :]), 16)
                    store_done[key] = st.n
            pe.wait(A_)
            act.inc(A_.activation(out=xdbl_sb[:, :], in_=ps_x[:, :], func=AF.Identity))
            act.wait(SY)
            st.inc(SY.dma_start(out=xdbl_out[:, sl], in_=xdbl_sb[:, :]), 16)
            act.wait(V)
            dve.inc(V.tensor_copy(out=dtlow_bf[:, :], in_=xdbl_sb[0:64, :]))
            for e2 in range(16):
                p = e2 % 2
                act.wait(PE, evac_done[p])
                dve.wait(PE)
                pe.inc(PE.matmul(out=ps_dt[p][:, :],
                                 lhsT=dtw_bf[:, e2 * 128:(e2 + 1) * 128],
                                 rhs=dtlow_bf[:, :], start=True, stop=True))
                pe.wait(A_)
                dve.wait(A_)
                act.inc(A_.activation(out=sp_s1[:, :], in_=ps_dt[p][:, :],
                                      func=AF.Identity, bias=dtb_sb[:, e2:e2 + 1]))
                evac_done[p] = act.n
                act.inc(A_.activation(out=sp_ax[:, :], in_=sp_s1[:, :], func=AF.Abs))
                act.inc(A_.activation(out=sp_ax[:, :], in_=sp_ax[:, :], func=AF.Exp,
                                      scale=-1.0))
                act.inc(A_.activation(out=sp_ln[:, :], in_=sp_ax[:, :], func=AF.Ln,
                                      bias=cone[:, :]))
                act.inc(A_.activation(out=sp_ax[:, :], in_=sp_s1[:, :], func=AF.Relu))
                key = ("dt", p)
                st.wait(V, store_done.get(key, 0))
                act.wait(V)
                dve.inc(V.tensor_add(out=dtbf_sb[p][:, :], in0=sp_ax[:, :],
                                     in1=sp_ln[:, :]))
                dve.wait(SY)
                st.inc(SY.dma_start(out=dt3[e2][:, sl], in_=dtbf_sb[p][:, :]), 16)
                store_done[key] = st.n
        st.wait(SY)
    return nc


def _host_l1_inputs(inp):
    d = {}
    d["w_in_T"] = np.ascontiguousarray(inp["L0_attn_in_proj_w"].T)
    d["nw_t"] = _cvec(inp["L0_attn_norm_w"])
    d["conv_w_t"] = np.ascontiguousarray(
        inp["L0_attn_conv_w"].reshape(16, 128, 4).transpose(1, 0, 2))
    d["conv_b_t"] = _cvec(inp["L0_attn_conv_b"])
    d["xproj_T"] = np.ascontiguousarray(inp["L0_attn_x_proj_w"].T)
    d["dtw_T"] = np.ascontiguousarray(inp["L0_attn_dt_proj_w"].T)
    d["dtb_t"] = _cvec(inp["L0_attn_dt_proj_b"])
    d["ones_bf"] = np.ones((128, 128), dtype=BF)
    return d


def run_l1(inp):
    if "l1" not in _CACHE:
        _CACHE["l1"] = build_l1()
    nc = _CACHE["l1"]
    shared = _host_l1_inputs(inp)
    x = np.asarray(inp["L0_attn_x"])[0]
    in_maps = []
    for core in range(NCORE):
        m = dict(shared)
        m["x_fm"] = np.ascontiguousarray(x[_tok_rows(core)].T)
        xh = np.zeros((8, DM), np.float32)
        c0, c1 = CORE_CHUNKS[core]
        if c0 > 0:
            xh[0:4] = x[c0 * CH - 4:c0 * CH]
        xh[4:8] = x[c1 * CH - 4:c1 * CH]
        m["x_halo"] = np.ascontiguousarray(xh.T)
        in_maps.append(m)
    res = run_bass_kernel_spmd(nc, in_maps, core_ids=list(range(NCORE)))
    return res.results


def build_l2():
    nc = bass.Bass("TRN2", target_bir_lowering=False)
    NT2 = S // 512
    TT = 512
    u_in = nc.dram_tensor("u_sl", [EPC, S], BF16, kind="ExternalInput").ap()
    dt_in = nc.dram_tensor("dt_sl", [EPC, S], BF16, kind="ExternalInput").ap()
    zs_in = nc.dram_tensor("zs_sl", [EPC, S], BF16, kind="ExternalInput").ap()
    B_in = nc.dram_tensor("B_t", [NT2, NST, TT], F32, kind="ExternalInput").ap()
    C_in = nc.dram_tensor("C_t", [NT2, NST, TT], F32, kind="ExternalInput").ap()
    Al_in = nc.dram_tensor("A_log_t", [2, 128, NST], F32, kind="ExternalInput").ap()
    D_in = nc.dram_tensor("D_t", [128, 2], F32, kind="ExternalInput").ap()

    yg_out = nc.dram_tensor("yg_sl", [EPC, S], BF16, kind="ExternalOutput").ap()

    u3 = u_in.rearrange("(c p) t -> c p t", p=128)
    dt3 = dt_in.rearrange("(c p) t -> c p t", p=128)
    zs3 = zs_in.rearrange("(c p) t -> c p t", p=128)
    yg3 = yg_out.rearrange("(c p) t -> c p t", p=128)

    with contextlib.ExitStack() as es:
        def sb(name, shape, dt=F32):
            return es.enter_context(nc.sbuf_tensor(name, shape, dt))

        A_t = [sb(f"A{e}", [128, NST]) for e in range(2)]
        A_scr = sb("Ascr", [128, NST])
        cm1 = sb("cm1", [128, 1])
        D_sb = sb("D", [128, 2])
        hc = [sb(f"hc{e}", [128, NST], BF16) for e in range(2)]
        dt_sb = [sb(f"dt{p}", [128, TT], BF16) for p in range(2)]
        u_sb = [sb(f"u{p}", [128, TT], BF16) for p in range(2)]
        zs_sb = [sb(f"zs{p}", [128, TT], BF16) for p in range(2)]
        Br = [sb(f"Br{p}", [128, NST, TT], BF16) for p in range(2)]
        Cr = [sb(f"Cr{p}", [128, NST, TT], BF16) for p in range(2)]
        dtu = sb("dtu", [128, TT], BF16)
        dA = sb("dA", [128, NST, TT], BF16)
        bt = sb("bt", [128, NST, TT], BF16)
        ht = sb("ht", [128, NST, TT], BF16)
        tr8 = sb("tr8", [128, 8, TT], BF16)
        tr4 = sb("tr4", [128, 4, TT], BF16)
        tr2 = sb("tr2", [128, 2, TT], BF16)
        y_sb = sb("y", [128, TT])
        yd_sb = sb("yd", [128, TT], BF16)
        yg_sb = [sb(f"yg{p}", [128, TT], BF16) for p in range(2)]

        ld = Ctr(es, nc, "ld")
        gld = Ctr(es, nc, "gld")
        act = Ctr(es, nc, "act")
        dve = Ctr(es, nc, "dve")
        st = Ctr(es, nc, "st")

        SY, G, A_, V = nc.sync, nc.gpsimd, nc.scalar, nc.vector

        ld.inc(SY.dma_start(out=D_sb[:, :], in_=D_in[:, :]), 16)
        for e in range(2):
            ld.inc(SY.dma_start(out=A_t[e][:, :], in_=Al_in[e]), 16)
        ld.wait(V)
        ld.wait(A_)
        dve.inc(V.memset(cm1[:, :], -1.0))
        for e in range(2):
            dve.inc(V.memset(hc[e][:, :], 0.0))
        dve.wait(A_)
        for e in range(2):
            act.inc(A_.activation(out=A_scr[:, :], in_=A_t[e][:, :], func=AF.Exp))
            act.wait(V)
            dve.wait(V)
            dve.inc(V.tensor_scalar_mul(out=A_t[e][:, :], in0=A_scr[:, :],
                                        scalar1=cm1[:, :]))
            dve.wait(V)
            dve.wait(A_)

        ld_done = [0, 0]
        gld_done = [0, 0]
        use_done = [0, 0]
        bc_use_done = [0, 0]
        stp_done = [0, 0]
        for tt in range(NT2):
            pb = tt % 2
            dve.wait(G, bc_use_done[pb])
            gld.inc(G.dma_start(out=Br[pb][:, :, :],
                                in_=B_in[tt].unsqueeze(0).broadcast_to([128, NST, TT])), 16)
            gld.inc(G.dma_start(out=Cr[pb][:, :, :],
                                in_=C_in[tt].unsqueeze(0).broadcast_to([128, NST, TT])), 16)
            gld_done[pb] = gld.n
            for ec in range(2):
                p = (2 * tt + ec) % 2
                sl = slice(tt * TT, (tt + 1) * TT)
                dve.wait(SY, use_done[p])
                act.wait(SY)
                ld.inc(SY.dma_start(out=dt_sb[p][:, :], in_=dt3[ec][:, sl]), 16)
                ld.inc(SY.dma_start(out=u_sb[p][:, :], in_=u3[ec][:, sl]), 16)
                ld.inc(SY.dma_start(out=zs_sb[p][:, :], in_=zs3[ec][:, sl]), 16)
                ld_done[p] = ld.n
                ld.wait(V, ld_done[p])
                dve.inc(V.tensor_mul(out=dtu[:, :], in0=dt_sb[p][:, :], in1=u_sb[p][:, :]))
                dve.wait(V)
                ld.wait(A_, ld_done[p])
                dve.wait(A_)
                for n in range(NST):
                    act.inc(A_.activation(out=dA[:, n, :], in_=dt_sb[p][:, :],
                                          func=AF.Exp, scale=A_t[ec][:, n:n + 1]))
                gld.wait(V, gld_done[pb])
                dve.inc(V.tensor_mul(out=bt[:, :, :],
                                     in0=dtu[:, :].unsqueeze(1).broadcast_to([128, NST, TT]),
                                     in1=Br[pb][:, :, :]))
                dve.wait(V)
                act.wait(V)
                for n in range(NST):
                    dve.inc(V.tensor_tensor_scan(out=ht[:, n, :], data0=dA[:, n, :],
                                                 data1=bt[:, n, :],
                                                 initial=hc[ec][:, n:n + 1],
                                                 op0=ALU.mult, op1=ALU.add))
                dve.wait(V)
                dve.inc(V.tensor_copy(out=hc[ec][:, :], in_=ht[:, :, TT - 1:TT]))
                dve.wait(V)
                dve.inc(V.tensor_mul(out=dA[:, :, :], in0=ht[:, :, :], in1=Cr[pb][:, :, :]))
                bc_use_done[pb] = dve.n
                dve.wait(V)
                dve.inc(V.tensor_add(out=tr8[:, :, :], in0=dA[:, 0:8, :], in1=dA[:, 8:16, :]))
                dve.wait(V)
                dve.inc(V.tensor_add(out=tr4[:, :, :], in0=tr8[:, 0:4, :], in1=tr8[:, 4:8, :]))
                dve.wait(V)
                dve.inc(V.tensor_add(out=tr2[:, :, :], in0=tr4[:, 0:2, :], in1=tr4[:, 2:4, :]))
                dve.wait(V)
                dve.inc(V.tensor_add(out=y_sb[:, :], in0=tr2[:, 0, :], in1=tr2[:, 1, :]))
                dve.wait(V)
                dve.inc(V.scalar_tensor_tensor(out=yd_sb[:, :], in0=u_sb[p][:, :],
                                               scalar=D_sb[:, ec:ec + 1], in1=y_sb[:, :],
                                               op0=ALU.mult, op1=ALU.add))
                dve.wait(V)
                st.wait(V, stp_done[p])
                dve.inc(V.tensor_mul(out=yg_sb[p][:, :], in0=yd_sb[:, :], in1=zs_sb[p][:, :]))
                use_done[p] = dve.n
                dve.wait(SY)
                st.inc(SY.dma_start(out=yg3[ec][:, sl], in_=yg_sb[p][:, :]), 16)
                stp_done[p] = st.n
        st.wait(SY)
    return nc


def run_l2(u_full, dt_full, zs_full, B_full, C_full, A_log, Dv):
    if "l2" not in _CACHE:
        _CACHE["l2"] = build_l2()
    nc = _CACHE["l2"]
    NT2 = S // 512
    B_t = np.ascontiguousarray(B_full.reshape(NST, NT2, 512).transpose(1, 0, 2))
    C_t = np.ascontiguousarray(C_full.reshape(NST, NT2, 512).transpose(1, 0, 2))
    in_maps = []
    for core in range(NCORE):
        s0 = EPC * core
        in_maps.append({
            "u_sl": np.ascontiguousarray(u_full[s0:s0 + EPC]),
            "dt_sl": np.ascontiguousarray(dt_full[s0:s0 + EPC]),
            "zs_sl": np.ascontiguousarray(zs_full[s0:s0 + EPC]),
            "B_t": B_t, "C_t": C_t,
            "A_log_t": np.ascontiguousarray(A_log[s0:s0 + EPC].reshape(2, 128, NST)),
            "D_t": np.ascontiguousarray(Dv[s0:s0 + EPC].reshape(2, 128).T),
        })
    res = run_bass_kernel_spmd(nc, in_maps, core_ids=list(range(NCORE)))
    return np.concatenate([res.results[c]["yg_sl"] for c in range(NCORE)], axis=0)


def build_l3():
    nc = bass.Bass("TRN2", target_bir_lowering=False)
    TT = 512
    NTT = TPC // TT
    x_in = nc.dram_tensor("x_fm", [DM, TPC], F32, kind="ExternalInput").ap()
    yg_in = nc.dram_tensor("yg_tok", [E, TPC], BF16, kind="ExternalInput").ap()
    r1_in = nc.dram_tensor("r1_rep", [128, TPC], F32, kind="ExternalInput").ap()
    m1_in = nc.dram_tensor("m1_rep", [128, TPC], F32, kind="ExternalInput").ap()
    ow_in = nc.dram_tensor("outw_T", [E, DM], F32, kind="ExternalInput").ap()
    w0_in = nc.dram_tensor("w0_T", [DM, DM], F32, kind="ExternalInput").ap()
    al_in = nc.dram_tensor("alpha_rep", [128, 1], F32, kind="ExternalInput").ap()
    fb_in = nc.dram_tensor("fb_t", [128, 8], F32, kind="ExternalInput").ap()
    wq_in = nc.dram_tensor("wq_T", [DM, DM], F32, kind="ExternalInput").ap()
    wk_in = nc.dram_tensor("wk_T", [DM, DM], F32, kind="ExternalInput").ap()
    wv_in = nc.dram_tensor("wv_T", [DM, DM], F32, kind="ExternalInput").ap()
    ones_in = nc.dram_tensor("ones_bf", [128, 128], BF16, kind="ExternalInput").ap()
    onesf_in = nc.dram_tensor("ones_f", [128, 128], F32, kind="ExternalInput").ap()

    x2_out = nc.dram_tensor("x2_fm", [DM, TPC], F32, kind="ExternalOutput").ap()
    q_out = nc.dram_tensor("q_bf", [DM, TPC], BF16, kind="ExternalOutput").ap()
    k_out = nc.dram_tensor("k_bf", [DM, TPC], BF16, kind="ExternalOutput").ap()
    v_out = nc.dram_tensor("v_bf", [TPC, DM], BF16, kind="ExternalOutput").ap()
    dbg_mo = nc.dram_tensor("dbg_mo", [DM, TPC], F32, kind="ExternalOutput").ap()
    dbg_x1 = nc.dram_tensor("dbg_x1", [DM, TPC], F32, kind="ExternalOutput").ap()

    x3 = x_in.rearrange("(c p) t -> c p t", p=128)
    yg3 = yg_in.rearrange("(c p) t -> c p t", p=128)
    ow3 = ow_in.rearrange("(c p) d -> c p d", p=128)
    w03 = w0_in.rearrange("(c p) d -> c p d", p=128)
    wq3 = wq_in.rearrange("(c p) d -> c p d", p=128)
    wk3 = wk_in.rearrange("(c p) d -> c p d", p=128)
    wv3 = wv_in.rearrange("(c p) d -> c p d", p=128)
    x23 = x2_out.rearrange("(c p) t -> c p t", p=128)
    q3 = q_out.rearrange("(c p) t -> c p t", p=128)
    k3 = k_out.rearrange("(c p) t -> c p t", p=128)
    v3 = v_out.rearrange("(tc p) d -> tc p d", p=128)

    with contextlib.ExitStack() as es:
        def sb(name, shape, dt=F32, st_=None):
            return (st_ or es).enter_context(nc.sbuf_tensor(name, shape, dt))

        def psum(name, shape, dt=F32, st_=None):
            return (st_ or es).enter_context(nc.psum_tensor(name, shape, dt))

        ones_sb = sb("ones", [128, 128], BF16)
        onesf_sb = sb("onesf", [128, 128], F32)
        ceps = sb("ceps", [128, 1])
        cm1 = sb("cm1", [128, 1])
        stat = {k: sb("st_" + k, [128, TT])
                for k in ["m", "q", "v", "r", "sr", "ar", "br", "t1", "t2"]}
        xb_scr2 = [sb(f"xb{i}", [128, TT], BF16) for i in range(2)]
        sq_scr2 = [sb(f"sqs{i}", [128, TT], BF16) for i in range(2)]
        xb_scr = xb_scr2[0]
        sq_scr = sq_scr2[0]
        x_sb = [[sb(f"x{p}_{c}", [128, TT]) for c in range(8)] for p in range(2)]
        ps_s = psum("ps_s", [128, TT])
        ps_q = psum("ps_q", [128, TT])

        ld = Ctr(es, nc, "ld")
        gld = Ctr(es, nc, "gld")
        pe = Ctr(es, nc, "pe")
        act = Ctr(es, nc, "act")
        dve = Ctr(es, nc, "dve")
        st = Ctr(es, nc, "st")

        SY, G, PE, A_, V = nc.sync, nc.gpsimd, nc.tensor, nc.scalar, nc.vector

        ld.inc(SY.dma_start(out=ones_sb[:, :], in_=ones_in[:, :]), 16)
        ld.inc(SY.dma_start(out=onesf_sb[:, :], in_=onesf_in[:, :]), 16)
        ld.wait(V)
        dve.inc(V.memset(ceps[:, :], EPS))
        dve.inc(V.memset(cm1[:, :], -1.0))
        dve.wait(V); dve.wait(A_); dve.wait(PE)

        def ln_stats(src_tiles, sl_or_none, pe_, want_rms=False):
            for c in range(8):
                pe.wait(V); pe.wait(A_)
                dve.inc(V.tensor_copy(out=xb_scr[:, :], in_=src_tiles[c]))
                act.inc(A_.activation(out=sq_scr[:, :], in_=src_tiles[c], func=AF.Square))
                dve.wait(PE); act.wait(PE)
                pe.inc(PE.matmul(out=ps_s[:, :], lhsT=ones_sb[:, :], rhs=xb_scr[:, :],
                                 start=(c == 0), stop=(c == 7)))
                pe.inc(PE.matmul(out=ps_q[:, :], lhsT=ones_sb[:, :], rhs=sq_scr[:, :],
                                 start=(c == 0), stop=(c == 7)))
            pe.wait(A_); pe.wait(V)
            _ln_stats_math(nc, dve, act, stat, ps_s[:, :], ps_q[:, :], ceps[:, :],
                           want_rms=want_rms)

        with contextlib.ExitStack() as esa:
            ow_bf = [sb(f"ow{c}", [128, DM], BF16, esa) for c in range(16)]
            tern = [sb(f"tern{c}", [128, DM], BF16, esa) for c in range(8)]
            w0f = sb("w0f", [128, DM], F32, esa)
            scrA = sb("scrA", [128, DM], F32, esa)
            scrB = sb("scrB", [128, DM], F32, esa)
            asum = sb("asum", [128, 16], F32, esa)
            delta = sb("delta", [128, 1], F32, esa)
            ndelta = sb("ndelta", [128, 1], F32, esa)
            al_sb = sb("al", [128, 1], F32, esa)
            fb_sb = sb("fb", [128, 8], F32, esa)
            yg_sb = [[sb(f"yg{p}_{c}", [128, TT], BF16, esa) for c in range(16)]
                     for p in range(2)]
            r1_sb = sb("r1s", [128, TT], F32, esa)
            m1_sb = sb("m1s", [128, TT], F32, esa)
            tmp_sb = [sb(f"tmp{c}", [128, TT], F32, esa) for c in range(8)]
            fo_sb = [sb(f"fo{c}", [128, TT], F32, esa) for c in range(8)]
            xln_bf = [sb(f"xln{c}", [128, TT], BF16, esa) for c in range(8)]
            evac = sb("evac", [128, TT], F32, esa)
            ps_a = [psum(f"ps_a{p}", [128, TT], st_=esa) for p in range(2)]
            ps_d = psum("ps_d", [128, 16], st_=esa)

            ld.inc(SY.dma_start(out=al_sb[:, :], in_=al_in[:, :]), 16)
            ld.inc(SY.dma_start(out=fb_sb[:, :], in_=fb_in[:, :]), 16)
            dve.inc(V.memset(asum[:, :], 0.0))
            dve.wait(V)
            for c in range(16):
                gld.inc(G.dma_start(out=ow_bf[c][:, :], in_=ow3[c]), 16)
            for c in range(8):
                ld.inc(SY.dma_start(out=w0f[:, :], in_=w03[c]), 16)
                ld.wait(A_)
                act.inc(A_.activation(out=scrA[:, :], in_=w0f[:, :], func=AF.Abs))
                act.wait(V)
                dve.inc(V.tensor_reduce(out=asum[:, c:c + 1], in_=scrA[:, :],
                                        axis=mybir.AxisListType.X, op=ALU.add))
                dve.wait(V)
                dve.wait(SY); act.wait(SY)
            pe.wait(V)
            dve.wait(PE)
            pe.inc(PE.matmul(out=ps_d[:, :], lhsT=onesf_sb[:, :], rhs=asum[:, :16],
                             start=True, stop=True))
            pe.wait(V)
            dve.inc(V.tensor_reduce(out=delta[:, :], in_=ps_d[:, 0:8],
                                    axis=mybir.AxisListType.X, op=ALU.add))
            dve.wait(V); dve.wait(A_)
            act.inc(A_.activation(out=delta[:, :], in_=delta[:, :], func=AF.Identity,
                                  scale=0.7 / (DM * DM)))
            act.wait(V)
            dve.inc(V.tensor_scalar_mul(out=ndelta[:, :], in0=delta[:, :],
                                        scalar1=cm1[:, :]))
            dve.wait(V)
            for c in range(8):
                dve.wait(SY)
                ld.inc(SY.dma_start(out=w0f[:, :], in_=w03[c]), 16)
                ld.wait(V)
                dve.inc(V.tensor_scalar(out=scrA[:, :], in0=w0f[:, :],
                                        scalar1=delta[:, 0:1], scalar2=None,
                                        op0=ALU.is_gt))
                dve.inc(V.tensor_scalar(out=scrB[:, :], in0=w0f[:, :],
                                        scalar1=ndelta[:, 0:1], scalar2=None,
                                        op0=ALU.is_lt))
                dve.wait(V)
                dve.inc(V.tensor_sub(out=tern[c][:, :], in0=scrA[:, :], in1=scrB[:, :]))
                dve.wait(V)

            gld.wait(PE)
            ld.wait(PE)
            ld.wait(V)

            yld_done = [0, 0]
            yuse_done = [0, 0]
            evac_done = [0, 0]
            r1use_done = 0
            x2store_done = [0, 0]
            for t in range(NTT):
                sl = slice(t * TT, (t + 1) * TT)
                p = t % 2
                dve.wait(SY, yuse_done[p])
                st.wait(SY, x2store_done[p])
                for c in range(8):
                    ld.inc(SY.dma_start(out=x_sb[p][c][:, :], in_=x3[c][:, sl]), 16)
                for c in range(16):
                    ld.inc(SY.dma_start(out=yg_sb[p][c][:, :], in_=yg3[c][:, sl]), 16)
                dve.wait(SY, r1use_done)
                ld.inc(SY.dma_start(out=r1_sb[:, :], in_=r1_in[:, sl]), 16)
                ld.inc(SY.dma_start(out=m1_sb[:, :], in_=m1_in[:, sl]), 16)
                yld_done[p] = ld.n
                ld.wait(V, yld_done[p])
                for c in range(8):
                    dve.inc(V.tensor_sub(out=tmp_sb[c][:, :], in0=x_sb[p][c][:, :],
                                         in1=m1_sb[:, :]))
                    dve.wait(V)
                    dve.inc(V.tensor_mul(out=tmp_sb[c][:, :], in0=tmp_sb[c][:, :],
                                         in1=r1_sb[:, :]))
                    dve.wait(V)
                r1use_done = dve.n
                for do in range(8):
                    pp = do % 2
                    act.wait(PE, evac_done[pp])
                    ld.wait(PE, yld_done[p])
                    for ec in range(16):
                        pe.inc(PE.matmul(out=ps_a[pp][:, :],
                                         lhsT=ow_bf[ec][:, do * 128:(do + 1) * 128],
                                         rhs=yg_sb[p][ec][:, :],
                                         start=(ec == 0), stop=(ec == 15)))
                    pe.wait(A_)
                    dve.wait(A_)
                    act.inc(A_.activation(out=evac[:, :], in_=ps_a[pp][:, :],
                                          func=AF.Identity))
                    evac_done[pp] = act.n
                    act.wait(V)
                    dve.inc(V.tensor_add(out=tmp_sb[do][:, :], in0=tmp_sb[do][:, :],
                                         in1=evac[:, :]))
                    dve.wait(V)
                yuse_done[p] = dve.n
                dve.wait(SY)
                for c in range(8):
                    st.inc(SY.dma_start(out=dbg_mo.rearrange("(c p) t -> c p t", p=128)[c][:, sl], in_=tmp_sb[c][:, :]), 16)
                ln_stats([tmp_sb[c][:, :] for c in range(8)], None, pe)
                for c in range(8):
                    dve.inc(V.tensor_mul(out=stat["t1"][:, :], in0=tmp_sb[c][:, :],
                                         in1=stat["ar"][:, :]))
                    dve.wait(V)
                    dve.inc(V.tensor_sub(out=stat["t1"][:, :], in0=stat["t1"][:, :],
                                         in1=stat["br"][:, :]))
                    dve.wait(V)
                    dve.inc(V.tensor_add(out=x_sb[p][c][:, :], in0=x_sb[p][c][:, :],
                                         in1=stat["t1"][:, :]))
                    dve.wait(V)
                dve.wait(SY)
                for c in range(8):
                    st.inc(SY.dma_start(out=dbg_x1.rearrange("(c p) t -> c p t", p=128)[c][:, sl], in_=x_sb[p][c][:, :]), 16)
                st.wait(V)
                ln_stats([x_sb[p][c][:, :] for c in range(8)], None, pe)
                for c in range(8):
                    dve.inc(V.tensor_mul(out=stat["t1"][:, :], in0=x_sb[p][c][:, :],
                                         in1=stat["ar"][:, :]))
                    dve.wait(V)
                    dve.inc(V.tensor_sub(out=xln_bf[c][:, :], in0=stat["t1"][:, :],
                                         in1=stat["br"][:, :]))
                    dve.wait(V)
                for do in range(8):
                    pp = do % 2
                    act.wait(PE, evac_done[pp])
                    dve.wait(PE)
                    for c in range(8):
                        pe.inc(PE.matmul(out=ps_a[pp][:, :],
                                         lhsT=tern[c][:, do * 128:(do + 1) * 128],
                                         rhs=xln_bf[c][:, :],
                                         start=(c == 0), stop=(c == 7)))
                    pe.wait(A_)
                    act.inc(A_.activation(out=fo_sb[do][:, :], in_=ps_a[pp][:, :],
                                          func=AF.Identity, bias=fb_sb[:, do:do + 1],
                                          scale=al_sb[:, :]))
                    evac_done[pp] = act.n
                act.wait(V)
                ln_stats([fo_sb[c][:, :] for c in range(8)], None, pe)
                st.wait(V)
                for c in range(8):
                    dve.inc(V.tensor_mul(out=stat["t1"][:, :], in0=fo_sb[c][:, :],
                                         in1=stat["ar"][:, :]))
                    dve.wait(V)
                    dve.inc(V.tensor_sub(out=stat["t1"][:, :], in0=stat["t1"][:, :],
                                         in1=stat["br"][:, :]))
                    dve.wait(V)
                    dve.inc(V.tensor_add(out=x_sb[p][c][:, :], in0=x_sb[p][c][:, :],
                                         in1=stat["t1"][:, :]))
                    dve.wait(V)
                    dve.wait(SY)
                    st.inc(SY.dma_start(out=x23[c][:, sl], in_=x_sb[p][c][:, :]), 16)
                x2store_done[p] = st.n
            st.wait(SY)
            nc.all_engine_barrier()

        with contextlib.ExitStack() as esb:
            wq_bf = [sb(f"wqb{c}", [128, DM], BF16, esb) for c in range(8)]
            wk_bf = [sb(f"wkb{c}", [128, DM], BF16, esb) for c in range(8)]
            wv_bf = [sb(f"wvb{c}", [128, DM], BF16, esb) for c in range(8)]
            xln_bf = [sb(f"xl{c}", [128, TT], BF16, esb) for c in range(8)]
            qkv_scr = [sb(f"qkvs{p}", [128, TT], BF16, esb) for p in range(2)]
            v_scr = [sb(f"vs{p}", [128, DM], BF16, esb) for p in range(2)]
            ps_b = [psum(f"ps_b{p}", [128, TT], st_=esb) for p in range(2)]

            for c in range(8):
                gld.inc(G.dma_start(out=wq_bf[c][:, :], in_=wq3[c]), 16)
                gld.inc(G.dma_start(out=wk_bf[c][:, :], in_=wk3[c]), 16)
                gld.inc(G.dma_start(out=wv_bf[c][:, :], in_=wv3[c]), 16)
            gld.wait(PE)
            evac_done = [0, 0]
            store_done = {}
            for t in range(NTT):
                sl = slice(t * TT, (t + 1) * TT)
                p = t % 2
                dve.wait(SY)
                for c in range(8):
                    ld.inc(SY.dma_start(out=x_sb[p][c][:, :], in_=x23[c][:, sl]), 16)
                ld.wait(V); ld.wait(A_); ld.wait(PE)
                ln_stats([x_sb[p][c][:, :] for c in range(8)], None, pe)
                for c in range(8):
                    dve.inc(V.tensor_mul(out=stat["t1"][:, :], in0=x_sb[p][c][:, :],
                                         in1=stat["ar"][:, :]))
                    dve.wait(V)
                    dve.inc(V.tensor_sub(out=xln_bf[c][:, :], in0=stat["t1"][:, :],
                                         in1=stat["br"][:, :]))
                    dve.wait(V)
                dve.wait(PE)
                for wname, wtiles, out3 in (("q", wq_bf, q3), ("k", wk_bf, k3)):
                    for do in range(8):
                        pp = do % 2
                        act.wait(PE, evac_done[pp])
                        for c in range(8):
                            pe.inc(PE.matmul(out=ps_b[pp][:, :],
                                             lhsT=wtiles[c][:, do * 128:(do + 1) * 128],
                                             rhs=xln_bf[c][:, :],
                                             start=(c == 0), stop=(c == 7)))
                        pe.wait(A_)
                        key = (wname, pp)
                        st.wait(A_, store_done.get(key, 0))
                        act.inc(A_.activation(out=qkv_scr[pp][:, :], in_=ps_b[pp][:, :],
                                              func=AF.Identity))
                        evac_done[pp] = act.n
                        act.wait(SY)
                        st.inc(SY.dma_start(out=out3[do][:, sl], in_=qkv_scr[pp][:, :]), 16)
                        store_done[key] = st.n
                for tq in range(TT // 128):
                    pp = tq % 2
                    tc_global = (t * TT + tq * 128) // 128
                    for dv in range(2):
                        act.wait(PE, evac_done[pp])
                        for c in range(8):
                            pe.inc(PE.matmul(
                                out=ps_b[pp][:, dv * 256:dv * 256 + 256] if False else ps_b[pp][:, :],
                                lhsT=xln_bf[c][:, tq * 128:(tq + 1) * 128],
                                rhs=wv_bf[c][:, dv * 512:(dv + 1) * 512],
                                start=(c == 0), stop=(c == 7)))
                        pe.wait(A_)
                        key = ("v", pp, dv)
                        st.wait(A_, store_done.get(key, 0))
                        act.inc(A_.activation(out=v_scr[pp][:, dv * 512:(dv + 1) * 512],
                                              in_=ps_b[pp][:, :], func=AF.Identity))
                        evac_done[pp] = act.n
                        act.wait(SY)
                        st.inc(SY.dma_start(out=v3[tc_global][:, dv * 512:(dv + 1) * 512],
                                            in_=v_scr[pp][:, dv * 512:(dv + 1) * 512]), 16)
                        store_done[key] = st.n
            st.wait(SY)
    return nc


def run_l3(inp, x_slices, yg_full, r1m1_list):
    if "l3" not in _CACHE:
        _CACHE["l3"] = build_l3()
    nc = _CACHE["l3"]
    shared = {
        "outw_T": np.ascontiguousarray(inp["L0_attn_out_proj_w"].T),
        "w0_T": np.ascontiguousarray(inp["L0_ffn_weight"].T),
        "alpha_rep": np.ascontiguousarray(np.broadcast_to(
            inp["L0_ffn_alpha"].reshape(1, 1), (128, 1))).astype(np.float32),
        "fb_t": _cvec(inp["L0_ffn_bias"]),
        "wq_T": np.ascontiguousarray(inp["L1_attn_Wq"].T),
        "wk_T": np.ascontiguousarray(inp["L1_attn_Wk"].T),
        "wv_T": np.ascontiguousarray(inp["L1_attn_Wv"].T),
        "ones_bf": np.ones((128, 128), dtype=BF),
        "ones_f": np.ones((128, 128), dtype=np.float32),
    }
    in_maps = []
    for core in range(NCORE):
        rows = _tok_rows(core)
        m = dict(shared)
        m["x_fm"] = x_slices[core]
        m["yg_tok"] = np.ascontiguousarray(yg_full[:, rows])
        m["r1_rep"] = np.ascontiguousarray(
            np.broadcast_to(r1m1_list[core][0:1], (128, TPC)))
        m["m1_rep"] = np.ascontiguousarray(
            np.broadcast_to(r1m1_list[core][1:2], (128, TPC)))
        in_maps.append(m)
    res = run_bass_kernel_spmd(nc, in_maps, core_ids=list(range(NCORE)))
    return res.results


def build_l4():
    nc = bass.Bass("TRN2", target_bir_lowering=False)
    TT = 512
    q_in = nc.dram_tensor("q_sl", [DM, TPC], BF16, kind="ExternalInput").ap()
    k_in = nc.dram_tensor("k_all", [DM, S], BF16, kind="ExternalInput").ap()
    v_in = nc.dram_tensor("v_all", [S, DM], BF16, kind="ExternalInput").ap()
    x2_in = nc.dram_tensor("x2_fm", [DM, TPC], F32, kind="ExternalInput").ap()
    mb_in = nc.dram_tensor("mask_big", [128, 1920], F32, kind="ExternalInput").ap()
    sel_in = nc.dram_tensor("sel_rep", [128, 32], F32, kind="ExternalInput").ap()
    binv_in = nc.dram_tensor("binv_rep", [128, 32], F32, kind="ExternalInput").ap()
    idb_in = nc.dram_tensor("ident_bf", [128, 128], BF16, kind="ExternalInput").ap()
    ones_in = nc.dram_tensor("ones_bf", [128, 128], BF16, kind="ExternalInput").ap()
    x3_out = nc.dram_tensor("x3_fm", [DM, TPC], F32, kind="ExternalOutput").ap()

    q3 = q_in.rearrange("(c p) t -> c p t", p=128)
    k3 = k_in.rearrange("(c p) t -> c p t", p=128)
    v3 = v_in.rearrange("(tc p) d -> tc p d", p=128)
    x23 = x2_in.rearrange("(c p) t -> c p t", p=128)
    x33 = x3_out.rearrange("(c p) t -> c p t", p=128)

    KT = 1024

    with contextlib.ExitStack() as es:
        def sb(name, shape, dt=F32):
            return es.enter_context(nc.sbuf_tensor(name, shape, dt))

        def psum(name, shape, dt=F32):
            return es.enter_context(nc.psum_tensor(name, shape, dt))

        ones_sb = sb("ones", [128, 128], BF16)
        ident_bf = sb("identb", [128, 128], BF16)
        mask_sb = sb("mask", [128, 1920])
        sel_sb = sb("sel", [128, 32])
        binv_sb = sb("binv", [128, 32])
        cn32 = sb("cn32", [128, 1])
        ceps = sb("ceps", [128, 1])
        stat = {k: sb("st_" + k, [128, TT])
                for k in ["m", "q", "v", "r", "sr", "ar", "br", "t1", "t2"]}
        xb_scr2 = [sb(f"xb{i}", [128, TT], BF16) for i in range(2)]
        sq_scr2 = [sb(f"sqs{i}", [128, TT], BF16) for i in range(2)]
        xb_scr = xb_scr2[0]
        sq_scr = sq_scr2[0]
        q_sb = [sb(f"q{c}", [128, CH], BF16) for c in range(8)]
        kt_sb = [[sb(f"k{p}_{c}", [128, KT], BF16) for c in range(8)] for p in range(2)]
        vt_sb = [[sb(f"v{p}_{c}", [128, DM], BF16) for c in range(8)] for p in range(2)]
        O_sb = [sb(f"O{j}", [128, DM]) for j in range(8)]
        mrow = [sb(f"m{j}", [128, 1]) for j in range(8)]
        lrow = [sb(f"l{j}", [128, 1]) for j in range(8)]
        mnew = sb("mnew", [128, 1])
        mloc = sb("mloc", [128, 1])
        frow = sb("frow", [128, 1])
        bneg = sb("bneg", [128, 1])
        rs0 = sb("rs0", [128, 1])
        rl = sb("rl", [128, 1])
        p_bf = sb("pbf", [128, KT], BF16)
        pt_sb = sb("pt", [128, KT], BF16)
        of_bf = sb("of", [128, DM], BF16)
        attn_bf = [sb(f"at{c}", [128, CH], BF16) for c in range(8)]
        x2_sb = [sb(f"x2_{c}", [128, TT]) for c in range(8)]

        ps_S = psum("ps_S", [128, KT])
        ps_pt = psum("ps_pt", [128, KT], BF16)
        ps_av = psum("ps_av", [128, DM])
        ps_s = psum("ps_s", [128, TT])
        ps_q = psum("ps_q", [128, TT])

        ld = Ctr(es, nc, "ld")
        pe = Ctr(es, nc, "pe")
        act = Ctr(es, nc, "act")
        dve = Ctr(es, nc, "dve")
        st = Ctr(es, nc, "st")

        SY, G, PE, A_, V = nc.sync, nc.gpsimd, nc.tensor, nc.scalar, nc.vector

        ld.inc(SY.dma_start(out=ones_sb[:, :], in_=ones_in[:, :]), 16)
        ld.inc(SY.dma_start(out=ident_bf[:, :], in_=idb_in[:, :]), 16)
        ld.inc(SY.dma_start(out=mask_sb[:, :], in_=mb_in[:, :]), 16)
        ld.inc(SY.dma_start(out=sel_sb[:, :], in_=sel_in[:, :]), 16)
        ld.inc(SY.dma_start(out=binv_sb[:, :], in_=binv_in[:, :]), 16)
        ld.wait(V)
        dve.inc(V.memset(ceps[:, :], EPS))
        dve.inc(V.memset(cn32[:, :], -1.0 / 32.0))
        dve.wait(V); dve.wait(A_)

        def ln_stats(src_tiles, pss, psq, want_rms=False):
            mm_done = [pe.n, pe.n]
            for c in range(8):
                cp = c % 2
                pe.wait(V, mm_done[cp])
                pe.wait(A_, mm_done[cp])
                dve.inc(V.tensor_copy(out=xb_scr2[cp][:, :], in_=src_tiles[c]))
                act.inc(A_.activation(out=sq_scr2[cp][:, :], in_=src_tiles[c], func=AF.Square))
                dve.wait(PE); act.wait(PE)
                pe.inc(PE.matmul(out=pss, lhsT=ones_sb[:, :], rhs=xb_scr2[cp][:, :],
                                 start=(c == 0), stop=(c == 7)))
                pe.inc(PE.matmul(out=psq, lhsT=ones_sb[:, :], rhs=sq_scr2[cp][:, :],
                                 start=(c == 0), stop=(c == 7)))
                mm_done[cp] = pe.n
            pe.wait(A_); pe.wait(V)
            _ln_stats_math(nc, dve, act, stat, pss, psq, ceps[:, :],
                           want_rms=want_rms)

        kld_done = [0, 0]
        for qc in range(2):
            dve.wait(SY); act.wait(SY); pe.wait(SY)
            for c in range(8):
                ld.inc(SY.dma_start(out=q_sb[c][:, :],
                                    in_=q3[c][:, qc * CH:(qc + 1) * CH]), 16)
            qld = ld.n
            for j in range(8):
                dve.inc(V.memset(O_sb[j][:, :], 0.0))
                dve.inc(V.memset(mrow[j][:, :], -1e30))
                dve.inc(V.memset(lrow[j][:, :], 0.0))
            dve.wait(V); dve.wait(A_); dve.wait(PE)
            for kt in range(16):
                idx = qc * 16 + kt
                p = kt % 2
                pe.wait(SY)
                for c in range(8):
                    ld.inc(SY.dma_start(out=kt_sb[p][c][:, :],
                                        in_=k3[c][:, kt * KT:(kt + 1) * KT]), 16)
                for c in range(8):
                    ld.inc(SY.dma_start(out=vt_sb[p][c][:, :],
                                        in_=v3[kt * 8 + c]), 16)
                kld_done[p] = ld.n
                ld.wait(PE, kld_done[p])
                ld.wait(PE, qld)
                for j in range(8):
                    act.wait(PE)
                    dve.wait(PE)
                    for half in range(2):
                        for c in range(8):
                            pe.inc(PE.matmul(
                                out=ps_S[:, half * 512:(half + 1) * 512],
                                lhsT=q_sb[c][:, j * 128:(j + 1) * 128],
                                rhs=kt_sb[p][c][:, half * 512:(half + 1) * 512],
                                start=(c == 0), stop=(c == 7)))
                    pe.wait(V); pe.wait(A_)
                    dve.inc(V.scalar_tensor_tensor(
                        out=ps_S[:, :], in0=mask_sb[:, 896 - 128 * j:1920 - 128 * j],
                        scalar=sel_sb[:, idx:idx + 1], in1=ps_S[:, :],
                        op0=ALU.mult, op1=ALU.add))
                    dve.wait(V)
                    dve.inc(V.tensor_reduce(out=mloc[:, :], in_=ps_S[:, :],
                                            axis=mybir.AxisListType.X, op=ALU.max))
                    dve.wait(V)
                    dve.inc(V.tensor_max(out=mnew[:, :], in0=mrow[j][:, :],
                                         in1=mloc[:, :]))
                    dve.wait(V)
                    dve.inc(V.tensor_sub(out=frow[:, :], in0=mrow[j][:, :],
                                         in1=mnew[:, :]))
                    dve.wait(V)
                    dve.inc(V.scalar_tensor_tensor(
                        out=bneg[:, :], in0=mnew[:, :], scalar=cn32[:, :],
                        in1=binv_sb[:, idx:idx + 1], op0=ALU.mult, op1=ALU.add))
                    dve.wait(V); dve.wait(A_)
                    act.inc(A_.activation(out=frow[:, :], in_=frow[:, :], func=AF.Exp,
                                          scale=1.0 / 32.0))
                    act.inc(A_.activation(out=p_bf[:, :], in_=ps_S[:, :], func=AF.Exp,
                                          scale=1.0 / 32.0, bias=bneg[:, :],
                                          accum_out=rs0[:, :]))
                    act.wait(V)
                    dve.inc(V.scalar_tensor_tensor(out=lrow[j][:, :], in0=lrow[j][:, :],
                                                   scalar=frow[:, :], in1=rs0[:, :],
                                                   op0=ALU.mult, op1=ALU.add))
                    dve.wait(V)
                    dve.inc(V.tensor_copy(out=mrow[j][:, :], in_=mnew[:, :]))
                    dve.wait(V)
                    act.wait(PE)
                    for kc in range(8):
                        pe.inc(PE.transpose(out=ps_pt[:, kc * 128:(kc + 1) * 128],
                                            in_=p_bf[:, kc * 128:(kc + 1) * 128],
                                            identity=ident_bf[:, :]))
                    pe.wait(A_)
                    act.inc(A_.activation(out=pt_sb[:, :], in_=ps_pt[:, :],
                                          func=AF.Identity))
                    act.wait(PE)
                    dve.wait(PE)
                    for kc in range(8):
                        pe.inc(PE.matmul(out=ps_av[:, 0:512],
                                         lhsT=pt_sb[:, kc * 128:(kc + 1) * 128],
                                         rhs=vt_sb[p][kc][:, 0:512],
                                         start=(kc == 0), stop=(kc == 7)))
                        pe.inc(PE.matmul(out=ps_av[:, 512:1024],
                                         lhsT=pt_sb[:, kc * 128:(kc + 1) * 128],
                                         rhs=vt_sb[p][kc][:, 512:1024],
                                         start=(kc == 0), stop=(kc == 7)))
                    pe.wait(V)
                    dve.inc(V.scalar_tensor_tensor(out=O_sb[j][:, :], in0=O_sb[j][:, :],
                                                   scalar=frow[:, :], in1=ps_av[:, :],
                                                   op0=ALU.mult, op1=ALU.add))
                    dve.wait(V)
            for j in range(8):
                dve.wait(A_)
                act.inc(A_.activation(out=rl[:, :], in_=lrow[j][:, :], func=AF.Ln))
                act.inc(A_.activation(out=rl[:, :], in_=rl[:, :], func=AF.Exp,
                                      scale=-1.0))
                act.wait(V)
                dve.inc(V.tensor_scalar_mul(out=of_bf[:, :], in0=O_sb[j][:, :],
                                            scalar1=rl[:, :]))
                dve.wait(V); dve.wait(PE)
                act.wait(PE)
                for dc in range(8):
                    pe.inc(PE.transpose(out=ps_pt[:, dc * 128:(dc + 1) * 128],
                                        in_=of_bf[:, dc * 128:(dc + 1) * 128],
                                        identity=ident_bf[:, :]))
                pe.wait(A_)
                for dc in range(8):
                    act.inc(A_.activation(out=attn_bf[dc][:, j * 128:(j + 1) * 128],
                                          in_=ps_pt[:, dc * 128:(dc + 1) * 128],
                                          func=AF.Identity))
                act.wait(PE); act.wait(V)
            for tloc in range(CH // TT):
                sl_loc = slice(tloc * TT, (tloc + 1) * TT)
                sl_out = slice(qc * CH + tloc * TT, qc * CH + (tloc + 1) * TT)
                st.wait(SY)
                dve.wait(SY)
                for c in range(8):
                    ld.inc(SY.dma_start(out=x2_sb[c][:, :], in_=x23[c][:, sl_out]), 16)
                ld.wait(V); ld.wait(A_); ld.wait(PE)
                ln_stats([attn_bf[c][:, sl_loc] for c in range(8)])
                for c in range(8):
                    dve.inc(V.tensor_mul(out=stat["t1"][:, :], in0=attn_bf[c][:, sl_loc],
                                         in1=stat["ar"][:, :]))
                    dve.wait(V)
                    dve.inc(V.tensor_sub(out=stat["t1"][:, :], in0=stat["t1"][:, :],
                                         in1=stat["br"][:, :]))
                    dve.wait(V)
                    dve.inc(V.tensor_add(out=x2_sb[c][:, :], in0=x2_sb[c][:, :],
                                         in1=stat["t1"][:, :]))
                    dve.wait(V); dve.wait(SY)
                    st.inc(SY.dma_start(out=x33[c][:, sl_out], in_=x2_sb[c][:, :]), 16)
        st.wait(SY)
    return nc


def run_l4(q_slices, k_full, v_full, x2_slices):
    if "l4" not in _CACHE:
        _CACHE["l4"] = build_l4()
    nc = _CACHE["l4"]
    mask_big = np.zeros((128, 1920), np.float32)
    for rr in range(128):
        mask_big[rr, 897 + rr:] = -1e9
    shared = {
        "k_all": np.ascontiguousarray(k_full),
        "v_all": np.ascontiguousarray(v_full),
        "mask_big": mask_big,
        "ident_bf": np.eye(128, dtype=BF),
        "ones_bf": np.ones((128, 128), dtype=BF),
    }
    in_maps = []
    for core in range(NCORE):
        m = dict(shared)
        m["q_sl"] = q_slices[core]
        m["x2_fm"] = x2_slices[core]
        sel = np.zeros((1, 32), np.float32)
        binv = np.zeros((1, 32), np.float32)
        for qc in range(2):
            gq = CORE_CHUNKS[core][qc]
            sel[0, qc * 16 + gq] = 1.0
            for kt in range(gq + 1, 16):
                binv[0, qc * 16 + kt] = -1e30
        m["sel_rep"] = np.ascontiguousarray(np.broadcast_to(sel, (128, 32)))
        m["binv_rep"] = np.ascontiguousarray(np.broadcast_to(binv, (128, 32)))
        in_maps.append(m)
    res = run_bass_kernel_spmd(nc, in_maps, core_ids=list(range(NCORE)))
    return res.results


def build_l5():
    nc = bass.Bass("TRN2", target_bir_lowering=False)
    TT = 512
    NTT = TPC // TT
    DFF = 4096
    FSL = 1024
    NFS = DFF // FSL
    x3_in = nc.dram_tensor("x3_fm", [DM, TPC], F32, kind="ExternalInput").ap()
    rw_in = nc.dram_tensor("router_w", [DM, 8], F32, kind="ExternalInput").ap()
    rb_in = nc.dram_tensor("rb_rep", [128, 8], F32, kind="ExternalInput").ap()
    w1_in = nc.dram_tensor("W1", [8, DM, DFF], F32, kind="ExternalInput").ap()
    b1_in = nc.dram_tensor("b1_t", [8, 128, 32], F32, kind="ExternalInput").ap()
    w2_in = nc.dram_tensor("W2", [8, DFF, DM], F32, kind="ExternalInput").ap()
    b2_in = nc.dram_tensor("b2", [8, DM], F32, kind="ExternalInput").ap()
    ones_in = nc.dram_tensor("ones_bf", [128, 128], BF16, kind="ExternalInput").ap()
    onesf_in = nc.dram_tensor("ones_f", [128, 128], F32, kind="ExternalInput").ap()
    idf_in = nc.dram_tensor("ident_f", [128, 128], F32, kind="ExternalInput").ap()

    out_out = nc.dram_tensor("out_fm", [DM, TPC], F32, kind="ExternalOutput").ap()
    cfm_dram = nc.dram_tensor("cfm_scr", [8, TPC], F32).ap()

    x33 = x3_in.rearrange("(c p) t -> c p t", p=128)
    rw3 = rw_in.rearrange("(c p) e -> c p e", p=128)
    o3 = out_out.rearrange("(c p) t -> c p t", p=128)

    with contextlib.ExitStack() as es:
        def sb(name, shape, dt=F32):
            return es.enter_context(nc.sbuf_tensor(name, shape, dt))

        def psum(name, shape, dt=F32):
            return es.enter_context(nc.psum_tensor(name, shape, dt))

        ones_sb = sb("ones", [128, 128], BF16)
        onesf_sb = sb("onesf", [128, 128], F32)
        idf_sb = sb("idf", [128, 128], F32)
        ceps = sb("ceps", [128, 1])
        cm1 = sb("cm1", [128, 1])
        cone = sb("cone", [128, 1])
        cneg = sb("cneg", [128, 1])
        stat = {k: sb("st_" + k, [128, TT])
                for k in ["m", "q", "v", "r", "sr", "ar", "br", "t1", "t2"]}
        xb_scr2 = [sb(f"xb{i}", [128, TT], BF16) for i in range(2)]
        sq_scr2 = [sb(f"sqs{i}", [128, TT], BF16) for i in range(2)]
        xb_scr = xb_scr2[0]
        sq_scr = sq_scr2[0]
        x_sb = [sb(f"x{c}", [128, TT]) for c in range(8)]
        xln_bf = [sb(f"xln{c}", [128, TPC], BF16) for c in range(8)]
        rw_sb = [sb(f"rw{c}", [128, 8], BF16) for c in range(8)]
        rb_sb = sb("rb", [128, 8])
        lg = sb("lg", [128, 8])
        ge1 = sb("ge1", [128, 8])
        msk = sb("msk", [128, 8])
        m1c = sb("m1c", [128, 1])
        m2c = sb("m2c", [128, 1])
        nm1 = sb("nm1", [128, 1])
        d21 = sb("d21", [128, 1])
        rinv = sb("rinv", [128, 1])
        ee = sb("ee", [128, 8])
        cc_sb = sb("cc", [128, 8])
        cfm = sb("cfm", [8, TPC])
        crep_all = sb("crep", [128, TPC])
        b1_sb = sb("b1", [128, 32])
        b2_sb = sb("b2s", [8, DM])
        w1_bf = [sb(f"w1_{c}", [128, FSL], BF16) for c in range(8)]
        w2_bf = [sb(f"w2_{c}", [128, DM], BF16) for c in range(8)]
        h_bf = [sb(f"h{f}", [128, TT], BF16) for f in range(8)]
        yev = sb("yev", [128, TT])
        y_acc = [sb(f"ya{c}", [128, TPC]) for c in range(8)]

        ps_r = psum("ps_r", [128, 8])
        ps_ct = psum("ps_ct", [8, 128])
        ps_h = psum("ps_h", [128, TT])
        ps_y = [psum(f"ps_y{p}", [128, TT]) for p in range(2)]
        ps_s = psum("ps_s", [128, TT])
        ps_q = psum("ps_q", [128, TT])

        ld = Ctr(es, nc, "ld")
        gld = Ctr(es, nc, "gld")
        pe = Ctr(es, nc, "pe")
        act = Ctr(es, nc, "act")
        dve = Ctr(es, nc, "dve")
        st = Ctr(es, nc, "st")

        SY, G, PE, A_, V = nc.sync, nc.gpsimd, nc.tensor, nc.scalar, nc.vector

        ld.inc(SY.dma_start(out=ones_sb[:, :], in_=ones_in[:, :]), 16)
        ld.inc(SY.dma_start(out=onesf_sb[:, :], in_=onesf_in[:, :]), 16)
        ld.inc(SY.dma_start(out=idf_sb[:, :], in_=idf_in[:, :]), 16)
        ld.inc(SY.dma_start(out=rb_sb[:, :], in_=rb_in[:, :]), 16)
        ld.inc(SY.dma_start(out=b2_sb[:, :], in_=b2_in[:, :]), 16)
        for c in range(8):
            gld.inc(G.dma_start(out=rw_sb[c][:, :], in_=rw3[c]), 16)
        ld.wait(V)
        gld.wait(PE)
        dve.inc(V.memset(ceps[:, :], EPS))
        dve.inc(V.memset(cm1[:, :], -1.0))
        dve.inc(V.memset(cone[:, :], 1.0))
        dve.inc(V.memset(cneg[:, :], -1e30))
        dve.wait(V); dve.wait(A_)

        def ln_stats(src_tiles, pss, psq, want_rms=False):
            mm_done = [pe.n, pe.n]
            for c in range(8):
                cp = c % 2
                pe.wait(V, mm_done[cp])
                pe.wait(A_, mm_done[cp])
                dve.inc(V.tensor_copy(out=xb_scr2[cp][:, :], in_=src_tiles[c]))
                act.inc(A_.activation(out=sq_scr2[cp][:, :], in_=src_tiles[c], func=AF.Square))
                dve.wait(PE); act.wait(PE)
                pe.inc(PE.matmul(out=pss, lhsT=ones_sb[:, :], rhs=xb_scr2[cp][:, :],
                                 start=(c == 0), stop=(c == 7)))
                pe.inc(PE.matmul(out=psq, lhsT=ones_sb[:, :], rhs=sq_scr2[cp][:, :],
                                 start=(c == 0), stop=(c == 7)))
                mm_done[cp] = pe.n
            pe.wait(A_); pe.wait(V)
            _ln_stats_math(nc, dve, act, stat, pss, psq, ceps[:, :],
                           want_rms=want_rms)

        xload_done = [0, 0]
        for t in range(NTT):
            sl = slice(t * TT, (t + 1) * TT)
            dve.wait(SY)
            for c in range(8):
                ld.inc(SY.dma_start(out=x_sb[c][:, :], in_=x33[c][:, sl]), 16)
            ld.wait(V); ld.wait(A_); ld.wait(PE)
            ln_stats([x_sb[c][:, :] for c in range(8)], ps_s[:, :], ps_q[:, :])
            for c in range(8):
                dve.inc(V.tensor_mul(out=stat["t1"][:, :], in0=x_sb[c][:, :],
                                     in1=stat["ar"][:, :]))
                dve.wait(V)
                dve.inc(V.tensor_sub(out=xln_bf[c][:, sl], in0=stat["t1"][:, :],
                                     in1=stat["br"][:, :]))
                dve.wait(V)

        for tc in range(TPC // 128):
            tsl = slice(tc * 128, (tc + 1) * 128)
            act.wait(PE); dve.wait(PE)
            for c in range(8):
                pe.inc(PE.matmul(out=ps_r[:, :], lhsT=xln_bf[c][:, tsl],
                                 rhs=rw_sb[c][:, :], start=(c == 0), stop=(c == 7)))
            pe.wait(A_); pe.wait(V)
            act.inc(A_.activation(out=lg[:, :], in_=ps_r[:, :], func=AF.Identity))
            act.wait(V)
            dve.inc(V.tensor_add(out=lg[:, :], in0=lg[:, :], in1=rb_sb[:, :]))
            dve.wait(V)
            dve.inc(V.tensor_reduce(out=m1c[:, :], in_=lg[:, :],
                                    axis=mybir.AxisListType.X, op=ALU.max))
            dve.wait(V)
            dve.inc(V.tensor_scalar(out=ge1[:, :], in0=lg[:, :], scalar1=m1c[:, 0:1],
                                    scalar2=None, op0=ALU.is_ge))
            dve.wait(V)
            dve.inc(V.scalar_tensor_tensor(out=msk[:, :], in0=ge1[:, :],
                                           scalar=cneg[:, :], in1=lg[:, :],
                                           op0=ALU.mult, op1=ALU.add))
            dve.wait(V)
            dve.inc(V.tensor_reduce(out=m2c[:, :], in_=msk[:, :],
                                    axis=mybir.AxisListType.X, op=ALU.max))
            dve.wait(V)
            dve.inc(V.tensor_scalar(out=ge1[:, :], in0=lg[:, :], scalar1=m2c[:, 0:1],
                                    scalar2=None, op0=ALU.is_ge))
            dve.wait(V)
            dve.inc(V.tensor_scalar_mul(out=nm1[:, :], in0=m1c[:, :],
                                        scalar1=cm1[:, :]))
            dve.wait(V)
            dve.inc(V.tensor_sub(out=d21[:, :], in0=m2c[:, :], in1=m1c[:, :]))
            dve.wait(V); dve.wait(A_)
            act.inc(A_.activation(out=ee[:, :], in_=lg[:, :], func=AF.Exp,
                                  bias=nm1[:, :]))
            act.inc(A_.activation(out=d21[:, :], in_=d21[:, :], func=AF.Exp))
            act.wait(V)
            dve.inc(V.tensor_add(out=d21[:, :], in0=d21[:, :], in1=cone[:, :]))
            dve.wait(V); dve.wait(A_)
            act.inc(A_.activation(out=d21[:, :], in_=d21[:, :], func=AF.Ln))
            act.inc(A_.activation(out=rinv[:, :], in_=d21[:, :], func=AF.Exp,
                                  scale=-1.0))
            act.wait(V)
            dve.inc(V.tensor_mul(out=cc_sb[:, :], in0=ee[:, :], in1=ge1[:, :]))
            dve.wait(V)
            dve.inc(V.tensor_scalar_mul(out=cc_sb[:, :], in0=cc_sb[:, :],
                                        scalar1=rinv[:, 0:1]))
            dve.wait(V); dve.wait(PE)
            pe.inc(PE.transpose(out=ps_ct[:, :], in_=cc_sb[:, :].bitcast(F32) if False else cc_sb[:, :],
                                identity=idf_sb[:, :]))
            pe.wait(A_)
            act.inc(A_.activation(out=cfm[:, tsl], in_=ps_ct[:, :], func=AF.Identity))
            act.wait(PE); act.wait(V)

        act.wait(SY)
        st.inc(SY.dma_start(out=cfm_dram[:, :], in_=cfm[:, :]), 16)

        for t in range(NTT):
            sl = slice(t * TT, (t + 1) * TT)
            for do in range(8):
                pp = do % 2
                act.wait(PE)
                pe.inc(PE.matmul(out=ps_y[pp][:, :], lhsT=b2_sb[:, do * 128:(do + 1) * 128],
                                 rhs=cfm[:, sl], start=True, stop=True))
                pe.wait(A_)
                act.inc(A_.activation(out=y_acc[do][:, sl], in_=ps_y[pp][:, :],
                                      func=AF.Identity))
        act.wait(V); act.wait(PE)

        w1_3 = w1_in.rearrange("e (c p) f -> e c p f", p=128)
        w2_3 = w2_in.rearrange("e (c p) d -> e c p d", p=128)
        st.wait(G)
        for e in range(8):
            dve.wait(G)
            gld.inc(G.dma_start(out=crep_all[:, :],
                                in_=cfm_dram[e:e + 1, :].broadcast_to([128, TPC])), 16)
            gld.wait(V)
            act.wait(SY)
            ld.inc(SY.dma_start(out=b1_sb[:, :], in_=b1_in[e]), 16)
            for fs in range(NFS):
                gld.wait(G) if False else None
                pe.wait(G)
                for c in range(8):
                    gld.inc(G.dma_start(out=w1_bf[c][:, :],
                                        in_=w1_3[e, c][:, fs * FSL:(fs + 1) * FSL]), 16)
                for fc in range(8):
                    gld.inc(G.dma_start(out=w2_bf[fc][:, :],
                                        in_=w2_3[e, fs * 8 + fc]), 16)
                gld.wait(PE)
                ld.wait(A_)
                for t in range(NTT):
                    sl = slice(t * TT, (t + 1) * TT)
                    for fo in range(8):
                        act.wait(PE)
                        dve.wait(PE)
                        for c in range(8):
                            pe.inc(PE.matmul(out=ps_h[:, :],
                                             lhsT=w1_bf[c][:, fo * 128:(fo + 1) * 128],
                                             rhs=xln_bf[c][:, sl],
                                             start=(c == 0), stop=(c == 7)))
                        pe.wait(A_)
                        act.inc(A_.activation(out=h_bf[fo][:, :], in_=ps_h[:, :],
                                              func=AF.Relu,
                                              bias=b1_sb[:, fs * 8 + fo:fs * 8 + fo + 1]))
                    act.wait(PE)
                    for do in range(8):
                        pp = do % 2
                        dve.wait(PE)
                        for fc in range(8):
                            pe.inc(PE.matmul(out=ps_y[pp][:, :],
                                             lhsT=w2_bf[fc][:, do * 128:(do + 1) * 128],
                                             rhs=h_bf[fc][:, :],
                                             start=(fc == 0), stop=(fc == 7)))
                        pe.wait(V)
                        dve.inc(V.tensor_mul(out=yev[:, :], in0=ps_y[pp][:, :],
                                             in1=crep_all[:, sl]))
                        dve.wait(V)
                        dve.inc(V.tensor_add(out=y_acc[do][:, sl], in0=y_acc[do][:, sl],
                                             in1=yev[:, :]))
                        dve.wait(V)

        for t in range(NTT):
            sl = slice(t * TT, (t + 1) * TT)
            dve.wait(SY); st.wait(SY)
            for c in range(8):
                ld.inc(SY.dma_start(out=x_sb[c][:, :], in_=x33[c][:, sl]), 16)
            ld.wait(V); ld.wait(A_); ld.wait(PE)
            ln_stats([y_acc[c][:, sl] for c in range(8)], ps_s[:, :], ps_q[:, :])
            for c in range(8):
                dve.inc(V.tensor_mul(out=stat["t1"][:, :], in0=y_acc[c][:, sl],
                                     in1=stat["ar"][:, :]))
                dve.wait(V)
                dve.inc(V.tensor_sub(out=stat["t1"][:, :], in0=stat["t1"][:, :],
                                     in1=stat["br"][:, :]))
                dve.wait(V)
                dve.inc(V.tensor_add(out=x_sb[c][:, :], in0=x_sb[c][:, :],
                                     in1=stat["t1"][:, :]))
                dve.wait(V); dve.wait(SY)
                st.inc(SY.dma_start(out=o3[c][:, sl], in_=x_sb[c][:, :]), 16)
        st.wait(SY)
    return nc


def run_l5(inp, x3_slices):
    if "l5" not in _CACHE:
        _CACHE["l5"] = build_l5()
    nc = _CACHE["l5"]
    shared = {
        "router_w": np.ascontiguousarray(inp["L1_ffn_router_w"]),
        "rb_rep": np.ascontiguousarray(np.broadcast_to(
            inp["L1_ffn_router_b"].reshape(1, 8), (128, 8))).astype(np.float32),
        "W1": np.ascontiguousarray(inp["L1_ffn_W1"]),
        "b1_t": np.ascontiguousarray(
            inp["L1_ffn_b1"].reshape(8, 32, 128).transpose(0, 2, 1)),
        "W2": np.ascontiguousarray(inp["L1_ffn_W2"]),
        "b2": np.ascontiguousarray(inp["L1_ffn_b2"]),
        "ones_bf": np.ones((128, 128), dtype=BF),
        "ones_f": np.ones((128, 128), dtype=np.float32),
        "ident_f": np.eye(128, dtype=np.float32),
    }
    in_maps = []
    for core in range(NCORE):
        m = dict(shared)
        m["x3_fm"] = x3_slices[core]
        in_maps.append(m)
    res = run_bass_kernel_spmd(nc, in_maps, core_ids=list(range(NCORE)))
    return res.results


def kernel(**inp):
    inp = {k: np.asarray(v) for k, v in inp.items()}
    x = inp["L0_attn_x"][0]

    l1 = run_l1(inp)
    u_full = np.empty((E, S), dtype=BF)
    dt_full = np.empty((E, S), dtype=BF)
    zs_full = np.empty((E, S), dtype=BF)
    B_full = np.empty((NST, S), dtype=np.float32)
    C_full = np.empty((NST, S), dtype=np.float32)
    r1m1 = []
    for core in range(NCORE):
        rows = _tok_rows(core)
        r = l1[core]
        u_full[:, rows] = r["u_bf"]
        dt_full[:, rows] = r["dt_bf"]
        zs_full[:, rows] = r["zs_bf"]
        B_full[:, rows] = r["xdbl"][64:80]
        C_full[:, rows] = r["xdbl"][80:96]
        r1m1.append(r["r1m1"])

    yg_full = run_l2(u_full, dt_full, zs_full, B_full, C_full,
                     inp["L0_attn_A_log"], inp["L0_attn_D"])

    x_slices = [np.ascontiguousarray(x[_tok_rows(c)].T) for c in range(NCORE)]
    l3 = run_l3(inp, x_slices, yg_full, r1m1)
    k_full = np.empty((DM, S), dtype=BF)
    v_full = np.empty((S, DM), dtype=BF)
    q_slices, x2_slices = [], []
    for core in range(NCORE):
        rows = _tok_rows(core)
        r = l3[core]
        k_full[:, rows] = r["k_bf"]
        v_full[rows] = r["v_bf"]
        q_slices.append(r["q_bf"])
        x2_slices.append(r["x2_fm"])

    l4 = run_l4(q_slices, k_full, v_full, x2_slices)
    x3_slices = [l4[c]["x3_fm"] for c in range(NCORE)]

    l5 = run_l5(inp, x3_slices)
    out = np.empty((S, DM), dtype=np.float32)
    for core in range(NCORE):
        out[_tok_rows(core)] = l5[core]["out_fm"].T
    return out.reshape(1, S, DM)


from concourse.bass_core_rust import IndirectOffsetOnAxis

I32 = mybir.dt.int32


def _chunk_loc(c):
    r = min(c, NCHUNK - 1 - c)
    return r, (0 if c < NCORE else CH)


def build_fused():
    nc = bass.Bass("TRN2", target_bir_lowering=False)
    TT = 512
    NTT = TPC // TT
    NT2 = S // TT
    DFF = 4096
    FSL = 512
    NFS = DFF // FSL
    KT = 1024

    x_in = nc.dram_tensor("x_fm", [DM, TPC], F32, kind="ExternalInput").ap()
    xh_in = nc.dram_tensor("x_halo", [DM, 8], F32, kind="ExternalInput").ap()
    wi_in = nc.dram_tensor("w_in_T", [DM, 2 * E], F32, kind="ExternalInput").ap()
    nw_in = nc.dram_tensor("nw_t", [128, 8], F32, kind="ExternalInput").ap()
    cw_in = nc.dram_tensor("conv_w_t", [128, 16, 4], F32, kind="ExternalInput").ap()
    cb_in = nc.dram_tensor("conv_b_t", [128, 16], F32, kind="ExternalInput").ap()
    xp_in = nc.dram_tensor("xproj_T", [E, 96], F32, kind="ExternalInput").ap()
    dtw_in = nc.dram_tensor("dtw_T", [RLOW, E], F32, kind="ExternalInput").ap()
    dtb_in = nc.dram_tensor("dtb_t", [128, 16], F32, kind="ExternalInput").ap()
    Al_in = nc.dram_tensor("A_log_t", [2, 128, NST], F32, kind="ExternalInput").ap()
    D_in = nc.dram_tensor("D_t", [128, 2], F32, kind="ExternalInput").ap()
    ow_in = nc.dram_tensor("outw_T", [E, DM], F32, kind="ExternalInput").ap()
    w0_in = nc.dram_tensor("w0_T", [DM, DM], F32, kind="ExternalInput").ap()
    al_in = nc.dram_tensor("alpha_rep", [128, 1], F32, kind="ExternalInput").ap()
    fb_in = nc.dram_tensor("fb_t", [128, 8], F32, kind="ExternalInput").ap()
    wq_in = nc.dram_tensor("wq_T", [DM, DM], F32, kind="ExternalInput").ap()
    wk_in = nc.dram_tensor("wk_T", [DM, DM], F32, kind="ExternalInput").ap()
    wv_in = nc.dram_tensor("wv_T", [DM, DM], F32, kind="ExternalInput").ap()
    mb_in = nc.dram_tensor("mask_big", [128, 1920], F32, kind="ExternalInput").ap()
    sel_in = nc.dram_tensor("sel_rep", [128, 32], F32, kind="ExternalInput").ap()
    binv_in = nc.dram_tensor("binv_rep", [128, 32], F32, kind="ExternalInput").ap()
    rw_in = nc.dram_tensor("router_w", [DM, 8], F32, kind="ExternalInput").ap()
    rb_in = nc.dram_tensor("rb_rep", [128, 8], F32, kind="ExternalInput").ap()
    w1_in = nc.dram_tensor("W1", [8, DM, DFF], F32, kind="ExternalInput").ap()
    b1_in = nc.dram_tensor("b1_t", [8, 128, 32], F32, kind="ExternalInput").ap()
    w2_in = nc.dram_tensor("W2", [8, DFF, DM], F32, kind="ExternalInput").ap()
    b2_in = nc.dram_tensor("b2", [8, DM], F32, kind="ExternalInput").ap()
    ones_in = nc.dram_tensor("ones_bf", [128, 128], BF16, kind="ExternalInput").ap()
    onesf_in = nc.dram_tensor("ones_f", [128, 128], F32, kind="ExternalInput").ap()
    idb_in = nc.dram_tensor("ident_bf", [128, 128], BF16, kind="ExternalInput").ap()
    idf_in = nc.dram_tensor("ident_f", [128, 128], F32, kind="ExternalInput").ap()
    idx0_in = nc.dram_tensor("idx0", [128, 1], I32, kind="ExternalInput").ap()
    dtws_in = nc.dram_tensor("dtw_sl", [RLOW, EPC], F32, kind="ExternalInput").ap()
    dtbs_in = nc.dram_tensor("dtb_sl", [128, 2], F32, kind="ExternalInput").ap()
    idxq_in = nc.dram_tensor("idxq", [128, 2], I32, kind="ExternalInput").ap()

    out_out = nc.dram_tensor("out_fm", [DM, TPC], F32, kind="ExternalOutput").ap()

    ag1_in = nc.dram_tensor("ag1_in", [2 * E, TPC], BF16).ap()
    ag1_out = nc.dram_tensor("ag1_out", [NCORE * 2 * E, TPC], BF16, addr_space="Shared").ap()
    ag2_in = nc.dram_tensor("ag2_in", [NTT * 96, TT], F32).ap()
    ag2_out = nc.dram_tensor("ag2_out", [NCORE * NTT * 96, TT], F32, addr_space="Shared").ap()
    ag3_in = nc.dram_tensor("ag3_in", [NCHUNK * EPC, CH], BF16).ap()
    ag3_out = nc.dram_tensor("ag3_out", [NCORE * NCHUNK * EPC, CH], BF16, addr_space="Shared").ap()
    ag4_in = nc.dram_tensor("ag4_in", [DM, TPC], BF16).ap()
    ag4_out = nc.dram_tensor("ag4_out", [NCORE * DM, TPC], BF16, addr_space="Shared").ap()
    ag5_in = nc.dram_tensor("ag5_in", [TPC, DM], BF16).ap()
    ag5_out = nc.dram_tensor("ag5_out", [NCORE * TPC, DM], BF16, addr_space="Shared").ap()
    r1m1_loc = nc.dram_tensor("r1m1_loc", [2, TPC], F32).ap()
    dt_loc = nc.dram_tensor("dt_loc", [EPC, S], BF16).ap()
    x2_loc = nc.dram_tensor("x2_loc", [DM, TPC], F32).ap()
    q_loc = nc.dram_tensor("q_loc", [DM, TPC], BF16).ap()
    x3_loc = nc.dram_tensor("x3_loc", [DM, TPC], F32).ap()
    cfm_dram = nc.dram_tensor("cfm_scr", [8, TPC], F32).ap()

    x3r = x_in.rearrange("(c p) t -> c p t", p=128)
    xh3 = xh_in.rearrange("(c p) t -> c p t", p=128)
    wi3 = wi_in.rearrange("(c p) e -> c p e", p=128)
    xp3 = xp_in.rearrange("(c p) f -> c p f", p=128)
    u3 = ag1_in[0:E, :].rearrange("(c p) t -> c p t", p=128)
    zs3 = ag1_in[E:2 * E, :].rearrange("(c p) t -> c p t", p=128)
    yg3d = ag3_in.rearrange("(k p) t -> k p t", p=128)
    ow3 = ow_in.rearrange("(c p) d -> c p d", p=128)
    w03 = w0_in.rearrange("(c p) d -> c p d", p=128)
    wq3 = wq_in.rearrange("(c p) d -> c p d", p=128)
    wk3 = wk_in.rearrange("(c p) d -> c p d", p=128)
    wv3 = wv_in.rearrange("(c p) d -> c p d", p=128)
    x23 = x2_loc.rearrange("(c p) t -> c p t", p=128)
    q3 = q_loc.rearrange("(c p) t -> c p t", p=128)
    k3w = ag4_in.rearrange("(c p) t -> c p t", p=128)
    v3w = ag5_in.rearrange("(tc p) d -> tc p d", p=128)
    k3r = ag4_out.rearrange("(r c p) t -> r c p t", p=128, c=8)
    v3r = ag5_out.rearrange("(r tc p) d -> r tc p d", p=128, tc=16)
    x33 = x3_loc.rearrange("(c p) t -> c p t", p=128)
    dtl3 = dt_loc.rearrange("(c p) t -> c p t", p=128)
    rw3 = rw_in.rearrange("(c p) e -> c p e", p=128)
    o3 = out_out.rearrange("(c p) t -> c p t", p=128)
    w1_3 = w1_in.rearrange("e (c p) f -> e c p f", p=128)
    w2_3 = w2_in.rearrange("e (c p) d -> e c p d", p=128)

    with contextlib.ExitStack() as es:
        def sb(name, shape, dt=F32, st_=None):
            return (st_ or es).enter_context(nc.sbuf_tensor(name, shape, dt))

        def psum(name, shape, dt=F32, st_=None):
            return (st_ or es).enter_context(nc.psum_tensor(name, shape, dt))

        ones_sb = sb("ones", [128, 128], BF16)
        onesf_sb = sb("onesf", [128, 128], F32)
        idb_sb = sb("identb", [128, 128], BF16)
        idf_sb = sb("identf", [128, 128], F32)
        ceps = sb("ceps", [128, 1])
        cm1 = sb("cm1", [128, 1])
        cone = sb("cone", [128, 1])
        cneg = sb("cneg", [128, 1])
        cn32 = sb("cn32", [128, 1])
        stat = {k: sb("st_" + k, [128, TT])
                for k in ["m", "q", "v", "r", "sr", "ar", "br", "t1", "t2"]}
        xb_scr2 = [sb(f"xb{i}", [128, TT], BF16) for i in range(2)]
        sq_scr2 = [sb(f"sqs{i}", [128, TT], BF16) for i in range(2)]
        xb_scr = xb_scr2[0]
        sq_scr = sq_scr2[0]
        idx0_sb = sb("idx0_sb", [128, 1], I32)
        idxq_sb = sb("idxq_sb", [128, 2], I32)

        ld = Ctr(es, nc, "ld")
        gld = Ctr(es, nc, "gld")
        pe = Ctr(es, nc, "pe")
        act = Ctr(es, nc, "act")
        dve = Ctr(es, nc, "dve")
        st = Ctr(es, nc, "st")
        cc = [es.enter_context(nc.semaphore(f"cc{i}")) for i in range(5)]
        sx = [Ctr(es, nc, f"sx{i}") for i in range(2)]
        su = [Ctr(es, nc, f"su{i}") for i in range(2)]
        sbc = [Ctr(es, nc, f"sbc{i}") for i in range(2)]
        sx3 = [Ctr(es, nc, f"sx3{i}") for i in range(2)]
        syg = [Ctr(es, nc, f"syg{i}") for i in range(2)]
        sr1 = Ctr(es, nc, "sr1")
        sq4 = Ctr(es, nc, "sq4")
        skv = [Ctr(es, nc, f"skv{i}") for i in range(2)]
        sw5 = [Ctr(es, nc, f"sw5{i}") for i in range(2)]
        scr5 = Ctr(es, nc, "scr5")

        SY, G, PE, A_, V = nc.sync, nc.gpsimd, nc.tensor, nc.scalar, nc.vector

        ld.inc(SY.dma_start(out=ones_sb[:, :], in_=ones_in[:, :]), 16)
        ld.inc(SY.dma_start(out=onesf_sb[:, :], in_=onesf_in[:, :]), 16)
        ld.inc(SY.dma_start(out=idb_sb[:, :], in_=idb_in[:, :]), 16)
        ld.inc(SY.dma_start(out=idf_sb[:, :], in_=idf_in[:, :]), 16)
        ld.inc(SY.dma_start(out=idx0_sb[:, :], in_=idx0_in[:, :]), 16)
        ld.inc(SY.dma_start(out=idxq_sb[:, :], in_=idxq_in[:, :]), 16)
        ld.wait(V)
        dve.inc(V.memset(ceps[:, :], EPS))
        dve.inc(V.memset(cm1[:, :], -1.0))
        dve.inc(V.memset(cone[:, :], 1.0))
        dve.inc(V.memset(cneg[:, :], -1e30))
        dve.inc(V.memset(cn32[:, :], -1.0 / 32.0))
        dve.wait(V); dve.wait(A_); dve.wait(PE)

        def ln_stats(src_tiles, pss, psq, want_rms=False):
            mm_done = [pe.n, pe.n]
            for c in range(8):
                cp = c % 2
                pe.wait(V, mm_done[cp])
                pe.wait(A_, mm_done[cp])
                dve.inc(V.tensor_copy(out=xb_scr2[cp][:, :], in_=src_tiles[c]))
                act.inc(A_.activation(out=sq_scr2[cp][:, :], in_=src_tiles[c], func=AF.Square))
                dve.wait(PE); act.wait(PE)
                pe.inc(PE.matmul(out=pss, lhsT=ones_sb[:, :], rhs=xb_scr2[cp][:, :],
                                 start=(c == 0), stop=(c == 7)))
                pe.inc(PE.matmul(out=psq, lhsT=ones_sb[:, :], rhs=sq_scr2[cp][:, :],
                                 start=(c == 0), stop=(c == 7)))
                mm_done[cp] = pe.n
            pe.wait(A_); pe.wait(V)
            _ln_stats_math(nc, dve, act, stat, pss, psq, ceps[:, :],
                           want_rms=want_rms)

        with contextlib.ExitStack() as e1:
            x_sb = [[sb(f"p1x{p}_{c}", [128, TT], st_=e1) for c in range(8)]
                    for p in range(2)]
            w_bf = [sb(f"p1w{c}", [128, 2 * E], BF16, e1) for c in range(8)]
            nw_sb = sb("p1nw", [128, 8], st_=e1)
            cw_sb = sb("p1cw", [128, 16, 4], st_=e1)
            cb_sb = sb("p1cb", [128, 16], st_=e1)
            xp_bf = [sb(f"p1xp{c}", [128, 96], BF16, e1) for c in range(16)]
            xin_bf = [sb(f"p1xin{c}", [128, TPC], BF16, e1) for c in range(8)]
            ur = [sb(f"p1ur{p}", [128, TT + 3], st_=e1) for p in range(2)]
            halo = sb("p1halo", [128, 16, 3], st_=e1)
            halo2 = sb("p1halo2", [128, 16, 3], st_=e1)
            xh_sb = [sb(f"p1xh{c}", [128, 8], st_=e1) for c in range(8)]
            xinh_bf = [sb(f"p1xinh{c}", [128, 8], BF16, e1) for c in range(8)]
            uh_scr = sb("p1uh", [128, 8], st_=e1)
            cacc = [sb(f"p1cacc{p}", [128, TT], st_=e1) for p in range(2)]
            ubf_sb = [sb(f"p1ubf{p}", [128, TT], BF16, e1) for p in range(2)]
            zbf_sb = [sb(f"p1zbf{p}", [128, TT], BF16, e1) for p in range(2)]
            xdbl_sb = sb("p1xdbl", [96, TT], st_=e1)
            ps_u = [psum(f"p1ps_u{p}", [128, TT], st_=e1) for p in range(2)]
            ps_x = psum("p1ps_x", [96, TT], st_=e1)
            ps_s = psum("p1ps_s", [128, TT], st_=e1)
            ps_q = psum("p1ps_q", [128, TT], st_=e1)

            ld.inc(SY.dma_start(out=nw_sb[:, :], in_=nw_in[:, :]), 16)
            ld.inc(SY.dma_start(out=cw_sb[:, :, :], in_=cw_in[:, :, :]), 16)
            ld.inc(SY.dma_start(out=cb_sb[:, :], in_=cb_in[:, :]), 16)
            for c in range(8):
                ld.inc(SY.dma_start(out=xh_sb[c][:, :], in_=xh3[c]), 16)
            for c in range(16):
                gld.inc(G.dma_start(out=xp_bf[c][:, :], in_=xp3[c]), 16)
            for c in range(8):
                gld.inc(G.dma_start(out=w_bf[c][:, :], in_=wi3[c]), 16)
            ld.wait(V)
            gld.wait(V)
            for c in range(8):
                dve.inc(V.tensor_scalar_mul(out=w_bf[c][:, :], in0=w_bf[c][:, :],
                                            scalar1=nw_sb[:, c:c + 1]))
            dve.inc(V.memset(halo[:, :, :], 0.0))
            dve.wait(V)

            sh = {k: stat[k][:, 0:8] for k in stat}
            ld.wait(V); ld.wait(A_); ld.wait(PE)
            for c in range(8):
                pe.wait(V); pe.wait(A_)
                dve.inc(V.tensor_copy(out=xb_scr[:, 0:8], in_=xh_sb[c][:, :]))
                act.inc(A_.activation(out=sq_scr[:, 0:8], in_=xh_sb[c][:, :], func=AF.Square))
                dve.wait(PE); act.wait(PE)
                pe.inc(PE.matmul(out=ps_s[:, 0:8], lhsT=ones_sb[:, :], rhs=xb_scr[:, 0:8],
                                 start=(c == 0), stop=(c == 7)))
                pe.inc(PE.matmul(out=ps_q[:, 0:8], lhsT=ones_sb[:, :], rhs=sq_scr[:, 0:8],
                                 start=(c == 0), stop=(c == 7)))
            pe.wait(A_); pe.wait(V)
            _ln_stats_math(nc, dve, act, sh, ps_s[:, 0:8], ps_q[:, 0:8], ceps[:, :],
                           want_rms=True)
            for c in range(8):
                dve.inc(V.tensor_mul(out=sh["t1"], in0=xh_sb[c][:, :], in1=sh["ar"]))
                dve.wait(V)
                dve.inc(V.tensor_sub(out=xinh_bf[c][:, :], in0=sh["t1"], in1=sh["br"]))
                dve.wait(V)
            nc.all_engine_barrier()
            for eo in range(16):
                for c in range(8):
                    pe.inc(PE.matmul(out=ps_u[0][:, 0:8],
                                     lhsT=w_bf[c][:, eo * 128:(eo + 1) * 128],
                                     rhs=xinh_bf[c][:, :], start=(c == 0), stop=(c == 7)))
                pe.wait(A_)
                act.inc(A_.activation(out=uh_scr[:, :], in_=ps_u[0][:, 0:8], func=AF.Identity))
                act.wait(V)
                dve.inc(V.tensor_copy(out=halo[:, eo, :], in_=uh_scr[:, 1:4]))
                dve.inc(V.tensor_copy(out=halo2[:, eo, :], in_=uh_scr[:, 5:8]))
                dve.wait(PE)
            nc.all_engine_barrier()

            xload_done = [0, 0]
            xuse_done = [0, 0]
            for t in range(NTT):
                sl = slice(t * TT, (t + 1) * TT)
                p = t % 2
                dve.wait(SY, xuse_done[p])
                for c in range(8):
                    sx[p].inc(SY.dma_start(out=x_sb[p][c][:, :], in_=x3r[c][:, sl]), 16)
                for c in range(8):
                    pe.wait(V)
                    pe.wait(A_)
                    sx[p].wait(V)
                    sx[p].wait(A_)
                    dve.inc(V.tensor_copy(out=xb_scr[:, :], in_=x_sb[p][c][:, :]))
                    act.inc(A_.activation(out=sq_scr[:, :], in_=x_sb[p][c][:, :], func=AF.Square))
                    dve.wait(PE)
                    act.wait(PE)
                    pe.inc(PE.matmul(out=ps_s[:, :], lhsT=ones_sb[:, :], rhs=xb_scr[:, :],
                                     start=(c == 0), stop=(c == 7)))
                    pe.inc(PE.matmul(out=ps_q[:, :], lhsT=ones_sb[:, :], rhs=sq_scr[:, :],
                                     start=(c == 0), stop=(c == 7)))
                pe.wait(A_)
                pe.wait(V)
                _ln_stats_math(nc, dve, act, stat, ps_s[:, :], ps_q[:, :], ceps[:, :],
                               want_rms=True)
                dve.wait(SY)
                act.wait(SY)
                st.inc(SY.dma_start(out=r1m1_loc[0:1, sl], in_=stat["r"][0:1, :]), 16)
                st.inc(SY.dma_start(out=r1m1_loc[1:2, sl], in_=stat["m"][0:1, :]), 16)
                for c in range(8):
                    dve.inc(V.tensor_mul(out=stat["t1"][:, :], in0=x_sb[p][c][:, :],
                                         in1=stat["ar"][:, :]))
                    dve.inc(V.tensor_sub(out=xin_bf[c][:, sl], in0=stat["t1"][:, :],
                                         in1=stat["br"][:, :]))
                xuse_done[p] = dve.n
                st.wait(A_)
                st.wait(V)
            nc.all_engine_barrier()

            evac_done = [0, 0]
            store_done = {}
            silu_done = [0, 0]
            xproj_done = [0, 0]
            for t in range(NTT):
                sl = slice(t * TT, (t + 1) * TT)
                pend_xproj = None
                for eo in range(32):
                    p = eo % 2
                    act.wait(PE, evac_done[p])
                    for c in range(8):
                        pe.inc(PE.matmul(out=ps_u[p][:, :],
                                         lhsT=w_bf[c][:, eo * 128:(eo + 1) * 128],
                                         rhs=xin_bf[c][:, sl],
                                         start=(c == 0), stop=(c == 7)))
                    if pend_xproj is not None:
                        peo = pend_xproj
                        pp2 = peo % 2
                        act.wait(PE, silu_done[pp2])
                        pe.inc(PE.matmul(out=ps_x[:, :], lhsT=xp_bf[peo][:, :],
                                         rhs=ubf_sb[pp2][:, :],
                                         start=(peo == 0), stop=(peo == 15)))
                        xproj_done[pp2] = pe.n
                        pend_xproj = None
                    if eo < 16:
                        pe.wait(A_)
                        dve.wait(A_)
                        act.inc(A_.activation(out=ur[p][:, 3:3 + TT], in_=ps_u[p][:, :],
                                              func=AF.Identity))
                        evac_done[p] = act.n
                        act.wait(V)
                        hsrc = halo2 if t == 2 else halo
                        dve.inc(V.tensor_copy(out=ur[p][:, 0:3], in_=hsrc[:, eo, :]))
                        dve.wait(V)
                        dve.inc(V.tensor_scalar_mul(out=cacc[p][:, :], in0=ur[p][:, 0:TT],
                                                    scalar1=cw_sb[:, eo, 0:1]))
                        for j in range(1, 4):
                            dve.inc(V.scalar_tensor_tensor(
                                out=cacc[p][:, :], in0=ur[p][:, j:j + TT],
                                scalar=cw_sb[:, eo, j:j + 1], in1=cacc[p][:, :],
                                op0=ALU.mult, op1=ALU.add))
                        dve.inc(V.tensor_copy(out=halo[:, eo, :], in_=ur[p][:, TT:TT + 3]))
                        key = ("u", p)
                        st.wait(A_, store_done.get(key, 0))
                        pe.wait(A_, xproj_done[p])
                        dve.wait(A_)
                        act.inc(A_.activation(out=ubf_sb[p][:, :], in_=cacc[p][:, :],
                                              func=AF.Silu, bias=cb_sb[:, eo:eo + 1]))
                        silu_done[p] = act.n
                        act.wait(SY)
                        st.inc(SY.dma_start(out=u3[eo][:, sl], in_=ubf_sb[p][:, :]), 16)
                        store_done[key] = st.n
                        pend_xproj = eo
                    else:
                        if pend_xproj is not None:
                            pass
                        key = ("z", p)
                        st.wait(A_, store_done.get(key, 0))
                        pe.wait(A_)
                        act.inc(A_.activation(out=zbf_sb[p][:, :], in_=ps_u[p][:, :],
                                              func=AF.Silu))
                        evac_done[p] = act.n
                        act.wait(SY)
                        st.inc(SY.dma_start(out=zs3[eo - 16][:, sl], in_=zbf_sb[p][:, :]), 16)
                        store_done[key] = st.n
                pe.wait(A_)
                act.inc(A_.activation(out=xdbl_sb[:, :], in_=ps_x[:, :], func=AF.Identity))
                act.wait(SY)
                st.inc(SY.dma_start(out=ag2_in[t * 96:(t + 1) * 96, :], in_=xdbl_sb[:, :]), 16)
                st.wait(A_)
            st.wait(SY)
            st.wait(G)
            nc.all_engine_barrier()

        G.collective_compute("AllGather", ALU.bypass, replica_groups=[list(range(NCORE))],
                             ins=[ag2_in.opt()], outs=[ag2_out.opt()]).then_inc(cc[1])
        G.collective_compute("AllGather", ALU.bypass, replica_groups=[list(range(NCORE))],
                             ins=[ag1_in.opt()], outs=[ag1_out.opt()]).then_inc(cc[0])
        G.wait_ge(cc[1], 1)
        nc.all_engine_barrier()

        with contextlib.ExitStack() as e2s:
            A_t = [sb(f"p2A{e}", [128, NST], st_=e2s) for e in range(2)]
            A_scr = sb("p2Ascr", [128, NST], st_=e2s)
            D_sb = sb("p2D", [128, 2], st_=e2s)
            hc = [sb(f"p2hc{e}", [128, NST], BF16, e2s) for e in range(2)]
            dt_sb = [sb(f"p2dt{p}", [128, TT], BF16, e2s) for p in range(2)]
            u_sb = [sb(f"p2u{p}", [128, TT], BF16, e2s) for p in range(2)]
            zs_sb = [sb(f"p2zs{p}", [128, TT], BF16, e2s) for p in range(2)]
            Br = [sb(f"p2Br{p}", [128, NST, TT], BF16, e2s) for p in range(2)]
            Cr = [sb(f"p2Cr{p}", [128, NST, TT], BF16, e2s) for p in range(2)]
            dtu = sb("p2dtu", [128, TT], BF16, e2s)
            dA = [sb(f"p2dA{p}", [128, NST, TT], BF16, e2s) for p in range(2)]
            bt = sb("p2bt", [128, NST, TT], BF16, e2s)
            dtw_bf = sb("p2dtw", [RLOW, EPC], BF16, e2s)
            dtb_sb = sb("p2dtb", [128, 2], st_=e2s)
            dtlow_bf = [sb(f"p2dtlow{p}", [RLOW, TT], BF16, e2s) for p in range(2)]
            sp_s1 = sb("p2sps1", [128, TT], st_=e2s)
            sp_ax = sb("p2spax", [128, TT], st_=e2s)
            sp_ln = sb("p2spln", [128, TT], st_=e2s)
            ps_dt = [psum(f"p2ps_dt{p}", [128, TT], st_=e2s) for p in range(2)]
            ht = sb("p2ht", [128, NST, TT], BF16, e2s)
            tr8 = sb("p2tr8", [128, 8, TT], BF16, e2s)
            tr4 = sb("p2tr4", [128, 4, TT], BF16, e2s)
            tr2 = sb("p2tr2", [128, 2, TT], BF16, e2s)
            y_sb = sb("p2y", [128, TT], st_=e2s)
            yd_sb = sb("p2yd", [128, TT], BF16, e2s)
            yg_sb = [sb(f"p2yg{p}", [128, TT], BF16, e2s) for p in range(2)]

            ld.inc(SY.dma_start(out=D_sb[:, :], in_=D_in[:, :]), 16)
            ld.inc(SY.dma_start(out=dtb_sb[:, :], in_=dtbs_in[:, :]), 16)
            gld.inc(G.dma_start(out=dtw_bf[:, :], in_=dtws_in[:, :]), 16)
            for e in range(2):
                ld.inc(SY.dma_start(out=A_t[e][:, :], in_=Al_in[e]), 16)
            ld.wait(V)
            ld.wait(A_)
            gld.wait(PE)
            for e in range(2):
                dve.inc(V.memset(hc[e][:, :], 0.0))
            dve.wait(A_)
            for e in range(2):
                act.inc(A_.activation(out=A_scr[:, :], in_=A_t[e][:, :], func=AF.Exp))
                act.wait(V)
                dve.wait(V)
                dve.inc(V.tensor_scalar_mul(out=A_t[e][:, :], in0=A_scr[:, :],
                                            scalar1=cm1[:, :]))
                dve.wait(V)
                dve.wait(A_)

            dtlow_pe_done = [0, 0]
            dt_st = {}
            for tt in range(NT2):
                pb = tt % 2
                gc = tt // 2
                h2 = tt % 2
                rk, coff = _chunk_loc(gc)
                tb = (0 if gc < NCORE else 2) + h2
                rowb2 = rk * NTT * 96 + tb * 96
                pe.wait(G, dtlow_pe_done[pb])
                sbc[pb].inc(G.dma_start(out=dtlow_bf[pb][:, :],
                                        in_=ag2_out[rowb2:rowb2 + 64, :]), 16)
                for ec in range(2):
                    p = (2 * tt + ec) % 2
                    sbc[pb].wait(PE)
                    act.wait(PE, dt_st.get(("ps", p), 0))
                    pe.inc(PE.matmul(out=ps_dt[p][:, :],
                                     lhsT=dtw_bf[:, ec * 128:(ec + 1) * 128],
                                     rhs=dtlow_bf[pb][:, :], start=True, stop=True))
                    dtlow_pe_done[pb] = pe.n
                    pe.wait(A_)
                    dve.wait(A_)
                    act.inc(A_.activation(out=sp_s1[:, :], in_=ps_dt[p][:, :],
                                          func=AF.Identity, bias=dtb_sb[:, ec:ec + 1]))
                    dt_st[("ps", p)] = act.n
                    act.inc(A_.activation(out=sp_ax[:, :], in_=sp_s1[:, :], func=AF.Abs))
                    act.inc(A_.activation(out=sp_ax[:, :], in_=sp_ax[:, :], func=AF.Exp,
                                          scale=-1.0))
                    act.inc(A_.activation(out=sp_ln[:, :], in_=sp_ax[:, :], func=AF.Ln,
                                          bias=cone[:, :]))
                    act.inc(A_.activation(out=sp_ax[:, :], in_=sp_s1[:, :], func=AF.Relu))
                    st.wait(V, dt_st.get(("buf", p), 0))
                    act.wait(V)
                    dve.inc(V.tensor_add(out=dt_sb[p][:, :], in0=sp_ax[:, :],
                                         in1=sp_ln[:, :]))
                    dve.wait(SY)
                    st.inc(SY.dma_start(out=dtl3[ec][:, tt * TT:(tt + 1) * TT],
                                        in_=dt_sb[p][:, :]), 16)
                    dt_st[("buf", p)] = st.n
            st.wait(G)
            G.wait_ge(cc[0], 1)
            nc.all_engine_barrier()

            gld_done = [0, 0]
            bcld_done = [0, 0]
            use_done = [0, 0]
            bc_use_done = [0, 0]
            stp_done = [0, 0]
            dA_read_done = [0, 0]
            dA_wr_done = [0, 0]
            dt_done = [0, 0]
            for tt in range(NT2):
                pb = tt % 2
                gc = tt // 2
                h2 = tt % 2
                rk, coff = _chunk_loc(gc)
                colofs = coff + h2 * TT
                tb = (0 if gc < NCORE else 2) + h2
                rowb2 = rk * NTT * 96 + tb * 96
                dve.wait(G, bc_use_done[pb])
                sbc[pb].inc(G.dma_start(out=Br[pb][:, :, :],
                                    in_=ag2_out[rowb2 + 64:rowb2 + 80, :]
                                    .unsqueeze(0).broadcast_to([128, NST, TT])), 16)
                sbc[pb].inc(G.dma_start(out=Cr[pb][:, :, :],
                                    in_=ag2_out[rowb2 + 80:rowb2 + 96, :]
                                    .unsqueeze(0).broadcast_to([128, NST, TT])), 16)
                for ec in range(2):
                    p = (2 * tt + ec) % 2
                    dve.wait(G, use_done[p])
                    act.wait(G, dt_done[p])
                    for ti, tofs in ((0, 0), (1, E)):
                        dst = (u_sb if ti == 0 else zs_sb)[p]
                        eo_elem = ((rk * 2 * E + tofs + ec * 128) * TPC + colofs)
                        su[p].inc(G.indirect_dma_start(
                            out=dst[:, :], out_offset=None,
                            in_=ag1_out[:, :],
                            in_offset=IndirectOffsetOnAxis(idx0_sb[:, 0:1], axis=0),
                            element_offset=eo_elem), 16)
                    su[p].inc(G.dma_start(out=dt_sb[p][:, :],
                                          in_=dtl3[ec][:, tt * TT:(tt + 1) * TT]), 16)
                    su[p].wait(V)
                    dve.inc(V.tensor_mul(out=dtu[:, :], in0=dt_sb[p][:, :], in1=u_sb[p][:, :]))
                    su[p].wait(A_)
                    dve.wait(A_, dA_read_done[p])
                    for n in range(NST):
                        act.inc(A_.activation(out=dA[p][:, n, :], in_=dt_sb[p][:, :],
                                              func=AF.Exp, scale=A_t[ec][:, n:n + 1]))
                    dA_wr_done[p] = act.n
                    dt_done[p] = act.n
                    sbc[pb].wait(V)
                    dve.inc(V.tensor_mul(out=bt[:, :, :],
                                         in0=dtu[:, :].unsqueeze(1).broadcast_to([128, NST, TT]),
                                         in1=Br[pb][:, :, :]))
                    act.wait(V, dA_wr_done[p])
                    for n in range(NST):
                        dve.inc(V.tensor_tensor_scan(out=ht[:, n, :], data0=dA[p][:, n, :],
                                                     data1=bt[:, n, :],
                                                     initial=hc[ec][:, n:n + 1],
                                                     op0=ALU.mult, op1=ALU.add))
                    dve.wait(V)
                    dve.inc(V.tensor_copy(out=hc[ec][:, :], in_=ht[:, :, TT - 1:TT]))
                    dve.wait(V)
                    dve.inc(V.tensor_mul(out=dA[p][:, :, :], in0=ht[:, :, :], in1=Cr[pb][:, :, :]))
                    bc_use_done[pb] = dve.n
                    dve.inc(V.tensor_add(out=tr8[:, :, :], in0=dA[p][:, 0:8, :], in1=dA[p][:, 8:16, :]))
                    dA_read_done[p] = dve.n
                    dve.inc(V.tensor_add(out=tr4[:, :, :], in0=tr8[:, 0:4, :], in1=tr8[:, 4:8, :]))
                    dve.inc(V.tensor_add(out=tr2[:, :, :], in0=tr4[:, 0:2, :], in1=tr4[:, 2:4, :]))
                    dve.inc(V.tensor_add(out=y_sb[:, :], in0=tr2[:, 0, :], in1=tr2[:, 1, :]))
                    dve.inc(V.scalar_tensor_tensor(out=yd_sb[:, :], in0=u_sb[p][:, :],
                                                   scalar=D_sb[:, ec:ec + 1], in1=y_sb[:, :],
                                                   op0=ALU.mult, op1=ALU.add))
                    st.wait(V, stp_done[p])
                    dve.inc(V.tensor_mul(out=yg_sb[p][:, :], in0=yd_sb[:, :], in1=zs_sb[p][:, :]))
                    use_done[p] = dve.n
                    dve.wait(SY)
                    st.inc(SY.dma_start(out=yg3d[gc * 2 + ec][:, h2 * TT:(h2 + 1) * TT],
                                        in_=yg_sb[p][:, :]), 16)
                    stp_done[p] = st.n
            st.wait(SY)
            st.wait(G)
            nc.all_engine_barrier()

        G.collective_compute("AllGather", ALU.bypass, replica_groups=[list(range(NCORE))],
                             ins=[ag3_in.opt()], outs=[ag3_out.opt()]).then_inc(cc[2])
        nc.all_engine_barrier()

        with contextlib.ExitStack() as e3:
            x_sb = [[sb(f"p3x{p}_{c}", [128, TT], st_=e3) for c in range(8)]
                    for p in range(2)]
            ps_a = [psum(f"p3ps_a{p}", [128, TT], st_=e3) for p in range(2)]
            ps_s = psum("p3ps_s", [128, TT], st_=e3)
            ps_q = psum("p3ps_q", [128, TT], st_=e3)

            with contextlib.ExitStack() as e3a:
                ow_bf = [sb(f"p3ow{c}", [128, DM], BF16, e3a) for c in range(16)]
                tern = [sb(f"p3tern{c}", [128, DM], BF16, e3a) for c in range(8)]
                w0f = sb("p3w0f", [128, DM], F32, e3a)
                scrA = sb("p3scrA", [128, DM], F32, e3a)
                scrB = sb("p3scrB", [128, DM], F32, e3a)
                asum = sb("p3asum", [128, 16], st_=e3a)
                delta = sb("p3delta", [128, 1], st_=e3a)
                ndelta = sb("p3ndelta", [128, 1], st_=e3a)
                al_sb = sb("p3al", [128, 1], st_=e3a)
                fb_sb = sb("p3fb", [128, 8], st_=e3a)
                yg_sb = [[sb(f"p3yg{p}_{c}", [128, TT], BF16, e3a) for c in range(16)]
                         for p in range(2)]
                r1_sb = sb("p3r1s", [128, TT], st_=e3a)
                m1_sb = sb("p3m1s", [128, TT], st_=e3a)
                tmp_sb = [sb(f"p3tmp{c}", [128, TT], st_=e3a) for c in range(8)]
                fo_sb = [sb(f"p3fo{c}", [128, TT], st_=e3a) for c in range(8)]
                xln_bf = [sb(f"p3xln{c}", [128, TT], BF16, e3a) for c in range(8)]
                evac = sb("p3evac", [128, TT], st_=e3a)
                ps_d = psum("p3ps_d", [128, 16], st_=e3a)

                G.wait_ge(cc[2], 1)
                ld.inc(SY.dma_start(out=al_sb[:, :], in_=al_in[:, :]), 16)
                ld.inc(SY.dma_start(out=fb_sb[:, :], in_=fb_in[:, :]), 16)
                dve.inc(V.memset(asum[:, :], 0.0))
                dve.wait(V)
                for c in range(16):
                    gld.inc(G.dma_start(out=ow_bf[c][:, :], in_=ow3[c]), 16)
                for c in range(8):
                    ld.inc(SY.dma_start(out=w0f[:, :], in_=w03[c]), 16)
                    ld.wait(A_)
                    act.inc(A_.activation(out=scrA[:, :], in_=w0f[:, :], func=AF.Abs))
                    act.wait(V)
                    dve.inc(V.tensor_reduce(out=asum[:, c:c + 1], in_=scrA[:, :],
                                            axis=mybir.AxisListType.X, op=ALU.add))
                    dve.wait(V)
                    dve.wait(SY); act.wait(SY)
                pe.wait(V)
                dve.wait(PE)
                pe.inc(PE.matmul(out=ps_d[:, :], lhsT=onesf_sb[:, :], rhs=asum[:, :16],
                                 start=True, stop=True))
                pe.wait(V)
                dve.inc(V.tensor_reduce(out=delta[:, :], in_=ps_d[:, 0:8],
                                        axis=mybir.AxisListType.X, op=ALU.add))
                dve.wait(V); dve.wait(A_)
                act.inc(A_.activation(out=delta[:, :], in_=delta[:, :], func=AF.Identity,
                                      scale=0.7 / (DM * DM)))
                act.wait(V)
                dve.inc(V.tensor_scalar_mul(out=ndelta[:, :], in0=delta[:, :],
                                            scalar1=cm1[:, :]))
                dve.wait(V)
                for c in range(8):
                    dve.wait(SY)
                    ld.inc(SY.dma_start(out=w0f[:, :], in_=w03[c]), 16)
                    ld.wait(V)
                    dve.inc(V.tensor_scalar(out=scrA[:, :], in0=w0f[:, :],
                                            scalar1=delta[:, 0:1], scalar2=None,
                                            op0=ALU.is_gt))
                    dve.inc(V.tensor_scalar(out=scrB[:, :], in0=w0f[:, :],
                                            scalar1=ndelta[:, 0:1], scalar2=None,
                                            op0=ALU.is_lt))
                    dve.wait(V)
                    dve.inc(V.tensor_sub(out=tern[c][:, :], in0=scrA[:, :], in1=scrB[:, :]))
                    dve.wait(V)
                gld.wait(PE)
                ld.wait(PE)
                ld.wait(V)

                yld_done = [0, 0]
                yuse_done = [0, 0]
                evac_done = [0, 0]
                r1use_done = 0
                x2store_done = [0, 0]
                for t in range(NTT):
                    sl = slice(t * TT, (t + 1) * TT)
                    p = t % 2
                    qc = t // 2
                    h2 = t % 2
                    dve.wait(SY, yuse_done[p])
                    st.wait(SY, x2store_done[p])
                    for c in range(8):
                        sx3[p].inc(SY.dma_start(out=x_sb[p][c][:, :], in_=x3r[c][:, sl]), 16)
                    dve.wait(G, yuse_done[p])
                    for ec16 in range(16):
                        r = ec16 // 2
                        ecl = ec16 % 2
                        eo_elem = (r * NCHUNK * EPC + ecl * 128) * CH + h2 * TT
                        syg[p].inc(G.indirect_dma_start(
                            out=yg_sb[p][ec16][:, :], out_offset=None,
                            in_=ag3_out[:, :],
                            in_offset=IndirectOffsetOnAxis(idxq_sb[:, qc:qc + 1], axis=0),
                            element_offset=eo_elem), 16)
                    dve.wait(G, r1use_done)
                    sr1.inc(G.dma_start(out=r1_sb[:, :],
                                        in_=r1m1_loc[0:1, sl].broadcast_to([128, TT])), 16)
                    sr1.inc(G.dma_start(out=m1_sb[:, :],
                                        in_=r1m1_loc[1:2, sl].broadcast_to([128, TT])), 16)
                    syg[p].wait(V)
                    sr1.wait(V)
                    sx3[p].wait(V)
                    for c in range(8):
                        dve.inc(V.tensor_sub(out=tmp_sb[c][:, :], in0=x_sb[p][c][:, :],
                                             in1=m1_sb[:, :]))
                        dve.inc(V.tensor_mul(out=tmp_sb[c][:, :], in0=tmp_sb[c][:, :],
                                             in1=r1_sb[:, :]))
                    r1use_done = dve.n
                    for do in range(8):
                        pp = do % 2
                        act.wait(PE, evac_done[pp])
                        syg[p].wait(PE)
                        for ec in range(16):
                            pe.inc(PE.matmul(out=ps_a[pp][:, :],
                                             lhsT=ow_bf[ec][:, do * 128:(do + 1) * 128],
                                             rhs=yg_sb[p][ec][:, :],
                                             start=(ec == 0), stop=(ec == 15)))
                        pe.wait(A_)
                        dve.wait(A_)
                        act.inc(A_.activation(out=evac[:, :], in_=ps_a[pp][:, :],
                                              func=AF.Identity))
                        evac_done[pp] = act.n
                        act.wait(V)
                        dve.inc(V.tensor_add(out=tmp_sb[do][:, :], in0=tmp_sb[do][:, :],
                                             in1=evac[:, :]))
                    yuse_done[p] = dve.n
                    ln_stats([tmp_sb[c][:, :] for c in range(8)], ps_s[:, :], ps_q[:, :])
                    for c in range(8):
                        dve.inc(V.tensor_mul(out=stat["t1"][:, :], in0=tmp_sb[c][:, :],
                                             in1=stat["ar"][:, :]))
                        dve.inc(V.tensor_sub(out=stat["t1"][:, :], in0=stat["t1"][:, :],
                                             in1=stat["br"][:, :]))
                        dve.inc(V.tensor_add(out=x_sb[p][c][:, :], in0=x_sb[p][c][:, :],
                                             in1=stat["t1"][:, :]))
                    ln_stats([x_sb[p][c][:, :] for c in range(8)], ps_s[:, :], ps_q[:, :])
                    for c in range(8):
                        dve.inc(V.tensor_mul(out=stat["t1"][:, :], in0=x_sb[p][c][:, :],
                                             in1=stat["ar"][:, :]))
                        dve.inc(V.tensor_sub(out=xln_bf[c][:, :], in0=stat["t1"][:, :],
                                             in1=stat["br"][:, :]))
                    for do in range(8):
                        pp = do % 2
                        act.wait(PE, evac_done[pp])
                        dve.wait(PE)
                        for c in range(8):
                            pe.inc(PE.matmul(out=ps_a[pp][:, :],
                                             lhsT=tern[c][:, do * 128:(do + 1) * 128],
                                             rhs=xln_bf[c][:, :],
                                             start=(c == 0), stop=(c == 7)))
                        pe.wait(A_)
                        act.inc(A_.activation(out=fo_sb[do][:, :], in_=ps_a[pp][:, :],
                                              func=AF.Identity, bias=fb_sb[:, do:do + 1],
                                              scale=al_sb[:, :]))
                        evac_done[pp] = act.n
                    act.wait(V)
                    ln_stats([fo_sb[c][:, :] for c in range(8)], ps_s[:, :], ps_q[:, :])
                    st.wait(V)
                    for c in range(8):
                        dve.inc(V.tensor_mul(out=stat["t1"][:, :], in0=fo_sb[c][:, :],
                                             in1=stat["ar"][:, :]))
                        dve.inc(V.tensor_sub(out=stat["t1"][:, :], in0=stat["t1"][:, :],
                                             in1=stat["br"][:, :]))
                        dve.inc(V.tensor_add(out=x_sb[p][c][:, :], in0=x_sb[p][c][:, :],
                                             in1=stat["t1"][:, :]))
                        dve.wait(SY)
                        st.inc(SY.dma_start(out=x23[c][:, sl], in_=x_sb[p][c][:, :]), 16)
                    x2store_done[p] = st.n
                st.wait(SY)
                nc.all_engine_barrier()

            with contextlib.ExitStack() as e3b:
                wq_bf = [sb(f"p3wqb{c}", [128, DM], BF16, e3b) for c in range(8)]
                wk_bf = [sb(f"p3wkb{c}", [128, DM], BF16, e3b) for c in range(8)]
                wv_bf = [sb(f"p3wvb{c}", [128, DM], BF16, e3b) for c in range(8)]
                xln_bf = [sb(f"p3xl{c}", [128, TT], BF16, e3b) for c in range(8)]
                qkv_scr = [sb(f"p3qkvs{p}", [128, TT], BF16, e3b) for p in range(2)]
                v_scr = [sb(f"p3vs{p}", [128, DM], BF16, e3b) for p in range(2)]
                ps_b = [psum(f"p3ps_b{p}", [128, TT], st_=e3b) for p in range(2)]

                for c in range(8):
                    gld.inc(G.dma_start(out=wq_bf[c][:, :], in_=wq3[c]), 16)
                    gld.inc(G.dma_start(out=wk_bf[c][:, :], in_=wk3[c]), 16)
                    gld.inc(G.dma_start(out=wv_bf[c][:, :], in_=wv3[c]), 16)
                gld.wait(PE)
                evac_done = [0, 0]
                store_done = {}
                for t in range(NTT):
                    sl = slice(t * TT, (t + 1) * TT)
                    p = t % 2
                    dve.wait(SY)
                    for c in range(8):
                        ld.inc(SY.dma_start(out=x_sb[p][c][:, :], in_=x23[c][:, sl]), 16)
                    ld.wait(V); ld.wait(A_); ld.wait(PE)
                    ln_stats([x_sb[p][c][:, :] for c in range(8)], ps_s[:, :], ps_q[:, :])
                    for c in range(8):
                        dve.inc(V.tensor_mul(out=stat["t1"][:, :], in0=x_sb[p][c][:, :],
                                             in1=stat["ar"][:, :]))
                        dve.inc(V.tensor_sub(out=xln_bf[c][:, :], in0=stat["t1"][:, :],
                                             in1=stat["br"][:, :]))
                    dve.wait(PE)
                    for wname, wtiles, out3w in (("q", wq_bf, q3), ("k", wk_bf, k3w)):
                        for do in range(8):
                            pp = do % 2
                            act.wait(PE, evac_done[pp])
                            for c in range(8):
                                pe.inc(PE.matmul(out=ps_b[pp][:, :],
                                                 lhsT=wtiles[c][:, do * 128:(do + 1) * 128],
                                                 rhs=xln_bf[c][:, :],
                                                 start=(c == 0), stop=(c == 7)))
                            pe.wait(A_)
                            key = (wname, pp)
                            st.wait(A_, store_done.get(key, 0))
                            act.inc(A_.activation(out=qkv_scr[pp][:, :], in_=ps_b[pp][:, :],
                                                  func=AF.Identity))
                            evac_done[pp] = act.n
                            act.wait(SY)
                            st.inc(SY.dma_start(out=out3w[do][:, sl], in_=qkv_scr[pp][:, :]), 16)
                            store_done[key] = st.n
                    for tq in range(TT // 128):
                        pp = tq % 2
                        tc_global = (t * TT + tq * 128) // 128
                        for dv in range(2):
                            act.wait(PE, evac_done[pp])
                            for c in range(8):
                                pe.inc(PE.matmul(
                                    out=ps_b[pp][:, :],
                                    lhsT=xln_bf[c][:, tq * 128:(tq + 1) * 128],
                                    rhs=wv_bf[c][:, dv * 512:(dv + 1) * 512],
                                    start=(c == 0), stop=(c == 7)))
                            pe.wait(A_)
                            key = ("v", pp, dv)
                            st.wait(A_, store_done.get(key, 0))
                            act.inc(A_.activation(out=v_scr[pp][:, dv * 512:(dv + 1) * 512],
                                                  in_=ps_b[pp][:, :], func=AF.Identity))
                            evac_done[pp] = act.n
                            act.wait(SY)
                            st.inc(SY.dma_start(out=v3w[tc_global][:, dv * 512:(dv + 1) * 512],
                                                in_=v_scr[pp][:, dv * 512:(dv + 1) * 512]), 16)
                            store_done[key] = st.n
                st.wait(SY)
                st.wait(G)
                nc.all_engine_barrier()

        G.collective_compute("AllGather", ALU.bypass, replica_groups=[list(range(NCORE))],
                             ins=[ag4_in.opt()], outs=[ag4_out.opt()]).then_inc(cc[3])
        G.collective_compute("AllGather", ALU.bypass, replica_groups=[list(range(NCORE))],
                             ins=[ag5_in.opt()], outs=[ag5_out.opt()]).then_inc(cc[4])
        G.wait_ge(cc[3], 1)
        nc.all_engine_barrier()

        with contextlib.ExitStack() as e4:
            mask_sb = sb("p4mask", [128, 1920], st_=e4)
            sel_sb = sb("p4sel", [128, 32], st_=e4)
            binv_sb = sb("p4binv", [128, 32], st_=e4)
            q_sb = [sb(f"p4q{c}", [128, CH], BF16, e4) for c in range(8)]
            kt_sb = [[sb(f"p4k{p}_{c}", [128, KT], BF16, e4) for c in range(8)]
                     for p in range(2)]
            vt_sb = [[sb(f"p4v{p}_{c}", [128, DM], BF16, e4) for c in range(8)]
                     for p in range(2)]
            O_sb = [sb(f"p4O{j}", [128, DM], st_=e4) for j in range(8)]
            mrow = [sb(f"p4m{j}", [128, 1], st_=e4) for j in range(8)]
            lrow = [sb(f"p4l{j}", [128, 1], st_=e4) for j in range(8)]
            mnew = sb("p4mnew", [128, 1], st_=e4)
            mloc = sb("p4mloc", [128, 1], st_=e4)
            frow = [sb(f"p4frow{p}", [128, 1], st_=e4) for p in range(2)]
            bneg = sb("p4bneg", [128, 1], st_=e4)
            rs0 = [sb(f"p4rs0{p}", [128, 1], st_=e4) for p in range(2)]
            rl = sb("p4rl", [128, 1], st_=e4)
            p_bf = [sb(f"p4pbf{p}", [128, KT], BF16, e4) for p in range(2)]
            pt_sb = [sb(f"p4pt{p}", [128, KT], BF16, e4) for p in range(2)]
            of_bf = sb("p4of", [128, DM], BF16, e4)
            attn_bf = [sb(f"p4at{c}", [128, CH], BF16, e4) for c in range(8)]
            x2_sb = [sb(f"p4x2_{c}", [128, TT], st_=e4) for c in range(8)]
            ps_S = [psum(f"p4ps_S{p}", [128, KT], st_=e4) for p in range(2)]
            ps_pt = psum("p4ps_pt", [128, KT], BF16, e4)
            ps_av = psum("p4ps_av", [128, DM], st_=e4)

            ld.inc(SY.dma_start(out=mask_sb[:, :], in_=mb_in[:, :]), 16)
            ld.inc(SY.dma_start(out=sel_sb[:, :], in_=sel_in[:, :]), 16)
            ld.inc(SY.dma_start(out=binv_sb[:, :], in_=binv_in[:, :]), 16)
            SY.wait_ge(cc[4], 1)
            ld.wait(V); ld.wait(A_); ld.wait(PE)

            kvread_done = [0, 0]
            sread_done = [0, 0]
            sread_dve = [0, 0]
            pexp_done = [0, 0]
            ptevac_done = [0, 0]
            ostt_done = 0
            lstt_done = 0
            for qc in range(2):
                dve.wait(SY); act.wait(SY); pe.wait(SY)
                for c in range(8):
                    sq4.inc(SY.dma_start(out=q_sb[c][:, :],
                                        in_=q3[c][:, qc * CH:(qc + 1) * CH]), 16)
                for j in range(8):
                    dve.inc(V.memset(O_sb[j][:, :], 0.0))
                    dve.inc(V.memset(mrow[j][:, :], -1e30))
                    dve.inc(V.memset(lrow[j][:, :], 0.0))
                dve.wait(V); dve.wait(A_); dve.wait(PE)
                nkt = 8 if qc == 0 else 16
                kld_par = [0, 0]

                def load_kv(kt2):
                    p2 = kt2 % 2
                    rk2, coff2 = _chunk_loc(kt2)
                    pe.wait(SY, kvread_done[p2])
                    for c in range(8):
                        skv[p2].inc(SY.dma_start(out=kt_sb[p2][c][:, :],
                                            in_=k3r[rk2, c][:, coff2:coff2 + KT]), 16)
                    for c in range(8):
                        skv[p2].inc(SY.dma_start(out=vt_sb[p2][c][:, :],
                                            in_=v3r[rk2, (coff2 // 128) + c]), 16)

                load_kv(0)
                for kt in range(nkt):
                    idx = qc * 16 + kt
                    p = kt % 2
                    if kt + 1 < nkt:
                        load_kv(kt + 1)
                    skv[p].wait(PE)
                    sq4.wait(PE)

                    def s_stage(j):
                        sp = j % 2
                        act.wait(PE, sread_done[sp])
                        dve.wait(PE, sread_dve[sp])
                        for half in range(2):
                            for c in range(8):
                                pe.inc(PE.matmul(
                                    out=ps_S[sp][:, half * 512:(half + 1) * 512],
                                    lhsT=q_sb[c][:, j * 128:(j + 1) * 128],
                                    rhs=kt_sb[p][c][:, half * 512:(half + 1) * 512],
                                    start=(c == 0), stop=(c == 7)))
                        return pe.n

                    smm_done = {}
                    smm_done[0] = s_stage(0)
                    for j in range(8):
                        sp = j % 2
                        pe.wait(V, smm_done[j])
                        dve.inc(V.scalar_tensor_tensor(
                            out=ps_S[sp][:, :],
                            in0=mask_sb[:, 896 - 128 * j:1920 - 128 * j],
                            scalar=sel_sb[:, idx:idx + 1], in1=ps_S[sp][:, :],
                            op0=ALU.mult, op1=ALU.add))
                        dve.inc(V.tensor_reduce(out=mloc[:, :], in_=ps_S[sp][:, :],
                                                axis=mybir.AxisListType.X, op=ALU.max))
                        dve.wait(V)
                        dve.inc(V.tensor_max(out=mnew[:, :], in0=mrow[j][:, :],
                                             in1=mloc[:, :]))
                        dve.wait(V)
                        act.wait(V, pexp_done[sp])
                        dve.inc(V.tensor_sub(out=frow[sp][:, :], in0=mrow[j][:, :],
                                             in1=mnew[:, :]))
                        dve.wait(V)
                        dve.inc(V.scalar_tensor_tensor(
                            out=bneg[:, :], in0=mnew[:, :], scalar=cn32[:, :],
                            in1=binv_sb[:, idx:idx + 1], op0=ALU.mult, op1=ALU.add))
                        dve.wait(V)
                        dve.inc(V.tensor_copy(out=mrow[j][:, :], in_=mnew[:, :]))
                        dve.wait(V); dve.wait(A_)
                        act.inc(A_.activation(out=frow[sp][:, :], in_=frow[sp][:, :],
                                              func=AF.Exp, scale=1.0 / 32.0))
                        act.inc(A_.activation(out=p_bf[sp][:, :], in_=ps_S[sp][:, :],
                                              func=AF.Exp, scale=1.0 / 32.0,
                                              bias=bneg[:, :], accum_out=rs0[sp][:, :]))
                        pexp_done[sp] = act.n
                        sread_done[sp] = act.n
                        act.wait(V)
                        dve.inc(V.scalar_tensor_tensor(out=lrow[j][:, :],
                                                       in0=lrow[j][:, :],
                                                       scalar=frow[sp][:, :],
                                                       in1=rs0[sp][:, :],
                                                       op0=ALU.mult, op1=ALU.add))
                        lstt_done = dve.n
                        dve.wait(V)
                        if j + 1 < 8:
                            smm_done[j + 1] = s_stage(j + 1)
                        act.wait(PE, ptevac_done[sp])
                        act.wait(PE, pexp_done[sp])
                        for kc in range(8):
                            pe.inc(PE.transpose(out=ps_pt[:, kc * 128:(kc + 1) * 128],
                                                in_=p_bf[sp][:, kc * 128:(kc + 1) * 128],
                                                identity=idb_sb[:, :]))
                        pe.wait(A_)
                        act.inc(A_.activation(out=pt_sb[sp][:, :], in_=ps_pt[:, :],
                                              func=AF.Identity))
                        ptevac_done[sp] = act.n
                        act.wait(PE, ptevac_done[sp])
                        dve.wait(PE, ostt_done)
                        for kc in range(8):
                            pe.inc(PE.matmul(out=ps_av[:, 0:512],
                                             lhsT=pt_sb[sp][:, kc * 128:(kc + 1) * 128],
                                             rhs=vt_sb[p][kc][:, 0:512],
                                             start=(kc == 0), stop=(kc == 7)))
                            pe.inc(PE.matmul(out=ps_av[:, 512:1024],
                                             lhsT=pt_sb[sp][:, kc * 128:(kc + 1) * 128],
                                             rhs=vt_sb[p][kc][:, 512:1024],
                                             start=(kc == 0), stop=(kc == 7)))
                        pe.wait(V)
                        dve.inc(V.scalar_tensor_tensor(out=O_sb[j][:, :],
                                                       in0=O_sb[j][:, :],
                                                       scalar=frow[sp][:, :],
                                                       in1=ps_av[:, :],
                                                       op0=ALU.mult, op1=ALU.add))
                        ostt_done = dve.n
                        sread_dve[sp] = dve.n
                    kvread_done[p] = pe.n
                for j in range(8):
                    dve.wait(A_)
                    act.inc(A_.activation(out=rl[:, :], in_=lrow[j][:, :], func=AF.Ln))
                    act.inc(A_.activation(out=rl[:, :], in_=rl[:, :], func=AF.Exp,
                                          scale=-1.0))
                    act.wait(V)
                    dve.inc(V.tensor_scalar_mul(out=of_bf[:, :], in0=O_sb[j][:, :],
                                                scalar1=rl[:, :]))
                    dve.wait(V); dve.wait(PE)
                    act.wait(PE)
                    for dc in range(8):
                        pe.inc(PE.transpose(out=ps_pt[:, dc * 128:(dc + 1) * 128],
                                            in_=of_bf[:, dc * 128:(dc + 1) * 128],
                                            identity=idb_sb[:, :]))
                    pe.wait(A_)
                    for dc in range(8):
                        act.inc(A_.activation(out=attn_bf[dc][:, j * 128:(j + 1) * 128],
                                              in_=ps_pt[:, dc * 128:(dc + 1) * 128],
                                              func=AF.Identity))
                    act.wait(PE); act.wait(V)
                for tloc in range(CH // TT):
                    sl_loc = slice(tloc * TT, (tloc + 1) * TT)
                    sl_out = slice(qc * CH + tloc * TT, qc * CH + (tloc + 1) * TT)
                    st.wait(SY)
                    dve.wait(SY)
                    for c in range(8):
                        ld.inc(SY.dma_start(out=x2_sb[c][:, :], in_=x23[c][:, sl_out]), 16)
                    ld.wait(V); ld.wait(A_); ld.wait(PE)
                    ln_stats([attn_bf[c][:, sl_loc] for c in range(8)],
                             ps_S[0][:, 0:512], ps_S[0][:, 512:1024])
                    for c in range(8):
                        dve.inc(V.tensor_mul(out=stat["t1"][:, :], in0=attn_bf[c][:, sl_loc],
                                             in1=stat["ar"][:, :]))
                        dve.inc(V.tensor_sub(out=stat["t1"][:, :], in0=stat["t1"][:, :],
                                             in1=stat["br"][:, :]))
                        dve.inc(V.tensor_add(out=x2_sb[c][:, :], in0=x2_sb[c][:, :],
                                             in1=stat["t1"][:, :]))
                        dve.wait(V); dve.wait(SY)
                        st.inc(SY.dma_start(out=x33[c][:, sl_out], in_=x2_sb[c][:, :]), 16)
                pe.wait(A_); pe.wait(V)
                sread_done[0] = act.n
                sread_dve[0] = dve.n
            st.wait(SY)
            nc.all_engine_barrier()

        with contextlib.ExitStack() as e5:
            x_sb = [sb(f"p5x{c}", [128, TT], st_=e5) for c in range(8)]
            xln_bf = [sb(f"p5xln{c}", [128, TPC], BF16, e5) for c in range(8)]
            rw_sb = [sb(f"p5rw{c}", [128, 8], BF16, e5) for c in range(8)]
            rb_sb = sb("p5rb", [128, 8], st_=e5)
            lg = sb("p5lg", [128, 8], st_=e5)
            ge1 = sb("p5ge1", [128, 8], st_=e5)
            msk = sb("p5msk", [128, 8], st_=e5)
            m1c = sb("p5m1c", [128, 1], st_=e5)
            m2c = sb("p5m2c", [128, 1], st_=e5)
            nm1 = sb("p5nm1", [128, 1], st_=e5)
            d21 = sb("p5d21", [128, 1], st_=e5)
            rinv = sb("p5rinv", [128, 1], st_=e5)
            ee = sb("p5ee", [128, 8], st_=e5)
            cc_sb = sb("p5cc", [128, 8], st_=e5)
            cfm = sb("p5cfm", [8, TPC], st_=e5)
            crep_all = sb("p5crep", [128, TPC], BF16, e5)
            hs_bf = [sb(f"p5hs{f}", [128, TT], BF16, e5) for f in range(4)]
            b1_sb = sb("p5b1", [128, 32], st_=e5)
            b2_sb = sb("p5b2s", [8, DM], st_=e5)
            w1_bf = [[sb(f"p5w1_{pw}_{c}", [128, FSL], BF16, e5) for c in range(8)]
                     for pw in range(2)]
            w2_bf = [[sb(f"p5w2_{pw}_{c}", [128, DM], BF16, e5) for c in range(4)]
                     for pw in range(2)]
            h_bf = [sb(f"p5h{f}", [128, TT], BF16, e5) for f in range(4)]
            yev = sb("p5yev", [128, TT], st_=e5)
            y_acc = [sb(f"p5ya{c}", [128, TPC], st_=e5) for c in range(8)]
            ps_r = psum("p5ps_r", [128, 8], st_=e5)
            ps_ct = psum("p5ps_ct", [8, 128], st_=e5)
            ps_h = [psum(f"p5ps_h{p}", [128, TT], st_=e5) for p in range(2)]
            ps_y = [psum(f"p5ps_y{p}", [128, TT], st_=e5) for p in range(2)]
            ps_s = psum("p5ps_s", [128, TT], st_=e5)
            ps_q = psum("p5ps_q", [128, TT], st_=e5)

            ld.inc(SY.dma_start(out=rb_sb[:, :], in_=rb_in[:, :]), 16)
            ld.inc(SY.dma_start(out=b2_sb[:, :], in_=b2_in[:, :]), 16)
            for c in range(8):
                gld.inc(G.dma_start(out=rw_sb[c][:, :], in_=rw3[c]), 16)
            ld.wait(V)
            gld.wait(PE)

            for t in range(NTT):
                sl = slice(t * TT, (t + 1) * TT)
                dve.wait(SY)
                for c in range(8):
                    ld.inc(SY.dma_start(out=x_sb[c][:, :], in_=x33[c][:, sl]), 16)
                ld.wait(V); ld.wait(A_); ld.wait(PE)
                ln_stats([x_sb[c][:, :] for c in range(8)], ps_s[:, :], ps_q[:, :])
                for c in range(8):
                    dve.inc(V.tensor_mul(out=stat["t1"][:, :], in0=x_sb[c][:, :],
                                         in1=stat["ar"][:, :]))
                    dve.inc(V.tensor_sub(out=xln_bf[c][:, sl], in0=stat["t1"][:, :],
                                         in1=stat["br"][:, :]))

            for tc in range(TPC // 128):
                tsl = slice(tc * 128, (tc + 1) * 128)
                act.wait(PE); dve.wait(PE)
                for c in range(8):
                    pe.inc(PE.matmul(out=ps_r[:, :], lhsT=xln_bf[c][:, tsl],
                                     rhs=rw_sb[c][:, :], start=(c == 0), stop=(c == 7)))
                pe.wait(A_); pe.wait(V)
                act.inc(A_.activation(out=lg[:, :], in_=ps_r[:, :], func=AF.Identity))
                act.wait(V)
                dve.inc(V.tensor_add(out=lg[:, :], in0=lg[:, :], in1=rb_sb[:, :]))
                dve.wait(V)
                dve.inc(V.tensor_reduce(out=m1c[:, :], in_=lg[:, :],
                                        axis=mybir.AxisListType.X, op=ALU.max))
                dve.wait(V)
                dve.inc(V.tensor_scalar(out=ge1[:, :], in0=lg[:, :], scalar1=m1c[:, 0:1],
                                        scalar2=None, op0=ALU.is_ge))
                dve.wait(V)
                dve.inc(V.scalar_tensor_tensor(out=msk[:, :], in0=ge1[:, :],
                                               scalar=cneg[:, :], in1=lg[:, :],
                                               op0=ALU.mult, op1=ALU.add))
                dve.wait(V)
                dve.inc(V.tensor_reduce(out=m2c[:, :], in_=msk[:, :],
                                        axis=mybir.AxisListType.X, op=ALU.max))
                dve.wait(V)
                dve.inc(V.tensor_scalar(out=ge1[:, :], in0=lg[:, :], scalar1=m2c[:, 0:1],
                                        scalar2=None, op0=ALU.is_ge))
                dve.wait(V)
                dve.inc(V.tensor_scalar_mul(out=nm1[:, :], in0=m1c[:, :],
                                            scalar1=cm1[:, :]))
                dve.wait(V)
                dve.inc(V.tensor_sub(out=d21[:, :], in0=m2c[:, :], in1=m1c[:, :]))
                dve.wait(V); dve.wait(A_)
                act.inc(A_.activation(out=ee[:, :], in_=lg[:, :], func=AF.Exp,
                                      bias=nm1[:, :]))
                act.inc(A_.activation(out=d21[:, :], in_=d21[:, :], func=AF.Exp))
                act.wait(V)
                dve.inc(V.tensor_add(out=d21[:, :], in0=d21[:, :], in1=cone[:, :]))
                dve.wait(V); dve.wait(A_)
                act.inc(A_.activation(out=d21[:, :], in_=d21[:, :], func=AF.Ln))
                act.inc(A_.activation(out=rinv[:, :], in_=d21[:, :], func=AF.Exp,
                                      scale=-1.0))
                act.wait(V)
                dve.inc(V.tensor_mul(out=cc_sb[:, :], in0=ee[:, :], in1=ge1[:, :]))
                dve.wait(V)
                dve.inc(V.tensor_scalar_mul(out=cc_sb[:, :], in0=cc_sb[:, :],
                                            scalar1=rinv[:, 0:1]))
                dve.wait(V); dve.wait(PE)
                pe.inc(PE.transpose(out=ps_ct[:, :], in_=cc_sb[:, :],
                                    identity=idf_sb[:, :]))
                pe.wait(A_)
                act.inc(A_.activation(out=cfm[:, tsl], in_=ps_ct[:, :], func=AF.Identity))
                act.wait(PE); act.wait(V)

            act.wait(SY)
            st.inc(SY.dma_start(out=cfm_dram[:, :], in_=cfm[:, :]), 16)

            for t in range(NTT):
                sl = slice(t * TT, (t + 1) * TT)
                for do in range(8):
                    pp = do % 2
                    act.wait(PE)
                    pe.inc(PE.matmul(out=ps_y[pp][:, :], lhsT=b2_sb[:, do * 128:(do + 1) * 128],
                                     rhs=cfm[:, sl], start=True, stop=True))
                    pe.wait(A_)
                    act.inc(A_.activation(out=y_acc[do][:, sl], in_=ps_y[pp][:, :],
                                          func=AF.Identity))
            act.wait(V); act.wait(PE)

            st.wait(G)
            hevac_done = [0, 0]
            yread_done = [0, 0]
            wread_done = [0, 0]
            wld_done = [0, 0]

            def load_wslice(e2, fs2, pw2):
                pe.wait(G, wread_done[pw2])
                for c in range(8):
                    sw5[pw2].inc(G.dma_start(out=w1_bf[pw2][c][:, :],
                                        in_=w1_3[e2, c][:, fs2 * FSL:(fs2 + 1) * FSL]), 16)
                for fc in range(4):
                    sw5[pw2].inc(G.dma_start(out=w2_bf[pw2][fc][:, :],
                                        in_=w2_3[e2, fs2 * 4 + fc]), 16)

            load_wslice(0, 0, 0)
            pre_n = None
            fold0_done = 0
            for e in range(8):
                dve.wait(G)
                scr5.inc(G.dma_start(out=crep_all[:, :],
                                    in_=cfm_dram[e:e + 1, :].broadcast_to([128, TPC])), 16)
                scr5.wait(V)
                act.wait(SY)
                ld.inc(SY.dma_start(out=b1_sb[:, :], in_=b1_in[e]), 16)
                for fs in range(NFS):
                    pw = (e * NFS + fs) % 2
                    nxt = e * NFS + fs + 1
                    if nxt < 8 * NFS:
                        load_wslice(nxt // NFS, nxt % NFS, 1 - pw)
                    sw5[pw].wait(PE)
                    ld.wait(A_)
                    for t in range(NTT):
                        sl = slice(t * TT, (t + 1) * TT)
                        for fo in range(FSL // 128):
                            ph = fo % 2
                            if fo == 0 and pre_n is not None:
                                dve.wait(A_, fold0_done)
                                pe.wait(A_, pre_n)
                            else:
                                act.wait(PE, hevac_done[ph])
                                for c in range(8):
                                    pe.inc(PE.matmul(out=ps_h[ph][:, :],
                                                     lhsT=w1_bf[pw][c][:, fo * 128:(fo + 1) * 128],
                                                     rhs=xln_bf[c][:, sl],
                                                     start=(c == 0), stop=(c == 7)))
                                pe.wait(A_)
                            act.inc(A_.activation(out=h_bf[fo][:, :], in_=ps_h[ph][:, :],
                                                  func=AF.Relu,
                                                  bias=b1_sb[:, fs * 4 + fo:fs * 4 + fo + 1]))
                            hevac_done[ph] = act.n
                            act.wait(V)
                            dve.inc(V.tensor_mul(out=hs_bf[fo][:, :], in0=h_bf[fo][:, :],
                                                 in1=crep_all[:, sl]))
                            if fo == 0:
                                fold0_done = dve.n
                        if t + 1 < NTT:
                            sl_nx = slice((t + 1) * TT, (t + 2) * TT)
                            act.wait(PE, hevac_done[0])
                            for c in range(8):
                                pe.inc(PE.matmul(out=ps_h[0][:, :],
                                                 lhsT=w1_bf[pw][c][:, 0:128],
                                                 rhs=xln_bf[c][:, sl_nx],
                                                 start=(c == 0), stop=(c == 7)))
                            pre_n = pe.n
                        elif e * NFS + fs + 1 < 8 * NFS:
                            sw5[1 - pw].wait(PE)
                            act.wait(PE, hevac_done[0])
                            for c in range(8):
                                pe.inc(PE.matmul(out=ps_h[0][:, :],
                                                 lhsT=w1_bf[1 - pw][c][:, 0:128],
                                                 rhs=xln_bf[c][:, 0:TT],
                                                 start=(c == 0), stop=(c == 7)))
                            pre_n = pe.n
                        else:
                            pre_n = None
                        dve.wait(PE)
                        for do in range(8):
                            pp = do % 2
                            dve.wait(PE, yread_done[pp])
                            for fc in range(4):
                                pe.inc(PE.matmul(out=ps_y[pp][:, :],
                                                 lhsT=w2_bf[pw][fc][:, do * 128:(do + 1) * 128],
                                                 rhs=hs_bf[fc][:, :],
                                                 start=(fc == 0), stop=(fc == 3)))
                            pe.wait(V)
                            dve.inc(V.tensor_add(out=y_acc[do][:, sl], in0=y_acc[do][:, sl],
                                                 in1=ps_y[pp][:, :]))
                            yread_done[pp] = dve.n
                    wread_done[pw] = pe.n

            for t in range(NTT):
                sl = slice(t * TT, (t + 1) * TT)
                dve.wait(SY); st.wait(SY)
                for c in range(8):
                    ld.inc(SY.dma_start(out=x_sb[c][:, :], in_=x33[c][:, sl]), 16)
                ld.wait(V); ld.wait(A_); ld.wait(PE)
                ln_stats([y_acc[c][:, sl] for c in range(8)], ps_s[:, :], ps_q[:, :])
                for c in range(8):
                    dve.inc(V.tensor_mul(out=stat["t1"][:, :], in0=y_acc[c][:, sl],
                                         in1=stat["ar"][:, :]))
                    dve.inc(V.tensor_sub(out=stat["t1"][:, :], in0=stat["t1"][:, :],
                                         in1=stat["br"][:, :]))
                    dve.inc(V.tensor_add(out=x_sb[c][:, :], in0=x_sb[c][:, :],
                                         in1=stat["t1"][:, :]))
                    dve.wait(V); dve.wait(SY)
                    st.inc(SY.dma_start(out=o3[c][:, sl], in_=x_sb[c][:, :]), 16)
            st.wait(SY)
    return nc


def kernel_fused(inp):
    if "fused" not in _CACHE:
        _CACHE["fused"] = build_fused()
    nc = _CACHE["fused"]
    x = inp["L0_attn_x"][0]
    mask_big = np.zeros((128, 1920), np.float32)
    for rr in range(128):
        mask_big[rr, 897 + rr:] = -1e9
    shared = dict(_host_l1_inputs(inp))
    shared.update({
        "outw_T": np.ascontiguousarray(inp["L0_attn_out_proj_w"].T),
        "w0_T": np.ascontiguousarray(inp["L0_ffn_weight"].T),
        "alpha_rep": np.ascontiguousarray(np.broadcast_to(
            inp["L0_ffn_alpha"].reshape(1, 1), (128, 1))).astype(np.float32),
        "fb_t": _cvec(inp["L0_ffn_bias"]),
        "wq_T": np.ascontiguousarray(inp["L1_attn_Wq"].T),
        "wk_T": np.ascontiguousarray(inp["L1_attn_Wk"].T),
        "wv_T": np.ascontiguousarray(inp["L1_attn_Wv"].T),
        "mask_big": mask_big,
        "ident_bf": np.eye(128, dtype=BF),
        "ident_f": np.eye(128, dtype=np.float32),
        "ones_f": np.ones((128, 128), dtype=np.float32),
        "router_w": np.ascontiguousarray(inp["L1_ffn_router_w"]),
        "rb_rep": np.ascontiguousarray(np.broadcast_to(
            inp["L1_ffn_router_b"].reshape(1, 8), (128, 8))).astype(np.float32),
        "W1": np.ascontiguousarray(inp["L1_ffn_W1"]),
        "b1_t": np.ascontiguousarray(
            inp["L1_ffn_b1"].reshape(8, 32, 128).transpose(0, 2, 1)),
        "W2": np.ascontiguousarray(inp["L1_ffn_W2"]),
        "b2": np.ascontiguousarray(inp["L1_ffn_b2"]),
    })
    in_maps = []
    for core in range(NCORE):
        m = dict(shared)
        rows = _tok_rows(core)
        m["x_fm"] = np.ascontiguousarray(x[rows].T)
        xh = np.zeros((8, DM), np.float32)
        c0, c1 = CORE_CHUNKS[core]
        if c0 > 0:
            xh[0:4] = x[c0 * CH - 4:c0 * CH]
        xh[4:8] = x[c1 * CH - 4:c1 * CH]
        m["x_halo"] = np.ascontiguousarray(xh.T)
        s0 = EPC * core
        m["A_log_t"] = np.ascontiguousarray(
            inp["L0_attn_A_log"][s0:s0 + EPC].reshape(2, 128, NST))
        m["D_t"] = np.ascontiguousarray(
            inp["L0_attn_D"][s0:s0 + EPC].reshape(2, 128).T)
        sel = np.zeros((1, 32), np.float32)
        binv = np.zeros((1, 32), np.float32)
        for qc in range(2):
            gq = CORE_CHUNKS[core][qc]
            sel[0, qc * 16 + gq] = 1.0
            for kt in range(gq + 1, 16):
                binv[0, qc * 16 + kt] = -1e30
        m["sel_rep"] = np.ascontiguousarray(np.broadcast_to(sel, (128, 32)))
        m["binv_rep"] = np.ascontiguousarray(np.broadcast_to(binv, (128, 32)))
        m["idx0"] = (EPC * core + np.arange(128, dtype=np.int32)).reshape(128, 1)
        m["dtw_sl"] = np.ascontiguousarray(inp["L0_attn_dt_proj_w"].T[:, s0:s0 + EPC])
        m["dtb_sl"] = np.ascontiguousarray(
            inp["L0_attn_dt_proj_b"][s0:s0 + EPC].reshape(2, 128).T)
        idxq = np.zeros((128, 2), np.int32)
        for qc in range(2):
            idxq[:, qc] = CORE_CHUNKS[core][qc] * EPC + np.arange(128)
        m["idxq"] = np.ascontiguousarray(idxq)
        in_maps.append(m)
    res = run_bass_kernel_spmd(nc, in_maps, core_ids=list(range(NCORE)))
    out = np.empty((S, DM), dtype=np.float32)
    for core in range(NCORE):
        out[_tok_rows(core)] = res.results[core]["out_fm"].T
    return out.reshape(1, S, DM)


def kernel(**inp):
    return kernel_fused({k: np.asarray(v) for k, v in inp.items()})
